# version 07
# baseline (speedup 1.0000x reference)
import math
import numpy as np
import ml_dtypes
import concourse.bass as bass
import concourse.mybir as mybir
from concourse.bass_utils import run_bass_kernel_spmd
from concourse.alu_op_type import AluOpType as OP

F32 = mybir.dt.float32
BF16 = mybir.dt.bfloat16
AF = mybir.ActivationFunctionType
AX = mybir.AxisListType

S = 4096
DM = 1024
NT = S // 128
NF = DM // 128
SP = S + 1
BIG = 1.0e38

L0A_NH, L0A_HD = 16, 64
FF_HEADS, FF_HD, FF_WIN, FF_INNER = 8, 64, 512, 512
FF_HID = 2730
FF_HPAD = 2816
SSD_DS, SSD_HD, SSD_NH = 64, 32, 32
SSD_CS = 128
SSD_NC = S // SSD_CS
MOE_E, MOE_F = 8, 2048


def _rotary_tables():
    n, dim_head, window = S, FF_HD, FF_WIN
    half_idx = np.arange(0, dim_head, 2, dtype=np.float64)
    inv_freq = 1.0 / 10000.0 ** (half_idx / dim_head)
    t = np.arange(n, dtype=np.float64)
    freqs = t[:, None] * inv_freq[None, :]
    freqs = np.concatenate([freqs, freqs], axis=-1)
    base = (half_idx + 0.4 * dim_head) / (1.4 * dim_head)
    power = (t - n // 2) / float(window // 2)
    scale = base[None, :] ** power[:, None]
    scale = np.concatenate([scale, scale], axis=-1)
    cos, sin = np.cos(freqs), np.sin(freqs)
    qc = (cos * scale).astype(np.float32)
    qs = (sin * scale).astype(np.float32)
    kc = (cos / scale).astype(np.float32)
    ks = (sin / scale).astype(np.float32)
    def sgn(tab):
        out = tab.copy()
        out[:, :dim_head // 2] *= -1.0
        return out
    return qc, sgn(qs), kc, sgn(ks)


def _attn_masks():
    ii = np.arange(FF_WIN)[:, None]
    jj = np.arange(2 * FF_WIN)[None, :]
    valid = (jj >= ii) & (jj <= ii + FF_WIN)
    m_gen = np.where(valid, 0.0, -BIG).astype(np.float32)
    valid0 = valid & (jj >= FF_WIN)
    m_w0 = np.where(valid0, 0.0, -BIG).astype(np.float32)
    return m_gen, m_w0


def _ssd_mask():
    i = np.arange(SSD_CS)
    m = np.where(i[None, :] >= i[:, None], BIG, -BIG).astype(np.float32)
    return m


def _selectors():
    ps = np.zeros((16, 8, 128), np.float32)
    for j in range(8):
        ps[2 * j, j, 0:64] = 1.0
        ps[2 * j + 1, j, 64:128] = 1.0
    s32 = np.zeros((32, 32, 128), np.float32)
    for h in range(32):
        s32[h, h, :] = 1.0
    s8 = np.zeros((8, 8, 128), np.float32)
    for e in range(8):
        s8[e, e, :] = 1.0
    return (ps.reshape(16, 8 * 128), s32.reshape(32, 32 * 128),
            s8.reshape(8, 8 * 128))


def host_prep(I):
    d = {}
    f32 = lambda a: np.ascontiguousarray(a, dtype=np.float32)
    d['eye'] = np.eye(128, dtype=np.float32)
    d['eyebf'] = np.eye(128, dtype=np.float32).astype(ml_dtypes.bfloat16)
    d['onesbf'] = np.ones((128, 128), dtype=ml_dtypes.bfloat16)
    d['onesf'] = np.ones((2, 128), dtype=np.float32)
    qc, qs, kc, ks = _rotary_tables()
    d['rot_qc'], d['rot_qs'], d['rot_kc'], d['rot_ks'] = qc, qs, kc, ks
    mg, m0 = _attn_masks()
    d['mask_gen'] = mg.astype(ml_dtypes.bfloat16)
    d['mask_w0'] = m0.astype(ml_dtypes.bfloat16)
    d['ssd_mask'] = _ssd_mask()
    ps, s32, s8 = _selectors()
    d['pairsel'], d['sel32'], d['sel8'] = ps, s32, s8

    for w in ['Wq', 'Wk', 'Wv', 'Wd']:
        d['l0a_' + w] = f32(I['L0_attn_' + w].T)
    for b in ['bq', 'bk', 'bv', 'bd', 'gamma', 'beta']:
        d['l0a_' + b] = f32(I['L0_attn_' + b]).reshape(1, DM)
    d['l0a_bq_c'] = f32(np.asarray(I['L0_attn_bq'], np.float32).reshape(8, 128).T)
    d['ff_wqkv'] = f32(I['L0_ffn_wqkv'])
    d['ff_wo'] = f32(I['L0_ffn_wo'])
    w1 = np.asarray(I['L0_ffn_w1'], np.float32)
    b1 = np.asarray(I['L0_ffn_b1'], np.float32)
    w2 = np.asarray(I['L0_ffn_w2'], np.float32)
    w1p = np.zeros((2, DM, 2 * FF_HPAD), np.float32)
    w1p[:, :, :FF_HID] = w1[:, :, :FF_HID]
    w1p[:, :, FF_HPAD:FF_HPAD + FF_HID] = w1[:, :, FF_HID:]
    b1p = np.zeros((2, 2 * FF_HPAD), np.float32)
    b1p[:, :FF_HID] = b1[:, :FF_HID]
    b1p[:, FF_HPAD:FF_HPAD + FF_HID] = b1[:, FF_HID:]
    w2p = np.zeros((2, FF_HPAD, DM), np.float32)
    w2p[:, :FF_HID, :] = w2
    d['ff_w1'] = f32(w1p)
    d['ff_b1'] = f32(b1p.reshape(2, 1, 2 * FF_HPAD))
    d['ff_w2'] = f32(w2p)
    d['ff_b2'] = f32(np.asarray(I['L0_ffn_b2'], np.float32).reshape(2, 1, DM))
    d['ff_qs'] = f32(I['L0_ffn_q_scale']).reshape(2, 1, 64)
    d['ff_ks'] = f32(I['L0_ffn_k_scale']).reshape(2, 1, 64)
    d['ff_lng'] = f32(I['L0_ffn_ln_g']).reshape(2, 1, DM)
    d['ff_lnb'] = f32(I['L0_ffn_ln_b']).reshape(2, 1, DM)
    d['ssd_wint'] = f32(np.asarray(I['L1_attn_W_in'], np.float32).T)
    d['ssd_wout'] = f32(np.asarray(I['L1_attn_W_out'], np.float32).T)
    d['ssd_dtb'] = f32(I['L1_attn_dt_bias']).reshape(32, 1)
    d['ssd_alog'] = f32(I['L1_attn_A_log']).reshape(32, 1)
    d['ssd_d'] = f32(I['L1_attn_D']).reshape(1, 32)
    d['moe_rt'] = f32(np.asarray(I['L1_ffn_router_w'], np.float32).T)
    d['moe_wg'] = f32(np.transpose(np.asarray(I['L1_ffn_w_gate'], np.float32), (0, 2, 1)))
    d['moe_wu'] = f32(np.transpose(np.asarray(I['L1_ffn_w_up'], np.float32), (0, 2, 1)))
    d['moe_wd'] = f32(np.transpose(np.asarray(I['L1_ffn_w_down'], np.float32), (0, 2, 1)))
    return d


SCRATCH = {
    'XNT':   ([DM, SP], BF16),
    'XNTOK': ([S, DM], BF16),
    'XNF32': ([S, DM], F32),
    'S0':    ([S, DM], F32),
    'XIA':   ([S, DM], F32),
    'XIB':   ([S, DM], F32),
    'XITA':  ([DM, SP], BF16),
    'XITB':  ([DM, SP], BF16),
    'QHT':   ([FF_INNER, S], BF16),
    'KHT':   ([FF_INNER, S], BF16),
    'VH':    ([S, FF_INNER], BF16),
    'OT':    ([FF_INNER, S], BF16),
    'FF':    ([S, DM], F32),
    'X1':    ([S, DM], F32),
    'XN1T':  ([DM, SP], BF16),
    'XN1TLO': ([DM, SP], BF16),
    'XN1TOK': ([S, DM], BF16),
    'XTOK':  ([S, DM], BF16),
    'BT':    ([64, S], BF16),
    'CT':    ([64, S], BF16),
    'BTOK':  ([S, 64], BF16),
    'ACT_s': ([32, S], F32),
    'RT_s':  ([32, S], F32),
    'S1':    ([S, DM], F32),
    'MOEACC': ([S, DM], F32),
    'DWT':   ([8, S], F32),
    'YTOK':  ([S, DM], BF16),
    'KVT':   ([128, L0A_NH * 64], BF16),
    'KDIAG': ([128, 128], BF16),
}

CONSTS = {
    'eye': ([128, 128], F32), 'eyebf': ([128, 128], BF16),
    'onesbf': ([128, 128], BF16), 'onesf': ([2, 128], F32),
    'rot_qc': ([S, 64], F32), 'rot_qs': ([S, 64], F32),
    'rot_kc': ([S, 64], F32), 'rot_ks': ([S, 64], F32),
    'mask_gen': ([FF_WIN, 2 * FF_WIN], BF16), 'mask_w0': ([FF_WIN, 2 * FF_WIN], BF16),
    'ssd_mask': ([SSD_CS, SSD_CS], F32),
    'pairsel': ([16, 8 * 128], F32), 'sel32': ([32, 32 * 128], F32),
    'sel8': ([8, 8 * 128], F32),
    'l0a_Wq': ([DM, DM], F32), 'l0a_Wk': ([DM, DM], F32),
    'l0a_Wv': ([DM, DM], F32), 'l0a_Wd': ([DM, DM], F32),
    'l0a_bq': ([1, DM], F32), 'l0a_bk': ([1, DM], F32),
    'l0a_bv': ([1, DM], F32), 'l0a_bd': ([1, DM], F32),
    'l0a_gamma': ([1, DM], F32), 'l0a_beta': ([1, DM], F32),
    'l0a_bq_c': ([128, 8], F32),
    'ff_wqkv': ([2, DM, 1536], F32), 'ff_wo': ([2, FF_INNER, DM], F32),
    'ff_w1': ([2, DM, 2 * FF_HPAD], F32), 'ff_b1': ([2, 1, 2 * FF_HPAD], F32),
    'ff_w2': ([2, FF_HPAD, DM], F32), 'ff_b2': ([2, 1, DM], F32),
    'ff_qs': ([2, 1, 64], F32), 'ff_ks': ([2, 1, 64], F32),
    'ff_lng': ([2, 1, DM], F32), 'ff_lnb': ([2, 1, DM], F32),
    'ssd_wint': ([DM, 1184], F32), 'ssd_wout': ([DM, DM], F32),
    'ssd_dtb': ([32, 1], F32), 'ssd_alog': ([32, 1], F32), 'ssd_d': ([1, 32], F32),
    'moe_rt': ([DM, 8], F32),
    'moe_wg': ([8, DM, MOE_F], F32), 'moe_wu': ([8, DM, MOE_F], F32),
    'moe_wd': ([8, MOE_F, DM], F32),
}


class Ctx:

    def __init__(self, fused, phase_inputs=None, phase_outputs=None):
        self.nc = bass.Bass("TRN2", target_bir_lowering=False)
        self.fused = fused
        self.dram = {}
        self._stack = []
        nc = self.nc
        if fused:
            self.dram['x'] = nc.dram_tensor('x', [S, DM], F32, kind='ExternalInput').ap()
            self.dram['out'] = nc.dram_tensor('out', [S, DM], F32, kind='ExternalOutput').ap()
            for n, (shp, dt) in CONSTS.items():
                self.dram[n] = nc.dram_tensor(n, shp, dt, kind='ExternalInput').ap()
            for n, (shp, dt) in SCRATCH.items():
                self.dram[n] = nc.dram_tensor(n, shp, dt, kind='Internal').ap()
        else:
            ins = set(phase_inputs)
            outs = set(phase_outputs)
            for n in sorted(ins):
                shp, dt = self._spec(n)
                self.dram[n] = nc.dram_tensor(n, shp, dt, kind='ExternalInput').ap()
            for n in sorted(outs):
                shp, dt = self._spec(n)
                self.dram[n] = nc.dram_tensor(n, shp, dt, kind='ExternalOutput').ap()
        self.sems = {}
        self.cnt = {}
        for name in ['pe', 'dve', 'act', 'pool', 'qs', 'qg', 'qa', 'qv']:
            sem = nc.semaphore(f'sem_{name}')
            self.sems[name] = self._enter(sem)
            self.cnt[name] = 0

    def _spec(self, n):
        if n == 'x':
            return [S, DM], F32
        if n == 'out':
            return [S, DM], F32
        if n in CONSTS:
            return CONSTS[n]
        return SCRATCH[n]

    def _enter(self, cm):
        v = cm.__enter__()
        self._stack.append(cm)
        return v

    _uid = 0

    def sbuf(self, name, shape, dtype):
        Ctx._uid += 1
        return self._enter(self.nc.sbuf_tensor(f'{name}_u{Ctx._uid}', shape, dtype))

    def psum(self, name, shape, dtype=F32):
        Ctx._uid += 1
        return self._enter(self.nc.psum_tensor(f'{name}_u{Ctx._uid}', shape, dtype))

    def close(self):
        while self._stack:
            self._stack.pop().__exit__(None, None, None)

    def inc(self, name, instr, n=1):
        instr.then_inc(self.sems[name], n)
        self.cnt[name] += n
        return self.cnt[name]

    def wait(self, eng, name, val=None):
        if val is None:
            val = self.cnt[name]
        if val > 0:
            eng.wait_ge(self.sems[name], val)

    def barrier(self, engines):
        for e in engines:
            for name in self.sems:
                self.wait(e, name)


BATCH = 8


def to_bf16(a):
    return np.asarray(a).astype(ml_dtypes.bfloat16)


class Scope:
    def __init__(self, ctx):
        self.ctx = ctx
        self.mark = len(ctx._stack)

    def __enter__(self):
        return self

    def __exit__(self, *a):
        st = self.ctx._stack
        while len(st) > self.mark:
            st.pop().__exit__(None, None, None)


def p_ln0(ctx, src='x', dst_tok='XNTOK', dst_t='XNT', dst_f32='XNF32', eps=1e-5):
    nc = ctx.nc
    d = ctx.dram
    with Scope(ctx):
        xt = [ctx.sbuf(f'ln_x{i}', [128, DM], F32) for i in range(2)]
        sq = ctx.sbuf('ln_sq', [128, DM], F32)
        st = ctx.sbuf('ln_st', [128, 8], F32)
        xn = ctx.sbuf('ln_xn', [128, DM], BF16)
        xnf = ctx.sbuf('ln_xnf', [128, DM], F32)
        eyebf = ctx.sbuf('ln_eye', [128, 128], BF16)
        tacc = ctx.sbuf('ln_tacc', [128, NF * SP], BF16)
        tps = ctx.psum('ln_tps', [128, 1024], BF16)
        ld_tok = [0, 0]
        cons_tok = [0, 0]
        st_tok = 0
        tp_tok = 0
        q = nc.sync
        g = nc.gpsimd
        v = nc.vector
        s = nc.scalar
        t = nc.tensor
        ctx.inc('qs', q.dma_start(out=eyebf[:], in_=d['eyebf'][:, :]), 16)
        ctx.wait(v, 'qs')
        for k in range(NF):
            ctx.inc('dve', v.memset(tacc[:, k * SP:k * SP + 1], 0.0))
        for i in range(NT):
            b = i % 2
            if i >= 2:
                ctx.wait(q, 'dve', cons_tok[b])
            ctx.inc('qs', q.dma_start(out=xt[b][:], in_=d[src][i * 128:(i + 1) * 128, :]), 16)
            ld_tok[b] = ctx.cnt['qs']
            x = xt[b]
            ctx.wait(v, 'qs', ld_tok[b])
            ctx.inc('dve', v.reduce_sum(out=st[:, 0:1], in_=x[:], axis=AX.X))
            ctx.wait(s, 'dve')
            ctx.inc('act', s.activation(out=sq[:], in_=x[:], func=AF.Square))
            ctx.wait(v, 'act')
            ctx.inc('dve', v.reduce_sum(out=st[:, 1:2], in_=sq[:], axis=AX.X))
            v.drain()
            ctx.inc('dve', v.tensor_scalar(out=st[:, 2:3], in0=st[:, 0:1],
                                           scalar1=1.0 / DM, scalar2=None, op0=OP.mult))
            ctx.inc('dve', v.tensor_scalar(out=st[:, 3:4], in0=st[:, 1:2],
                                           scalar1=1.0 / DM, scalar2=None, op0=OP.mult))
            v.drain()
            ctx.inc('dve', v.tensor_tensor(out=st[:, 5:6], in0=st[:, 2:3],
                                           in1=st[:, 2:3], op=OP.mult))
            v.drain()
            ctx.inc('dve', v.tensor_tensor(out=st[:, 4:5], in0=st[:, 3:4],
                                           in1=st[:, 5:6], op=OP.subtract))
            v.drain()
            ctx.inc('dve', v.tensor_scalar(out=st[:, 4:5], in0=st[:, 4:5],
                                           scalar1=float(eps), scalar2=None, op0=OP.add))
            ctx.wait(s, 'dve')
            ctx.inc('act', s.activation(out=st[:, 6:7], in_=st[:, 4:5], func=AF.Sqrt))
            s.drain()
            ctx.wait(v, 'act')
            ctx.inc('dve', v.reciprocal(out=st[:, 6:7], in_=st[:, 6:7]))
            v.drain()
            ctx.inc('dve', v.tensor_tensor(out=st[:, 7:8], in0=st[:, 2:3],
                                           in1=st[:, 6:7], op=OP.mult))
            v.drain()
            ctx.inc('dve', v.tensor_scalar(out=st[:, 7:8], in0=st[:, 7:8],
                                           scalar1=-1.0, scalar2=None, op0=OP.mult))
            v.drain()
            ctx.wait(v, 'qs', st_tok)
            ctx.wait(v, 'pe', tp_tok)
            ctx.inc('dve', v.tensor_scalar(out=xnf[:], in0=x[:], scalar1=st[:, 6:7],
                                           scalar2=st[:, 7:8], op0=OP.mult, op1=OP.add))
            v.drain()
            ctx.inc('dve', v.tensor_copy(out=xn[:], in_=xnf[:]))
            cons_tok[b] = ctx.cnt['dve']
            ctx.wait(q, 'dve', cons_tok[b])
            ctx.inc('qs', q.dma_start(out=d[dst_tok][i * 128:(i + 1) * 128, :], in_=xn[:]), 16)
            if dst_f32 is not None:
                ctx.inc('qs', q.dma_start(out=d[dst_f32][i * 128:(i + 1) * 128, :], in_=xnf[:]), 16)
            st_tok = ctx.cnt['qs']
            ctx.wait(t, 'dve')
            for k in range(NF):
                mm = t.transpose(tps[:, k * 128:(k + 1) * 128], xn[:, k * 128:(k + 1) * 128],
                                 eyebf[:, 0:128])
                if k == NF - 1:
                    ctx.inc('pe', mm)
            tp_tok = ctx.cnt['pe']
            ctx.wait(v, 'pe', tp_tok)
            for k in range(NF):
                instr = v.tensor_copy(out=tacc[:, k * SP + 1 + i * 128: k * SP + 1 + (i + 1) * 128],
                                      in_=tps[:, k * 128:(k + 1) * 128])
                ctx.inc('dve', instr)
        ctx.wait(q, 'dve')
        for k in range(NF):
            ctx.inc('qs', q.dma_start(out=d[dst_t][k * 128:(k + 1) * 128, :],
                                      in_=tacc[:, k * SP:(k + 1) * SP]), 16)


def finish(ctx):
    ctx.barrier([ctx.nc.sync, ctx.nc.gpsimd])


PHASE_DEFS = {}


def run_phase_standalone(name, vals_list, inputs, outputs, emit, core_ids=None):
    ctx = Ctx(fused=False, phase_inputs=inputs, phase_outputs=outputs)
    emit(ctx)
    finish(ctx)
    ctx.close()
    in_maps = []
    for vals in vals_list:
        in_maps.append({n: vals[n] for n in inputs})
    if core_ids is None:
        core_ids = list(range(len(vals_list)))
    res = run_bass_kernel_spmd(ctx.nc, in_maps, core_ids=core_ids)
    return [{n: r[n] for n in outputs} for r in res.results]


def p_l0a_A(ctx):
    nc = ctx.nc
    d = ctx.dram
    q, g, v, s, t = nc.sync, nc.gpsimd, nc.vector, nc.scalar, nc.tensor
    with Scope(ctx):
        wk = ctx.sbuf('aA_wk', [128, NF * DM], BF16)
        wv = ctx.sbuf('aA_wv', [128, NF * DM], BF16)
        bk = ctx.sbuf('aA_bk', [1, DM], BF16)
        bv = ctx.sbuf('aA_bv', [1, DM], BF16)
        ones = ctx.sbuf('aA_ones', [128, 128], BF16)
        kt = ctx.sbuf('aA_kt', [128, NT * DM], BF16)
        vt = ctx.sbuf('aA_vt', [128, NT * DM], BF16)
        xc = [ctx.sbuf(f'aA_xc{i}', [128, NF * 128], BF16) for i in range(2)]
        t1 = ctx.sbuf('aA_t1', [128, 512], F32)
        t2 = ctx.sbuf('aA_t2', [128, 512], BF16)
        eb = ctx.sbuf('aA_eb', [128, 512], BF16)
        kvtb = ctx.sbuf('aA_kvtb', [128, L0A_NH * 64], BF16)
        ksf = ctx.sbuf('aA_ksf', [128, 8], F32)
        kdg = ctx.sbuf('aA_kdg', [128, 128], BF16)
        pk = ctx.psum('aA_pk', [128, 512], F32)
        pv = ctx.psum('aA_pv', [128, 512], F32)
        pA = ctx.psum('aA_pA', [128, 512], F32)
        pB = ctx.psum('aA_pB', [128, 512], F32)
        pks = ctx.psum('aA_pks', [128, 512], F32)
        kvbanks = [pk, pv, pA, pB]
        for k in range(NF):
            ctx.inc('qg', g.dma_start(out=wk[:, k * DM:(k + 1) * DM],
                                      in_=d['l0a_Wk'][k * 128:(k + 1) * 128, :]), 16)
            ctx.inc('qg', g.dma_start(out=wv[:, k * DM:(k + 1) * DM],
                                      in_=d['l0a_Wv'][k * 128:(k + 1) * 128, :]), 16)
        ctx.inc('qg', g.dma_start(out=bk[:, :], in_=d['l0a_bk'][:, :]), 16)
        ctx.inc('qg', g.dma_start(out=bv[:, :], in_=d['l0a_bv'][:, :]), 16)
        ctx.inc('qs', q.dma_start(out=ones[:], in_=d['onesbf'][:, :]), 16)
        wtok = ctx.cnt['qg']
        otok = ctx.cnt['qs']
        ld_tok = [0, 0]
        rel_tok = [0, 0]
        ps_rel = 0
        for i in range(NT):
            b = i % 2
            if i >= 2:
                ctx.wait(q, 'pe', rel_tok[b])
            for k in range(NF):
                ctx.inc('qs', q.dma_start(out=xc[b][:, k * 128:(k + 1) * 128],
                                          in_=d['XNT'][k * 128:(k + 1) * 128,
                                                       1 + i * 128:1 + (i + 1) * 128]), 16)
            ld_tok[b] = ctx.cnt['qs']
            ctx.wait(t, 'qs', ld_tok[b])
            if i == 0:
                ctx.wait(t, 'qg', wtok)
                ctx.wait(t, 'qs', otok)
            for half in range(2):
                ctx.wait(t, 'dve', ps_rel)
                for k in range(NF):
                    t.matmul(pk[:, :], xc[b][:, k * 128:(k + 1) * 128],
                             wk[:, k * DM + half * 512:k * DM + (half + 1) * 512],
                             start=(k == 0), stop=False)
                t.matmul(pk[:, :], ones[0:1, 0:128], bk[0:1, half * 512:(half + 1) * 512],
                         start=False, stop=True)
                for k in range(NF):
                    t.matmul(pv[:, :], xc[b][:, k * 128:(k + 1) * 128],
                             wv[:, k * DM + half * 512:k * DM + (half + 1) * 512],
                             start=(k == 0), stop=False)
                mm = t.matmul(pv[:, :], ones[0:1, 0:128], bv[0:1, half * 512:(half + 1) * 512],
                              start=False, stop=True)
                ctx.inc('pe', mm)
                if half == 1:
                    rel_tok[b] = ctx.cnt['pe']
                ctx.wait(v, 'pe')
                cbase = i * DM + half * 512
                ctx.wait(v, 'act')
                ctx.inc('dve', v.tensor_scalar(out=t1[:], in0=pk[:, :], scalar1=0.0,
                                               scalar2=None, op0=OP.min))
                ctx.wait(s, 'dve')
                ctx.inc('act', s.activation(out=eb[:], in_=t1[:], func=AF.Exp))
                ctx.inc('dve', v.tensor_scalar(out=t2[:], in0=pk[:, :], scalar1=0.0,
                                               scalar2=None, op0=OP.max))
                v.drain()
                ctx.wait(v, 'act')
                ctx.inc('dve', v.tensor_tensor(out=kt[:, cbase:cbase + 512],
                                               in0=eb[:], in1=t2[:], op=OP.add))
                ctx.inc('dve', v.tensor_copy(out=vt[:, cbase:cbase + 512], in_=pv[:, :]))
                ps_rel = ctx.cnt['dve']
        ctx.wait(t, 'dve')
        evac_tok = {}
        for h in range(L0A_NH):
            bank = kvbanks[h % 4]
            if h >= 4:
                ctx.wait(t, 'dve', evac_tok[h % 4])
            for i in range(NT):
                mm = t.matmul(bank[0:64, 0:64],
                              kt[:, i * DM + h * 64:i * DM + (h + 1) * 64],
                              vt[:, i * DM + h * 64:i * DM + (h + 1) * 64],
                              start=(i == 0), stop=(i == NT - 1))
            ctx.inc('pe', mm)
            ctx.wait(v, 'pe')
            ctx.inc('dve', v.tensor_copy(out=kvtb[0:64, h * 64:(h + 1) * 64],
                                         in_=bank[0:64, 0:64]))
            ctx.inc('dve', v.tensor_copy(out=kvtb[64:128, h * 64:(h + 1) * 64],
                                         in_=bank[0:64, 0:64]))
            evac_tok[h % 4] = ctx.cnt['dve']
        for kti in range(8):
            for i in range(NT):
                mm = t.matmul(pks[:, kti:kti + 1],
                              kt[:, i * DM + kti * 128:i * DM + (kti + 1) * 128],
                              ones[:, 0:1], start=(i == 0), stop=(i == NT - 1))
            ctx.inc('pe', mm)
        ctx.wait(v, 'pe')
        ctx.inc('dve', v.tensor_scalar(out=ksf[:], in0=pks[:, 0:8], scalar1=1e-6,
                                       scalar2=None, op0=OP.add))
        ctx.inc('dve', v.memset(kdg[:], 0.0))
        v.drain()
        for kti in range(8):
            for hh in range(2):
                h = 2 * kti + hh
                ctx.inc('dve', v.tensor_copy(out=kdg[hh * 64:(hh + 1) * 64,
                                                     kti * 16 + h:kti * 16 + h + 1],
                                             in_=ksf[hh * 64:(hh + 1) * 64, kti:kti + 1]))
        v.drain()
        ctx.wait(q, 'dve')
        ctx.inc('qs', q.dma_start(out=d['KVT'][:, :], in_=kvtb[:]), 16)
        ctx.inc('qs', q.dma_start(out=d['KDIAG'][:, :], in_=kdg[:]), 16)


def emit_lnstats(ctx, x_ap, st, sq, eps, width=DM):
    nc = ctx.nc
    v, s = nc.vector, nc.scalar
    ctx.inc('dve', v.reduce_sum(out=st[:, 0:1], in_=x_ap, axis=AX.X))
    ctx.wait(s, 'dve')
    ctx.inc('act', s.activation(out=sq[:, 0:width], in_=x_ap, func=AF.Square))
    ctx.wait(v, 'act')
    ctx.inc('dve', v.reduce_sum(out=st[:, 1:2], in_=sq[:, 0:width], axis=AX.X))
    v.drain()
    ctx.inc('dve', v.tensor_scalar(out=st[:, 2:3], in0=st[:, 0:1],
                                   scalar1=1.0 / width, scalar2=None, op0=OP.mult))
    ctx.inc('dve', v.tensor_scalar(out=st[:, 3:4], in0=st[:, 1:2],
                                   scalar1=1.0 / width, scalar2=None, op0=OP.mult))
    v.drain()
    ctx.inc('dve', v.tensor_tensor(out=st[:, 5:6], in0=st[:, 2:3],
                                   in1=st[:, 2:3], op=OP.mult))
    v.drain()
    ctx.inc('dve', v.tensor_tensor(out=st[:, 4:5], in0=st[:, 3:4],
                                   in1=st[:, 5:6], op=OP.subtract))
    v.drain()
    ctx.inc('dve', v.tensor_scalar(out=st[:, 4:5], in0=st[:, 4:5],
                                   scalar1=float(eps), scalar2=None, op0=OP.add))
    ctx.wait(s, 'dve')
    ctx.inc('act', s.activation(out=st[:, 6:7], in_=st[:, 4:5], func=AF.Sqrt))
    s.drain()
    ctx.wait(v, 'act')
    ctx.inc('dve', v.reciprocal(out=st[:, 6:7], in_=st[:, 6:7]))
    v.drain()
    ctx.inc('dve', v.tensor_tensor(out=st[:, 7:8], in0=st[:, 2:3],
                                   in1=st[:, 6:7], op=OP.mult))
    v.drain()
    ctx.inc('dve', v.tensor_scalar(out=st[:, 7:8], in0=st[:, 7:8],
                                   scalar1=-1.0, scalar2=None, op0=OP.mult))
    v.drain()


def cast_loads(ctx, dst_sb, src_name, nrow_tiles, cols, dst_stride=None):
    g = ctx.nc.gpsimd
    d = ctx.dram
    if dst_stride is None:
        dst_stride = cols
    for k in range(nrow_tiles):
        ctx.inc('qg', g.dma_start(out=dst_sb[:, k * dst_stride:k * dst_stride + cols],
                                  in_=d[src_name][k * 128:(k + 1) * 128, :]), 16)


def p_l0a_B(ctx):
    nc = ctx.nc
    d = ctx.dram
    q, g, v, s, t = nc.sync, nc.gpsimd, nc.vector, nc.scalar, nc.tensor
    with Scope(ctx):
        wq = ctx.sbuf('aB_wq', [128, NF * DM], BF16)
        wd = ctx.sbuf('aB_wd', [128, NF * DM], BF16)
        bqc = ctx.sbuf('aB_bqc', [128, 8], F32)
        bdr = ctx.sbuf('aB_bdr', [1, DM], BF16)
        kvt = ctx.sbuf('aB_kvt', [128, L0A_NH * 64], BF16)
        kdg = ctx.sbuf('aB_kdg', [128, 128], BF16)
        eye = ctx.sbuf('aB_eye', [128, 128], F32)
        ones = ctx.sbuf('aB_ones', [128, 128], BF16)
        onesf = ctx.sbuf('aB_onesf', [2, 128], F32)
        gam = ctx.sbuf('aB_gam', [1, DM], F32)
        bet = ctx.sbuf('aB_bet', [1, DM], F32)
        psel = ctx.sbuf('aB_psel', [16, 8 * 128], F32)
        gb = ctx.sbuf('aB_gb', [128, DM], F32)
        bb = ctx.sbuf('aB_bb', [128, DM], F32)
        xch = [ctx.sbuf(f'aB_xch{i}', [128, NF * 512], BF16) for i in range(2)]
        qh = ctx.sbuf('aB_qh', [128, NF * 512], BF16)
        qz = ctx.sbuf('aB_qz', [128, NF * 512], BF16)
        ctxT = ctx.sbuf('aB_ctxT', [128, NF * 512], BF16)
        t1 = ctx.sbuf('aB_t1', [128, 512], F32)
        t2 = ctx.sbuf('aB_t2', [128, 512], BF16)
        eb = ctx.sbuf('aB_eb', [128, 512], BF16)
        zt = ctx.sbuf('aB_zt', [128, 16], F32)
        ztr = ctx.sbuf('aB_ztr', [16, 512], F32)
        xtk = ctx.sbuf('aB_xtk', [128, DM], BF16)
        hs = ctx.sbuf('aB_hs', [128, DM], F32)
        hsn = ctx.sbuf('aB_hsn', [128, DM], F32)
        s0t = ctx.sbuf('aB_s0t', [128, DM], F32)
        sq = ctx.sbuf('aB_sq', [128, DM], F32)
        stt = ctx.sbuf('aB_stt', [128, 8], F32)
        pba = ctx.psum('aB_pa', [128, 512], F32)
        pbb = ctx.psum('aB_pb', [128, 512], F32)
        pbc = ctx.psum('aB_pc', [128, 512], F32)
        pbd = ctx.psum('aB_pd', [128, 512], F32)
        pbe = ctx.psum('aB_pe', [128, 512], F32)

        cast_loads(ctx, wq, 'l0a_Wq', NF, DM)
        cast_loads(ctx, wd, 'l0a_Wd', NF, DM)
        ctx.inc('qg', g.dma_start(out=bdr[:, :], in_=d['l0a_bd'][:, :]), 16)
        for nm_, sb_ in [('l0a_bq_c', bqc), ('KVT', kvt), ('KDIAG', kdg), ('eye', eye),
                         ('onesbf', ones), ('onesf', onesf), ('l0a_gamma', gam),
                         ('l0a_beta', bet), ('pairsel', psel)]:
            ctx.inc('qs', q.dma_start(out=sb_[:], in_=d[nm_][:, :]), 16)
        ctx.barrier([t])
        for oh in range(2):
            t.matmul(pba[:, :], onesf[0:1, 0:128], gam[0:1, oh * 512:(oh + 1) * 512],
                     start=True, stop=True)
            mm = t.matmul(pbb[:, :], onesf[0:1, 0:128], bet[0:1, oh * 512:(oh + 1) * 512],
                          start=True, stop=True)
            ctx.inc('pe', mm)
            ctx.wait(v, 'pe')
            ctx.inc('dve', v.tensor_copy(out=gb[:, oh * 512:(oh + 1) * 512], in_=pba[:, :]))
            ctx.inc('dve', v.tensor_copy(out=bb[:, oh * 512:(oh + 1) * 512], in_=pbb[:, :]))
            ctx.wait(t, 'dve')

        ld_tok = [0, 0]
        rel_tok = [0, 0]
        qrel = [0, 0]
        denrel = 0
        zbrel = 0
        ctxrel = [0, 0]
        wdrel = [0, 0]
        sttok = 0
        for c in range(8):
            b = c % 2
            if c >= 2:
                ctx.wait(q, 'pe', rel_tok[b])
            for k in range(NF):
                ctx.inc('qs', q.dma_start(out=xch[b][:, k * 512:(k + 1) * 512],
                                          in_=d['XNT'][k * 128:(k + 1) * 128,
                                                       1 + c * 512:1 + (c + 1) * 512]), 16)
            ld_tok[b] = ctx.cnt['qs']
            ctx.wait(t, 'qs', ld_tok[b])
            for o in range(NF):
                bank = [pba, pbb][o % 2]
                ctx.wait(t, 'dve', qrel[o % 2])
                for k in range(NF):
                    mm = t.matmul(bank[:, :], wq[:, k * DM + o * 128:k * DM + (o + 1) * 128],
                                  xch[b][:, k * 512:(k + 1) * 512],
                                  start=(k == 0), stop=(k == NF - 1))
                ctx.inc('pe', mm)
                if o == NF - 1:
                    rel_tok[b] = ctx.cnt['pe']
                ctx.wait(v, 'pe')
                ctx.wait(v, 'act')
                ctx.inc('dve', v.tensor_scalar(out=t1[:], in0=bank[:, :],
                                               scalar1=bqc[:, o:o + 1], scalar2=0.0,
                                               op0=OP.add, op1=OP.min))
                ctx.wait(s, 'dve')
                ctx.inc('act', s.activation(out=eb[:], in_=t1[:], func=AF.Exp))
                ctx.inc('dve', v.tensor_scalar(out=t2[:], in0=bank[:, :],
                                               scalar1=bqc[:, o:o + 1], scalar2=0.0,
                                               op0=OP.add, op1=OP.max))
                v.drain()
                ctx.wait(v, 'act')
                ctx.inc('dve', v.tensor_tensor(out=qh[:, o * 512:(o + 1) * 512],
                                               in0=eb[:], in1=t2[:], op=OP.add))
                qrel[o % 2] = ctx.cnt['dve']
            ctx.wait(t, 'dve')
            for j in range(4):
                ctx.wait(t, 'dve', denrel)
                for o in range(NF):
                    mm = t.matmul(pbc[:, 0:16],
                                  qh[:, o * 512 + j * 128:o * 512 + (j + 1) * 128],
                                  kdg[:, o * 16:(o + 1) * 16],
                                  start=(o == 0), stop=(o == NF - 1))
                ctx.inc('pe', mm)
                ctx.wait(v, 'pe')
                ctx.inc('dve', v.reciprocal(out=zt[:, 0:16], in_=pbc[:, 0:16]))
                denrel = ctx.cnt['dve']
                ctx.wait(t, 'dve', denrel)
                mm = t.transpose(pbd[0:16, j * 128:(j + 1) * 128], zt[:, 0:16],
                                 eye[:, 0:128])
                ctx.inc('pe', mm)
            ctx.wait(v, 'pe')
            ctx.inc('dve', v.tensor_copy(out=ztr[:, :], in_=pbd[0:16, 0:512]))
            ztok = ctx.cnt['dve']
            for jj in range(NF):
                ctx.wait(t, 'dve', max(zbrel, ztok))
                mm = t.matmul(pbe[:, :], psel[:, jj * 128:(jj + 1) * 128],
                              ztr[:, 0:512], start=True, stop=True)
                ctx.inc('pe', mm)
                ctx.wait(v, 'pe')
                ctx.inc('dve', v.tensor_tensor(out=qz[:, jj * 512:(jj + 1) * 512],
                                               in0=qh[:, jj * 512:(jj + 1) * 512],
                                               in1=pbe[:, :], op=OP.mult))
                zbrel = ctx.cnt['dve']
            ctx.wait(t, 'dve')
            for h in range(L0A_NH):
                bank = [pba, pbb][h % 2]
                ctx.wait(t, 'dve', ctxrel[h % 2])
                hb = (h % 2) * 64
                mm = t.matmul(bank[0:64, 0:512], kvt[hb:hb + 64, h * 64:(h + 1) * 64],
                              qz[hb:hb + 64, (h // 2) * 512:(h // 2 + 1) * 512],
                              start=True, stop=True)
                ctx.inc('pe', mm)
                ctx.wait(v, 'pe')
                ctx.inc('dve', v.tensor_copy(out=ctxT[(h % 2) * 64:(h % 2) * 64 + 64,
                                                      (h // 2) * 512:(h // 2 + 1) * 512],
                                             in_=bank[0:64, 0:512]))
                ctxrel[h % 2] = ctx.cnt['dve']
            ctx.wait(t, 'dve')
            for j in range(4):
                ctx.wait(q, 'dve')
                ctx.inc('qs', q.dma_start(out=xtk[:],
                                          in_=d['XNTOK'][c * 512 + j * 128:c * 512 + (j + 1) * 128, :]), 16)
                xload = ctx.cnt['qs']
                for oh in range(2):
                    bank = [pbc, pbd][oh]
                    ctx.wait(t, 'dve', wdrel[oh])
                    for m in range(NF):
                        t.matmul(bank[:, :], ctxT[:, m * 512 + j * 128:m * 512 + (j + 1) * 128],
                                 wd[:, m * DM + oh * 512:m * DM + (oh + 1) * 512],
                                 start=(m == 0), stop=False)
                    mm = t.matmul(bank[:, :], ones[0:1, 0:128],
                                  bdr[0:1, oh * 512:(oh + 1) * 512], start=False, stop=True)
                    ctx.inc('pe', mm)
                    ctx.wait(v, 'pe')
                    ctx.wait(v, 'qs', xload)
                    ctx.inc('dve', v.tensor_tensor(out=hs[:, oh * 512:(oh + 1) * 512],
                                                   in0=pbc[:, :] if oh == 0 else pbd[:, :],
                                                   in1=xtk[:, oh * 512:(oh + 1) * 512],
                                                   op=OP.add))
                    wdrel[oh] = ctx.cnt['dve']
                v.drain()
                emit_lnstats(ctx, hs[:, 0:DM], stt, sq, 1e-12)
                ctx.wait(v, 'qs', sttok)
                ctx.inc('dve', v.tensor_scalar(out=hsn[:], in0=hs[:], scalar1=stt[:, 6:7],
                                               scalar2=stt[:, 7:8], op0=OP.mult, op1=OP.add))
                v.drain()
                ctx.inc('dve', v.tensor_tensor(out=s0t[:], in0=hsn[:], in1=gb[:], op=OP.mult))
                v.drain()
                ctx.inc('dve', v.tensor_tensor(out=s0t[:], in0=s0t[:], in1=bb[:], op=OP.add))
                ctx.wait(q, 'dve')
                ctx.inc('qs', q.dma_start(out=d['S0'][c * 512 + j * 128:c * 512 + (j + 1) * 128, :],
                                          in_=s0t[:]), 16)
                sttok = ctx.cnt['qs']


def p_tr(ctx, src, dst):
    nc = ctx.nc
    d = ctx.dram
    q, g, v, t = nc.sync, nc.gpsimd, nc.vector, nc.tensor
    with Scope(ctx):
        xb = [ctx.sbuf(f'tr_x{i}', [128, DM], BF16) for i in range(2)]
        eyebf = ctx.sbuf('tr_eye', [128, 128], BF16)
        tacc = ctx.sbuf('tr_tacc', [128, NF * SP], BF16)
        tps = ctx.psum('tr_tps', [128, 1024], BF16)
        ctx.inc('qs', q.dma_start(out=eyebf[:], in_=d['eyebf'][:, :]), 16)
        ctx.wait(v, 'qs')
        for k in range(NF):
            ctx.inc('dve', v.memset(tacc[:, k * SP:k * SP + 1], 0.0))
        ld_tok = [0, 0]
        rel_tok = [0, 0]
        for i in range(NT):
            b = i % 2
            if i >= 2:
                ctx.wait(g, 'pe', rel_tok[b])
            ctx.inc('qg', g.dma_start(out=xb[b][:], in_=d[src][i * 128:(i + 1) * 128, :]), 16)
            ld_tok[b] = ctx.cnt['qg']
            ctx.wait(t, 'qg', ld_tok[b])
            ctx.wait(t, 'dve')
            for k in range(NF):
                mm = t.transpose(tps[:, k * 128:(k + 1) * 128],
                                 xb[b][:, k * 128:(k + 1) * 128], eyebf[:, 0:128])
            ctx.inc('pe', mm)
            rel_tok[b] = ctx.cnt['pe']
            ctx.wait(v, 'pe', rel_tok[b])
            for k in range(NF):
                ctx.inc('dve', v.tensor_copy(out=tacc[:, k * SP + 1 + i * 128:k * SP + 1 + (i + 1) * 128],
                                             in_=tps[:, k * 128:(k + 1) * 128]))
        ctx.wait(q, 'dve')
        for k in range(NF):
            ctx.inc('qs', q.dma_start(out=d[dst][k * 128:(k + 1) * 128, :],
                                      in_=tacc[:, k * SP:(k + 1) * SP]), 16)


def p_ffn_qkv(ctx, l, src_t):
    nc = ctx.nc
    d = ctx.dram
    q, g, v, s, t = nc.sync, nc.gpsimd, nc.vector, nc.scalar, nc.tensor
    with Scope(ctx):
        wq = ctx.sbuf('fq_w', [128, NF * 1536], BF16)
        hch = [ctx.sbuf(f'fq_h{i}', [128, NF * 128], BF16) for i in range(2)]
        taq = ctx.sbuf('fq_taq', [128, NT * 64], F32)
        tsq = ctx.sbuf('fq_tsq', [128, NT * 64], F32)
        tak = ctx.sbuf('fq_tak', [128, NT * 64], F32)
        tsk = ctx.sbuf('fq_tsk', [128, NT * 64], F32)
        rot = ctx.sbuf('fq_rot', [128, 64], F32)
        qsb = ctx.sbuf('fq_qsb', [128, 128], F32)
        qsw = ctx.sbuf('fq_qsw', [128, 128], F32)
        qsr = ctx.sbuf('fq_qsr', [1, 128], F32)
        onesf = ctx.sbuf('fq_onesf', [2, 128], F32)
        eyebf = ctx.sbuf('fq_eye', [128, 128], BF16)
        sq = ctx.sbuf('fq_sq', [128, 512], F32)
        nsq = ctx.sbuf('fq_nsq', [128, 16], F32)
        q0 = ctx.sbuf('fq_q0', [128, 512], F32)
        qa = ctx.sbuf('fq_qa', [128, 512], F32)
        qr = ctx.sbuf('fq_qr', [128, 512], F32)
        qf = ctx.sbuf('fq_qf', [128, 512], BF16)
        vh = ctx.sbuf('fq_vh', [128, 512], BF16)
        qht = ctx.sbuf('fq_qht', [128, 4 * S], BF16)
        kht = ctx.sbuf('fq_kht', [128, 4 * S], BF16)
        pq = ctx.psum('fq_pq', [128, 512], F32)
        pk = ctx.psum('fq_pk', [128, 512], F32)
        pv = ctx.psum('fq_pv', [128, 512], F32)
        ptr = ctx.psum('fq_ptr', [128, 512], BF16)
        for k in range(NF):
            ctx.inc('qg', g.dma_start(out=wq[:, k * 1536:(k + 1) * 1536],
                                      in_=d['ff_wqkv'][l, k * 128:(k + 1) * 128, :]), 16)
        ctx.inc('qs', q.dma_start(out=qsr[0:1, 0:64], in_=d['ff_qs'][l, :, :]), 16)
        ctx.inc('qs', q.dma_start(out=qsr[0:1, 64:128], in_=d['ff_ks'][l, :, :]), 16)
        ctx.inc('qs', q.dma_start(out=onesf[:], in_=d['onesf'][:, :]), 16)
        ctx.inc('qs', q.dma_start(out=eyebf[:], in_=d['eyebf'][:, :]), 16)
        ctx.barrier([t])
        t.matmul(pq[:, 0:64], onesf[0:1, 0:128], qsr[0:1, 0:64], start=True, stop=True)
        mm = t.matmul(pq[:, 64:128], onesf[0:1, 0:128], qsr[0:1, 64:128], start=True, stop=True)
        ctx.inc('pe', mm)
        ctx.wait(v, 'pe')
        ctx.inc('dve', v.tensor_copy(out=qsb[:, :], in_=pq[:, 0:128]))
        v.drain()
        ctx.inc('dve', v.tensor_scalar(out=qsb[:, 0:64], in0=qsb[:, 0:64],
                                       scalar1=float(8.0), scalar2=None, op0=OP.mult))
        v.drain()
        for base in (0, 64):
            ctx.inc('dve', v.tensor_copy(out=qsw[:, base:base + 32],
                                         in_=qsb[:, base + 32:base + 64]))
            ctx.inc('dve', v.tensor_copy(out=qsw[:, base + 32:base + 64],
                                         in_=qsb[:, base:base + 32]))
        v.drain()
        for i in range(NT):
            for (nm_, dst_, scb, sc_) in [('rot_qc', taq, qsb, 0), ('rot_qs', tsq, qsw, 0),
                                          ('rot_kc', tak, qsb, 64), ('rot_ks', tsk, qsw, 64)]:
                ctx.wait(q, 'dve')
                ctx.inc('qs', q.dma_start(out=rot[:, :],
                                          in_=d[nm_][i * 128:(i + 1) * 128, :]), 16)
                ctx.wait(v, 'qs')
                ctx.inc('dve', v.tensor_tensor(out=dst_[:, i * 64:(i + 1) * 64],
                                               in0=rot[:, :],
                                               in1=scb[:, sc_:sc_ + 64], op=OP.mult))
        qkvrel = [0, 0]
        ld_tok = [0, 0]
        tp_tok = 0
        vh_tok = [0]
        for i in range(NT):
            b = i % 2
            if i >= 2:
                ctx.wait(q, 'pe', qkvrel[b])
            for k in range(NF):
                off = 1 if k < 4 else 0
                ctx.inc('qs', q.dma_start(out=hch[b][:, k * 128:(k + 1) * 128],
                                          in_=d[src_t][k * 128:(k + 1) * 128,
                                                       off + i * 128:off + (i + 1) * 128]), 16)
            ld_tok[b] = ctx.cnt['qs']
            ctx.wait(t, 'qs', ld_tok[b])
            ctx.wait(t, 'dve')
            for p, bank in [(0, pq), (1, pk), (2, pv)]:
                for k in range(NF):
                    mm = t.matmul(bank[:, :], hch[b][:, k * 128:(k + 1) * 128],
                                  wq[:, k * 1536 + p * 512:k * 1536 + (p + 1) * 512],
                                  start=(k == 0), stop=(k == NF - 1))
            ctx.inc('pe', mm)
            qkvrel[b] = ctx.cnt['pe']
            ctx.wait(v, 'pe')
            ctx.wait(v, 'qs', vh_tok[0])
            ctx.inc('dve', v.tensor_copy(out=vh[:], in_=pv[:, :]))
            ctx.wait(q, 'dve')
            ctx.inc('qs', q.dma_start(out=d['VH'][i * 128:(i + 1) * 128, :], in_=vh[:]), 16)
            vh_tok[0] = ctx.cnt['qs']
            for (bank, tc, tss, scoff, acc) in [(pq, taq, tsq, 0, qht), (pk, tak, tsk, 8, kht)]:
                ctx.wait(s, 'pe')
                ctx.wait(s, 'dve')
                ctx.inc('act', s.activation(out=sq[:], in_=bank[:, :], func=AF.Square))
                ctx.wait(v, 'act')
                ctx.inc('dve', v.tensor_reduce(out=nsq[:, scoff:scoff + 8].unsqueeze(2),
                                               in_=sq[:].rearrange('p (h e) -> p h e', h=8),
                                               axis=AX.X, op=OP.add))
                v.drain()
                ctx.wait(s, 'dve')
                ctx.inc('act', s.activation(out=nsq[:, scoff:scoff + 8],
                                            in_=nsq[:, scoff:scoff + 8], func=AF.Sqrt))
                s.drain()
                ctx.wait(v, 'act')
                ctx.inc('dve', v.tensor_scalar(out=nsq[:, scoff:scoff + 8],
                                               in0=nsq[:, scoff:scoff + 8],
                                               scalar1=1e-12, scalar2=None, op0=OP.max))
                v.drain()
                ctx.inc('dve', v.reciprocal(out=nsq[:, scoff:scoff + 8],
                                            in_=nsq[:, scoff:scoff + 8]))
                v.drain()
                rn3 = nsq[:, scoff:scoff + 8].unsqueeze(2).broadcast_to([128, 8, 64])
                ctx.inc('dve', v.tensor_tensor(out=q0[:].rearrange('p (h e) -> p h e', h=8),
                                               in0=bank[:, :].rearrange('p (h e) -> p h e', h=8),
                                               in1=rn3, op=OP.mult))
                v.drain()
                dvtok = ctx.cnt['dve']
                tc3 = tc[:, i * 64:(i + 1) * 64].unsqueeze(1).broadcast_to([128, 8, 64])
                ctx.wait(g, 'dve', dvtok)
                ctx.inc('pool', g.tensor_tensor(out=qa[:].rearrange('p (h e) -> p h e', h=8),
                                                in0=q0[:].rearrange('p (h e) -> p h e', h=8),
                                                in1=tc3, op=OP.mult))
                tlo = tss[:, i * 64:i * 64 + 32].unsqueeze(1).broadcast_to([128, 8, 32])
                thi = tss[:, i * 64 + 32:(i + 1) * 64].unsqueeze(1).broadcast_to([128, 8, 32])
                q03 = q0[:].rearrange('p (h e) -> p h e', h=8)
                qr3 = qr[:].rearrange('p (h e) -> p h e', h=8)
                ctx.inc('pool', g.tensor_tensor(out=qr3[:, :, 0:32], in0=q03[:, :, 32:64],
                                                in1=tlo, op=OP.mult))
                ctx.inc('pool', g.tensor_tensor(out=qr3[:, :, 32:64], in0=q03[:, :, 0:32],
                                                in1=thi, op=OP.mult))
                g.drain()
                ptok = ctx.cnt['pool']
                ctx.wait(g, 'pe', tp_tok)
                ctx.inc('pool', g.tensor_tensor(out=qf[:], in0=qa[:], in1=qr[:], op=OP.add))
                ptok = ctx.cnt['pool']
                ctx.wait(t, 'pool', ptok)
                for db in range(4):
                    mm = t.transpose(ptr[:, db * 128:(db + 1) * 128],
                                     qf[:, db * 128:(db + 1) * 128], eyebf[:, 0:128])
                ctx.inc('pe', mm)
                tp_tok = ctx.cnt['pe']
                ctx.wait(v, 'pe', tp_tok)
                for db in range(4):
                    ctx.inc('dve', v.tensor_copy(out=acc[:, db * S + i * 128:db * S + (i + 1) * 128],
                                                 in_=ptr[:, db * 128:(db + 1) * 128]))
        ctx.wait(q, 'dve')
        for db in range(4):
            ctx.inc('qs', q.dma_start(out=d['QHT'][db * 128:(db + 1) * 128, :],
                                      in_=qht[:, db * S:(db + 1) * S]), 16)
            ctx.inc('qs', q.dma_start(out=d['KHT'][db * 128:(db + 1) * 128, :],
                                      in_=kht[:, db * S:(db + 1) * S]), 16)


def p_ffn_att(ctx, l):
    nc = ctx.nc
    d = ctx.dram
    q, g, v, s, t = nc.sync, nc.gpsimd, nc.vector, nc.scalar, nc.tensor
    with Scope(ctx):
        qht = ctx.sbuf('fa_qht', [128, 4 * S], BF16)
        kht = ctx.sbuf('fa_kht', [128, 4 * S], BF16)
        vhs = ctx.sbuf('fa_vh', [128, NT * FF_INNER], BF16)
        otacc = ctx.sbuf('fa_ot', [128, 4 * S], BF16)
        mgen = ctx.sbuf('fa_mgen', [128, 4 * 640], BF16)
        mw0 = ctx.sbuf('fa_mw0', [128, 4 * 512], BF16)
        eyebf = ctx.sbuf('fa_eye', [128, 128], BF16)
        mcol = [ctx.sbuf(f'fa_mc{i}', [128, 4], F32) for i in range(2)]
        ebuf = [ctx.sbuf(f'fa_e{i}', [128, 640], BF16) for i in range(2)]
        att = [ctx.sbuf(f'fa_at{i}', [128, 640], BF16) for i in range(2)]
        atT = [ctx.sbuf(f'fa_atT{i}', [128, 5 * 128], BF16) for i in range(2)]
        ovec = [ctx.sbuf(f'fa_ov{i}', [64, 512], BF16) for i in range(2)]
        psim = [ctx.psum(f'fa_psim{i}', [128, 1024], F32) for i in range(2)]
        ptr = [ctx.psum(f'fa_ptr{i}', [128, 640], BF16) for i in range(2)]
        po = [ctx.psum(f'fa_po{i}', [64, 512], F32) for i in range(2)]
        for db in range(4):
            ctx.inc('qs', q.dma_start(out=qht[:, db * S:(db + 1) * S],
                                      in_=d['QHT'][db * 128:(db + 1) * 128, :]), 16)
            ctx.inc('qs', q.dma_start(out=kht[:, db * S:(db + 1) * S],
                                      in_=d['KHT'][db * 128:(db + 1) * 128, :]), 16)
        for i in range(NT):
            ctx.inc('qg', g.dma_start(out=vhs[:, i * FF_INNER:(i + 1) * FF_INNER],
                                      in_=d['VH'][i * 128:(i + 1) * 128, :]), 16)
        for isub in range(4):
            ctx.inc('qs', q.dma_start(out=mgen[:, isub * 640:(isub + 1) * 640],
                                      in_=d['mask_gen'][isub * 128:(isub + 1) * 128,
                                                        isub * 128:isub * 128 + 640]), 16)
            ctx.inc('qs', q.dma_start(out=mw0[:, isub * 512:(isub + 1) * 512],
                                      in_=d['mask_w0'][isub * 128:(isub + 1) * 128,
                                                       512:1024]), 16)
        ctx.inc('qs', q.dma_start(out=eyebf[:], in_=d['eyebf'][:, :]), 16)
        ctx.barrier([t, v, s, g])
        units = []
        for h in range(FF_HEADS):
            for w in range(8):
                for isub in range(4):
                    units.append((h, w, isub))
        smtok = [0, 0]
        simtok = [0, 0]
        expdone = [0, 0]
        atttok = [0, 0]
        trtok = [0, 0]
        evactok = [0, 0]
        otok = [0, 0]
        ostok = [0, 0]
        uid = 0
        for (h, w, isub) in units:
            par = uid % 2
            hwpar = (h * 8 + w) % 2
            uid += 1
            hb = (h % 2) * 64
            db = h // 2
            if w == 0:
                W = (isub + 1) * 128
                kcol0 = 0
                mask_ap = mw0[:, isub * 512:isub * 512 + W]
            else:
                W = 640
                kcol0 = (w - 1) * FF_WIN + isub * 128
                mask_ap = mgen[:, isub * 640:isub * 640 + W]
            ctx.wait(t, 'act', expdone[par])
            ctx.wait(t, 'dve', smtok[par])
            lhs = qht[hb:hb + 64, db * S + w * FF_WIN + isub * 128:
                      db * S + w * FF_WIN + (isub + 1) * 128]
            done = 0
            while done < W:
                nn = min(512, W - done)
                t.matmul(psim[par][:, done:done + nn], lhs,
                         kht[hb:hb + 64, db * S + kcol0 + done:
                             db * S + kcol0 + done + nn],
                         start=True, stop=False)
                mm = t.matmul(psim[par][:, done:done + nn], eyebf[:, 0:128],
                              mask_ap[:, done:done + nn], start=False, stop=True)
                done += nn
            ctx.inc('pe', mm)
            simtok[par] = ctx.cnt['pe']
            ctx.wait(v, 'pe', simtok[par])
            ctx.wait(v, 'act', expdone[par])
            ctx.inc('dve', v.reduce_max(out=mcol[par][:, 0:1], in_=psim[par][:, 0:W], axis=AX.X))
            v.drain()
            ctx.inc('dve', v.tensor_scalar(out=mcol[par][:, 1:2], in0=mcol[par][:, 0:1],
                                           scalar1=-1.0, scalar2=None, op0=OP.mult))
            v.drain()
            smtok[par] = ctx.cnt['dve']
            ctx.wait(s, 'dve', smtok[par])
            ctx.inc('act', s.activation(out=ebuf[par][:, 0:W], in_=psim[par][:, 0:W],
                                        func=AF.Exp, bias=mcol[par][:, 1:2], scale=1.0,
                                        accum_out=mcol[par][:, 2:3]))
            s.drain()
            expdone[par] = ctx.cnt['act']
            ctx.wait(v, 'act', expdone[par])
            ctx.inc('dve', v.reciprocal(out=mcol[par][:, 3:4], in_=mcol[par][:, 2:3]))
            v.drain()
            ctx.wait(v, 'pe', trtok[par])
            ctx.inc('dve', v.tensor_scalar(out=att[par][:, 0:W], in0=ebuf[par][:, 0:W],
                                           scalar1=mcol[par][:, 3:4], scalar2=None,
                                           op0=OP.mult))
            atttok[par] = ctx.cnt['dve']
            ctx.wait(t, 'dve', atttok[par])
            nb = W // 128
            for bbk in range(nb):
                mm = t.transpose(ptr[par][:, bbk * 128:(bbk + 1) * 128],
                                 att[par][:, bbk * 128:(bbk + 1) * 128], eyebf[:, 0:128])
            ctx.inc('pe', mm)
            trtok[par] = ctx.cnt['pe']
            ctx.wait(s, 'pe', trtok[par])
            ctx.wait(s, 'pe', otok[par])
            ctx.inc('act', s.copy(out=atT[par][:, 0:nb * 128], in_=ptr[par][:, 0:nb * 128]))
            s.drain()
            evactok[par] = ctx.cnt['act']
            ctx.wait(t, 'act', evactok[par])
            if isub == 0:
                ctx.wait(t, 'dve', ostok[hwpar] and 0)
                ctx.wait(t, 'qs', ostok[hwpar])
            for bi in range(nb):
                ktok = kcol0 + bi * 128
                vtile = ktok // 128
                mm = t.matmul(po[hwpar][0:64, isub * 128:(isub + 1) * 128],
                              vhs[:, vtile * FF_INNER + h * 64:
                                  vtile * FF_INNER + (h + 1) * 64],
                              atT[par][:, bi * 128:(bi + 1) * 128],
                              start=(bi == 0), stop=(bi == nb - 1))
            ctx.inc('pe', mm)
            otok[par] = ctx.cnt['pe']
            if isub == 3:
                ctx.wait(v, 'pe', otok[par])
                ctx.wait(v, 'qs', ostok[hwpar])
                ctx.inc('dve', v.tensor_copy(out=ovec[hwpar][:, :], in_=po[hwpar][0:64, 0:512]))
                ctx.wait(q, 'dve')
                ctx.inc('qs', q.dma_start(
                    out=d['OT'].rearrange('(a p) s -> a p s', p=128)[db, hb:hb + 64,
                        w * FF_WIN:(w + 1) * FF_WIN],
                    in_=ovec[hwpar][:, :]), 16)
                ostok[hwpar] = ctx.cnt['qs']


def p_ffn_wo(ctx, l, res_src, res_f32):
    nc = ctx.nc
    d = ctx.dram
    q, g, v, s, t = nc.sync, nc.gpsimd, nc.vector, nc.scalar, nc.tensor
    with Scope(ctx):
        wo = ctx.sbuf('fw_wo', [128, 4 * DM], BF16)
        och = [ctx.sbuf(f'fw_o{i}', [128, 4 * 128], BF16) for i in range(2)]
        res = ctx.sbuf('fw_res', [128, DM], F32 if res_f32 else BF16)
        xo = ctx.sbuf('fw_xo', [128, DM], F32)
        p1 = ctx.psum('fw_p1', [128, 512], F32)
        p2 = ctx.psum('fw_p2', [128, 512], F32)
        for k in range(4):
            ctx.inc('qg', g.dma_start(out=wo[:, k * DM:(k + 1) * DM],
                                      in_=d['ff_wo'][l, k * 128:(k + 1) * 128, :]), 16)
        ctx.barrier([t])
        ld_tok = [0, 0]
        rel_tok = [0, 0]
        st_tok = 0
        for i in range(NT):
            b = i % 2
            if i >= 2:
                ctx.wait(q, 'pe', rel_tok[b])
            for k in range(4):
                ctx.inc('qs', q.dma_start(out=och[b][:, k * 128:(k + 1) * 128],
                                          in_=d['OT'][k * 128:(k + 1) * 128,
                                                      i * 128:(i + 1) * 128]), 16)
            ctx.wait(q, 'dve')
            ctx.inc('qs', q.dma_start(out=res[:], in_=d[res_src][i * 128:(i + 1) * 128, :]), 16)
            ld_tok[b] = ctx.cnt['qs']
            ctx.wait(t, 'qs', ld_tok[b])
            ctx.wait(t, 'dve')
            for oh, bank in [(0, p1), (1, p2)]:
                for k in range(4):
                    mm = t.matmul(bank[:, :], och[b][:, k * 128:(k + 1) * 128],
                                  wo[:, k * DM + oh * 512:k * DM + (oh + 1) * 512],
                                  start=(k == 0), stop=(k == 3))
                ctx.inc('pe', mm)
            rel_tok[b] = ctx.cnt['pe']
            ctx.wait(v, 'pe')
            ctx.wait(v, 'qs', st_tok)
            for oh, bank in [(0, p1), (1, p2)]:
                ctx.inc('dve', v.tensor_tensor(out=xo[:, oh * 512:(oh + 1) * 512],
                                               in0=bank[:, :],
                                               in1=res[:, oh * 512:(oh + 1) * 512], op=OP.add))
            ctx.wait(q, 'dve')
            ctx.inc('qs', q.dma_start(out=d['XIA'][i * 128:(i + 1) * 128, :], in_=xo[:]), 16)
            st_tok = ctx.cnt['qs']


def p_ffn_ffn(ctx, l, dst):
    nc = ctx.nc
    d = ctx.dram
    q, g, v, s, t = nc.sync, nc.gpsimd, nc.vector, nc.scalar, nc.tensor
    NJ = FF_HPAD // 128
    with Scope(ctx):
        hch0 = ctx.sbuf('fn_h0', [128, NF * 512], BF16)
        hch = [hch0, hch0]
        lnh = ctx.sbuf('fn_lnh', [128, NF * 512], BF16)
        lnh1 = ctx.sbuf('fn_lnh1', [128, 512], BF16)
        w1s = ctx.sbuf('fn_w1s', [128, NF * 2 * FF_HPAD], BF16)
        w2b = ctx.sbuf('fn_w2b', [128, NJ * DM], BF16)
        hsb = ctx.sbuf('fn_hsb', [128, NJ * 512], BF16)
        ggb = ctx.sbuf('fn_ggb', [128, 1024], BF16)
        b1r = ctx.sbuf('fn_b1r', [1, 2 * FF_HPAD], BF16)
        b2r = ctx.sbuf('fn_b2r', [1, DM], BF16)
        gamr = ctx.sbuf('fn_gamr', [1, DM], F32)
        betr = ctx.sbuf('fn_betr', [1, DM], F32)
        onesb = ctx.sbuf('fn_onesb', [128, 512], BF16)
        onesf = ctx.sbuf('fn_onesf', [1, 512], F32)
        srow = ctx.sbuf('fn_srow', [1, 4 * 512], F32)
        xi = ctx.sbuf('fn_xi', [128, 512], F32)
        pls = [ctx.psum(f'fn_p{i}', [128, 512], F32) for i in range(8)]
        pS, pS2, pAb, pBb, pa, pg_, pw1, pw2 = pls
        W1W = 2 * FF_HPAD
        for k in range(NF):
            ctx.inc('qg', g.dma_start(out=w1s[:, k * W1W:(k + 1) * W1W],
                                      in_=d['ff_w1'][l, k * 128:(k + 1) * 128, :]), 16)
        for j in range(NJ):
            ctx.inc('qg', g.dma_start(out=w2b[:, j * DM:(j + 1) * DM],
                                      in_=d['ff_w2'][l, j * 128:(j + 1) * 128, :]), 16)
        ctx.inc('qg', g.dma_start(out=b1r[:, :], in_=d['ff_b1'][l, :, :]), 16)
        ctx.inc('qg', g.dma_start(out=b2r[:, :], in_=d['ff_b2'][l, :, :]), 16)
        ctx.inc('qg', g.memset(onesb[:], 1.0), 1)
        ctx.inc('qg', g.memset(onesf[:], 1.0), 1)
        ctx.inc('qs', q.dma_start(out=gamr[:, :], in_=d['ff_lng'][l, :, :]), 16)
        ctx.inc('qs', q.dma_start(out=betr[:, :], in_=d['ff_lnb'][l, :, :]), 16)
        ctx.barrier([t])
        ld_tok = [0, 0]
        hrel = [0, 0]
        st_tok = 0
        w1ld = [0, 0]
        w1rel = [0, 0]
        for c in range(8):
            b = c % 2
            if c >= 2:
                ctx.wait(q, 'dve', hrel[b])
            for k in range(NF):
                off = 1 if k < 4 else 0
                ctx.inc('qs', q.dma_start(out=hch[b][:, k * 512:(k + 1) * 512],
                                          in_=d['XITA'][k * 128:(k + 1) * 128,
                                                        off + c * 512:off + (c + 1) * 512]), 16)
            ld_tok[b] = ctx.cnt['qs']
            ctx.wait(t, 'qs', ld_tok[b])
            ctx.wait(t, 'dve')
            ctx.wait(t, 'act')
            for k in range(NF):
                mm = t.matmul(pS[0:1, :], onesb[:, 0:1], hch[b][:, k * 512:(k + 1) * 512],
                              start=(k == 0), stop=(k == NF - 1))
            ctx.inc('pe', mm)
            sqtok = 0
            sqb = ggb[:, 0:512]
            for k in range(NF):
                ctx.wait(s, 'qs', ld_tok[b])
                if k == 0:
                    ctx.wait(s, 'dve')
                if k >= 1:
                    ctx.wait(s, 'pe', sqtok)
                ctx.inc('act', s.activation(out=sqb, in_=hch[b][:, k * 512:(k + 1) * 512],
                                            func=AF.Square))
                ctx.wait(t, 'act')
                mm = t.matmul(pS2[0:1, :], onesb[:, 0:1], sqb,
                              start=(k == 0), stop=(k == NF - 1))
                ctx.inc('pe', mm)
                sqtok = ctx.cnt['pe']
            ctx.wait(v, 'pe')
            ctx.inc('dve', v.tensor_scalar(out=srow[:, 0:512], in0=pS[0:1, 0:512],
                                           scalar1=1.0 / DM, scalar2=None, op0=OP.mult))
            ctx.inc('dve', v.tensor_scalar(out=srow[:, 512:1024], in0=pS2[0:1, 0:512],
                                           scalar1=1.0 / DM, scalar2=None, op0=OP.mult))
            v.drain()
            ctx.inc('dve', v.tensor_tensor(out=srow[:, 1024:1536], in0=srow[:, 0:512],
                                           in1=srow[:, 0:512], op=OP.mult))
            v.drain()
            ctx.inc('dve', v.tensor_tensor(out=srow[:, 1536:2048], in0=srow[:, 512:1024],
                                           in1=srow[:, 1024:1536], op=OP.subtract))
            v.drain()
            ctx.inc('dve', v.tensor_scalar(out=srow[:, 1536:2048], in0=srow[:, 1536:2048],
                                           scalar1=1e-5, scalar2=None, op0=OP.add))
            ctx.wait(s, 'dve')
            ctx.inc('act', s.activation(out=srow[:, 512:1024], in_=srow[:, 1536:2048],
                                        func=AF.Sqrt))
            s.drain()
            ctx.wait(v, 'act')
            ctx.inc('dve', v.reciprocal(out=srow[:, 512:1024], in_=srow[:, 512:1024]))
            v.drain()
            ctx.inc('dve', v.tensor_tensor(out=srow[:, 1024:1536], in0=srow[:, 0:512],
                                           in1=srow[:, 512:1024], op=OP.mult))
            v.drain()
            ctx.inc('dve', v.tensor_scalar(out=srow[:, 1024:1536], in0=srow[:, 1024:1536],
                                           scalar1=-1.0, scalar2=None, op0=OP.mult))
            v.drain()
            for kf in range(NF):
                ctx.wait(t, 'dve')
                t.matmul(pAb[:, :], gamr[0:1, kf * 128:(kf + 1) * 128],
                         srow[0:1, 512:1024], start=True, stop=True)
                t.matmul(pBb[:, :], gamr[0:1, kf * 128:(kf + 1) * 128],
                         srow[0:1, 1024:1536], start=True, stop=False)
                mm = t.matmul(pBb[:, :], betr[0:1, kf * 128:(kf + 1) * 128],
                              onesf[0:1, 0:512], start=False, stop=True)
                ctx.inc('pe', mm)
                ctx.wait(v, 'pe')
                ctx.wait(v, 'qs', st_tok)
                ctx.inc('dve', v.tensor_tensor(out=lnh1[:], in0=hch[b][:, kf * 512:(kf + 1) * 512],
                                               in1=pAb[:, :], op=OP.mult))
                v.drain()
                ctx.inc('dve', v.tensor_tensor(out=lnh[:, kf * 512:(kf + 1) * 512],
                                               in0=lnh1[:], in1=pBb[:, :], op=OP.add))
            hrel[b] = ctx.cnt['dve']
            if c == 0:
                ctx.wait(t, 'qg')
            ctx.wait(t, 'dve')
            jtt = [0, 0]
            jgg = [0, 0]
            for j in range(NJ):
                pa_, pg2 = [(pS, pS2), (pAb, pBb)][j % 2]
                ctx.wait(t, 'dve', jtt[j % 2])
                ctx.wait(t, 'act', jgg[j % 2])
                for k in range(NF):
                    t.matmul(pa_[:, :], w1s[:, k * W1W + j * 128:k * W1W + (j + 1) * 128],
                             lnh[:, k * 512:(k + 1) * 512], start=(k == 0), stop=False)
                t.matmul(pa_[:, :], b1r[0:1, j * 128:(j + 1) * 128], onesb[0:1, 0:512],
                         start=False, stop=True)
                for k in range(NF):
                    t.matmul(pg2[:, :], w1s[:, k * W1W + FF_HPAD + j * 128:k * W1W + FF_HPAD + (j + 1) * 128],
                             lnh[:, k * 512:(k + 1) * 512], start=(k == 0), stop=False)
                mm = t.matmul(pg2[:, :], b1r[0:1, FF_HPAD + j * 128:FF_HPAD + (j + 1) * 128],
                              onesb[0:1, 0:512], start=False, stop=True)
                ctx.inc('pe', mm)
                ctx.wait(s, 'pe')
                ctx.wait(s, 'dve', jtt[j % 2])
                ctx.inc('act', s.activation(out=ggb[:, (j % 2) * 512:(j % 2 + 1) * 512],
                                            in_=pg2[:, :], func=AF.Gelu))
                jgg[j % 2] = ctx.cnt['act']
                ctx.wait(v, 'act')
                ctx.inc('dve', v.tensor_tensor(out=hsb[:, j * 512:(j + 1) * 512],
                                               in0=pa_[:, :],
                                               in1=ggb[:, (j % 2) * 512:(j % 2 + 1) * 512],
                                               op=OP.mult))
                jtt[j % 2] = ctx.cnt['dve']
            for tsub in range(4):
                ctx.wait(t, 'dve')
                for oh, bank in [(0, pw1), (1, pw2)]:
                    for j in range(NJ):
                        t.matmul(bank[:, :], hsb[:, j * 512 + tsub * 128:j * 512 + (tsub + 1) * 128],
                                 w2b[:, j * DM + oh * 512:j * DM + (oh + 1) * 512],
                                 start=(j == 0), stop=False)
                    mm = t.matmul(bank[:, :], onesb[0:1, 0:128],
                                  b2r[0:1, oh * 512:(oh + 1) * 512], start=False, stop=True)
                    ctx.inc('pe', mm)
                ctx.wait(v, 'pe')
                for oh, bank in [(0, pw1), (1, pw2)]:
                    ctx.wait(q, 'dve')
                    ctx.inc('qs', q.dma_start(
                        out=xi[:, 0:512],
                        in_=d['XIA'][c * 512 + tsub * 128:c * 512 + (tsub + 1) * 128,
                                     oh * 512:(oh + 1) * 512]), 16)
                    xitok = ctx.cnt['qs']
                    ctx.wait(v, 'qs', xitok)
                    ctx.wait(v, 'qs', st_tok)
                    ov = lnh[:, 0:1024].bitcast(F32)
                    ctx.inc('dve', v.tensor_tensor(out=ov,
                                                   in0=bank[:, :],
                                                   in1=xi[:, 0:512], op=OP.add))
                    ctx.wait(q, 'dve')
                    ctx.inc('qs', q.dma_start(
                        out=d[dst][c * 512 + tsub * 128:c * 512 + (tsub + 1) * 128,
                                   oh * 512:(oh + 1) * 512],
                        in_=ov), 16)
                    st_tok = ctx.cnt['qs']


def p_comb0(ctx):
    nc = ctx.nc
    d = ctx.dram
    q, g, v, s, t = nc.sync, nc.gpsimd, nc.vector, nc.scalar, nc.tensor
    with Scope(ctx):
        xt = [ctx.sbuf(f'cb_x{i}', [128, DM], F32) for i in range(2)]
        st_ = [ctx.sbuf(f'cb_s{i}', [128, DM], F32) for i in range(2)]
        gt = [ctx.sbuf(f'cb_g{i}', [128, DM], F32) for i in range(2)]
        sg = ctx.sbuf('cb_sg', [128, DM], F32)
        x1 = ctx.sbuf('cb_xr', [128, DM], F32)
        sq = ctx.sbuf('cb_sq', [128, DM], F32)
        stt = ctx.sbuf('cb_st', [128, 8], F32)
        xnf = ctx.sbuf('cb_xnf', [128, DM], F32)
        xn = ctx.sbuf('cb_xn', [128, DM], BF16)
        xnlo = ctx.sbuf('cb_xnlo', [128, DM], BF16)
        eyebf = ctx.sbuf('cb_eye', [128, 128], BF16)
        tacc = ctx.sbuf('cb_tacc', [128, NF * SP], BF16)
        tacc2 = ctx.sbuf('cb_tacc2', [128, NF * SP], BF16)
        tps = ctx.psum('cb_tps', [128, 1024], BF16)
        tps2 = ctx.psum('cb_tps2', [128, 1024], BF16)
        ctx.inc('qs', q.dma_start(out=eyebf[:], in_=d['eyebf'][:, :]), 16)
        ctx.wait(v, 'qs')
        for k in range(NF):
            ctx.inc('dve', v.memset(tacc[:, k * SP:k * SP + 1], 0.0))
            ctx.inc('dve', v.memset(tacc2[:, k * SP:k * SP + 1], 0.0))
        ld_tok = [0, 0]
        cons_tok = [0, 0]
        st_tok = 0
        tp_tok = 0
        for i in range(NT):
            b = i % 2
            if i >= 2:
                ctx.wait(q, 'dve', cons_tok[b])
                ctx.wait(g, 'dve', cons_tok[b])
            ctx.inc('qs', q.dma_start(out=xt[b][:], in_=d['x'][i * 128:(i + 1) * 128, :]), 16)
            ctx.inc('qs', q.dma_start(out=st_[b][:], in_=d['S0'][i * 128:(i + 1) * 128, :]), 16)
            ctx.inc('qg', g.dma_start(out=gt[b][:], in_=d['FF'][i * 128:(i + 1) * 128, :]), 16)
            ld_tok[b] = (ctx.cnt['qs'], ctx.cnt['qg'])
            ctx.wait(s, 'qs', ld_tok[b][0])
            ctx.wait(s, 'qg', ld_tok[b][1])
            ctx.wait(s, 'dve')
            ctx.inc('act', s.activation(out=gt[b][:], in_=gt[b][:], func=AF.Silu))
            s.drain()
            ctx.wait(v, 'act')
            ctx.wait(v, 'qs', ld_tok[b][0])
            ctx.inc('dve', v.tensor_tensor(out=sg[:], in0=st_[b][:], in1=gt[b][:], op=OP.mult))
            v.drain()
            ctx.wait(v, 'qs', st_tok)
            ctx.inc('dve', v.tensor_tensor(out=x1[:], in0=xt[b][:], in1=sg[:], op=OP.add))
            cons_tok[b] = ctx.cnt['dve']
            v.drain()
            ctx.wait(q, 'dve', cons_tok[b])
            ctx.inc('qs', q.dma_start(out=d['X1'][i * 128:(i + 1) * 128, :], in_=x1[:]), 16)
            emit_lnstats(ctx, x1[:, 0:DM], stt, sq, 1e-5)
            ctx.wait(v, 'pe', tp_tok)
            ctx.inc('dve', v.tensor_scalar(out=xnf[:], in0=x1[:], scalar1=stt[:, 6:7],
                                           scalar2=stt[:, 7:8], op0=OP.mult, op1=OP.add))
            v.drain()
            ctx.inc('dve', v.tensor_copy(out=xn[:], in_=xnf[:]))
            v.drain()
            ctx.inc('dve', v.tensor_tensor(out=xnlo[:], in0=xnf[:], in1=xn[:], op=OP.subtract))
            v.drain()
            ctx.wait(q, 'dve')
            ctx.inc('qs', q.dma_start(out=d['XN1TOK'][i * 128:(i + 1) * 128, :], in_=xn[:]), 16)
            st_tok = ctx.cnt['qs']
            ctx.wait(t, 'dve')
            for k in range(NF):
                t.transpose(tps[:, k * 128:(k + 1) * 128], xn[:, k * 128:(k + 1) * 128],
                            eyebf[:, 0:128])
                mm = t.transpose(tps2[:, k * 128:(k + 1) * 128], xnlo[:, k * 128:(k + 1) * 128],
                                 eyebf[:, 0:128])
            ctx.inc('pe', mm)
            tp_tok = ctx.cnt['pe']
            ctx.wait(v, 'pe', tp_tok)
            for k in range(NF):
                ctx.inc('dve', v.tensor_copy(out=tacc[:, k * SP + 1 + i * 128:k * SP + 1 + (i + 1) * 128],
                                             in_=tps[:, k * 128:(k + 1) * 128]))
                ctx.inc('dve', v.tensor_copy(out=tacc2[:, k * SP + 1 + i * 128:k * SP + 1 + (i + 1) * 128],
                                             in_=tps2[:, k * 128:(k + 1) * 128]))
        ctx.wait(q, 'dve')
        for k in range(NF):
            ctx.inc('qs', q.dma_start(out=d['XN1T'][k * 128:(k + 1) * 128, :],
                                      in_=tacc[:, k * SP:(k + 1) * SP]), 16)
            ctx.inc('qs', q.dma_start(out=d['XN1TLO'][k * 128:(k + 1) * 128, :],
                                      in_=tacc2[:, k * SP:(k + 1) * SP]), 16)


def p_final(ctx):
    nc = ctx.nc
    d = ctx.dram
    q, g, v, s, t = nc.sync, nc.gpsimd, nc.vector, nc.scalar, nc.tensor
    with Scope(ctx):
        xt = [ctx.sbuf(f'fi_x{i}', [128, DM], F32) for i in range(2)]
        st_ = [ctx.sbuf(f'fi_s{i}', [128, DM], F32) for i in range(2)]
        gt = [ctx.sbuf(f'fi_g{i}', [128, DM], F32) for i in range(2)]
        sg = ctx.sbuf('fi_sg', [128, DM], F32)
        x1 = ctx.sbuf('fi_xr', [128, DM], F32)
        ld_tok = [0, 0]
        cons_tok = [0, 0]
        st_tok = 0
        for i in range(NT):
            b = i % 2
            if i >= 2:
                ctx.wait(q, 'dve', cons_tok[b])
                ctx.wait(g, 'dve', cons_tok[b])
            ctx.inc('qs', q.dma_start(out=xt[b][:], in_=d['X1'][i * 128:(i + 1) * 128, :]), 16)
            ctx.inc('qs', q.dma_start(out=st_[b][:], in_=d['S1'][i * 128:(i + 1) * 128, :]), 16)
            ctx.inc('qg', g.dma_start(out=gt[b][:], in_=d['MOEACC'][i * 128:(i + 1) * 128, :]), 16)
            ld_tok[b] = (ctx.cnt['qs'], ctx.cnt['qg'])
            ctx.wait(s, 'qs', ld_tok[b][0])
            ctx.wait(s, 'qg', ld_tok[b][1])
            ctx.wait(s, 'dve')
            ctx.inc('act', s.activation(out=gt[b][:], in_=gt[b][:], func=AF.Silu))
            s.drain()
            ctx.wait(v, 'act')
            ctx.wait(v, 'qs', ld_tok[b][0])
            ctx.inc('dve', v.tensor_tensor(out=sg[:], in0=st_[b][:], in1=gt[b][:], op=OP.mult))
            v.drain()
            ctx.wait(v, 'qs', st_tok)
            ctx.inc('dve', v.tensor_tensor(out=x1[:], in0=xt[b][:], in1=sg[:], op=OP.add))
            cons_tok[b] = ctx.cnt['dve']
            ctx.wait(q, 'dve', cons_tok[b])
            ctx.inc('qs', q.dma_start(out=d['out'][i * 128:(i + 1) * 128, :], in_=x1[:]), 16)
            st_tok = ctx.cnt['qs']


def p_ssd_proj(ctx):
    nc = ctx.nc
    d = ctx.dram
    q, g, v, s, t = nc.sync, nc.gpsimd, nc.vector, nc.scalar, nc.tensor
    with Scope(ctx):
        wint = ctx.sbuf('sp_w', [128, NF * 1184], BF16)
        xch = [ctx.sbuf(f'sp_x{i}', [128, NF * 512], BF16) for i in range(2)]
        btok = ctx.sbuf('sp_btok', [128, 64], BF16)
        bcsb = ctx.sbuf('sp_bcsb', [128, 512], BF16)
        dtr = ctx.sbuf('sp_dtr', [32, S], F32)
        dtb = ctx.sbuf('sp_dtb', [32, 1], F32)
        alog = ctx.sbuf('sp_alog', [32, 2], F32)
        zrow = ctx.sbuf('sp_z', [32, S], F32)
        tmp = ctx.sbuf('sp_tmp', [32, S], F32)
        dt_ = ctx.sbuf('sp_dt', [32, S], F32)
        at_ = ctx.sbuf('sp_at', [32, S], F32)
        act_ = ctx.sbuf('sp_act', [32, S], F32)
        rt_ = ctx.sbuf('sp_rt', [32, S], F32)
        onec = ctx.sbuf('sp_one', [32, 1], F32)
        xo = ctx.sbuf('sp_xo', [128, DM], BF16)
        pxa = ctx.psum('sp_pa', [128, 512], F32)
        pxb = ctx.psum('sp_pb', [128, 512], F32)
        pxc = ctx.psum('sp_pc', [128, 512], F32)
        for k in range(NF):
            ctx.inc('qg', g.dma_start(out=wint[:, k * 1184:(k + 1) * 1184],
                                      in_=d['ssd_wint'][k * 128:(k + 1) * 128, :]), 16)
        ctx.inc('qs', q.dma_start(out=dtb[:], in_=d['ssd_dtb'][:, :]), 16)
        ctx.inc('qs', q.dma_start(out=alog[:, 0:1], in_=d['ssd_alog'][:, :]), 16)
        ctx.inc('qg', g.memset(onec[:], 1.0), 1)
        ctx.barrier([t])
        ld_tok = [0, 0]
        rel_tok = [0, 0]
        evtok = 0
        xstok = 0
        for i in range(NT):
            b = i % 2
            if i >= 2:
                ctx.wait(q, 'pe', rel_tok[b])
            for k in range(NF):
                ctx.inc('qs', q.dma_start(out=xch[b][:, k * 512:k * 512 + 128],
                                          in_=d['XN1T'][k * 128:(k + 1) * 128,
                                                        1 + i * 128:1 + (i + 1) * 128]), 16)
            ld_tok[b] = ctx.cnt['qs']
            ctx.wait(t, 'qs', ld_tok[b])
            ctx.wait(t, 'dve', evtok)
            for oh, bank in [(0, pxa), (1, pxb)]:
                for k in range(NF):
                    mm = t.matmul(bank[:, :], xch[b][:, k * 512:k * 512 + 128],
                                  wint[:, k * 1184 + oh * 512:k * 1184 + (oh + 1) * 512],
                                  start=(k == 0), stop=(k == NF - 1))
                ctx.inc('pe', mm)
            for k in range(NF):
                mm = t.matmul(pxc[:, 0:64], xch[b][:, k * 512:k * 512 + 128],
                              wint[:, k * 1184 + 1024:k * 1184 + 1088],
                              start=(k == 0), stop=(k == NF - 1))
            ctx.inc('pe', mm)
            rel_tok[b] = ctx.cnt['pe']
            ctx.wait(v, 'pe')
            ctx.wait(v, 'qs', xstok)
            ctx.inc('dve', v.tensor_copy(out=xo[:, 0:512], in_=pxa[:, :]))
            ctx.inc('dve', v.tensor_copy(out=xo[:, 512:1024], in_=pxb[:, :]))
            ctx.inc('dve', v.tensor_copy(out=btok[:, :], in_=pxc[:, 0:64]))
            evtok = ctx.cnt['dve']
            ctx.wait(q, 'dve', evtok)
            ctx.inc('qs', q.dma_start(out=d['XTOK'][i * 128:(i + 1) * 128, :], in_=xo[:]), 16)
            ctx.inc('qs', q.dma_start(out=d['BTOK'][i * 128:(i + 1) * 128, :], in_=btok[:]), 16)
            xstok = ctx.cnt['qs']
        bcrel = 0
        bcst = 0
        for c in range(8):
            b = c % 2
            ctx.wait(q, 'pe', rel_tok[b])
            for k in range(NF):
                ctx.inc('qs', q.dma_start(out=xch[b][:, k * 512:(k + 1) * 512],
                                          in_=d['XN1T'][k * 128:(k + 1) * 128,
                                                        1 + c * 512:1 + (c + 1) * 512]), 16)
            ld_tok[b] = ctx.cnt['qs']
            ctx.wait(t, 'qs', ld_tok[b])
            ctx.wait(t, 'dve', bcrel)
            for k in range(NF):
                mm = t.matmul(pxa[:, :], wint[:, k * 1184 + 1024:k * 1184 + 1152],
                              xch[b][:, k * 512:(k + 1) * 512],
                              start=(k == 0), stop=(k == NF - 1))
            ctx.inc('pe', mm)
            for k in range(NF):
                mm = t.matmul(pxb[0:32, :], wint[:, k * 1184 + 1152:k * 1184 + 1184],
                              xch[b][:, k * 512:(k + 1) * 512],
                              start=(k == 0), stop=(k == NF - 1))
            ctx.inc('pe', mm)
            rel_tok[b] = ctx.cnt['pe']
            ctx.wait(v, 'pe')
            ctx.wait(v, 'qs', bcst)
            ctx.inc('dve', v.tensor_copy(out=bcsb[:, :], in_=pxa[:, :]))
            ctx.inc('dve', v.tensor_copy(out=dtr[0:32, c * 512:(c + 1) * 512],
                                         in_=pxb[0:32, :]))
            bcrel = ctx.cnt['dve']
            ctx.wait(q, 'dve', bcrel)
            ctx.inc('qs', q.dma_start(out=d['BT'][:, c * 512:(c + 1) * 512],
                                      in_=bcsb[0:64, :]), 16)
            ctx.inc('qs', q.dma_start(out=d['CT'][:, c * 512:(c + 1) * 512],
                                      in_=bcsb[64:128, :]), 16)
            bcst = ctx.cnt['qs']
        v.drain()
        ctx.inc('dve', v.tensor_scalar(out=zrow[:], in0=dtr[:], scalar1=dtb[:, 0:1],
                                       scalar2=None, op0=OP.add))
        v.drain()
        ctx.wait(s, 'dve')
        ctx.inc('act', s.activation(out=tmp[:], in_=zrow[:], func=AF.Abs))
        s.drain()
        ctx.inc('act', s.activation(out=tmp[:], in_=tmp[:], func=AF.Exp, scale=-1.0))
        s.drain()
        ctx.inc('act', s.activation(out=tmp[:], in_=tmp[:], func=AF.Ln, bias=1.0))
        s.drain()
        ctx.wait(v, 'act')
        ctx.inc('dve', v.tensor_scalar(out=dt_[:], in0=zrow[:], scalar1=0.0,
                                       scalar2=None, op0=OP.max))
        v.drain()
        ctx.inc('dve', v.tensor_tensor(out=dt_[:], in0=dt_[:], in1=tmp[:], op=OP.add))
        v.drain()
        ctx.wait(s, 'dve')
        ctx.inc('act', s.activation(out=alog[:, 1:2], in_=alog[:, 0:1], func=AF.Exp))
        s.drain()
        ctx.wait(v, 'act')
        ctx.inc('dve', v.tensor_scalar(out=alog[:, 1:2], in0=alog[:, 1:2],
                                       scalar1=-1.0, scalar2=None, op0=OP.mult))
        v.drain()
        ctx.inc('dve', v.tensor_scalar(out=at_[:], in0=dt_[:], scalar1=alog[:, 1:2],
                                       scalar2=None, op0=OP.mult))
        v.drain()
        for c in range(SSD_NC):
            ctx.inc('dve', v.tensor_tensor_scan(
                out=act_[:, c * SSD_CS:(c + 1) * SSD_CS],
                data0=onec[:, 0:1].broadcast_to([32, SSD_CS]),
                data1=at_[:, c * SSD_CS:(c + 1) * SSD_CS],
                initial=0.0, op0=OP.mult, op1=OP.add))
        v.drain()
        ctx.wait(s, 'dve')
        ctx.inc('act', s.activation(out=tmp[:], in_=dt_[:], func=AF.Ln))
        s.drain()
        ctx.wait(v, 'act')
        ctx.inc('dve', v.tensor_tensor(out=rt_[:], in0=act_[:], in1=tmp[:], op=OP.subtract))
        v.drain()
        ctx.wait(q, 'dve')
        ctx.inc('qs', q.dma_start(out=d['ACT_s'][:, :], in_=act_[:]), 16)
        ctx.inc('qs', q.dma_start(out=d['RT_s'][:, :], in_=rt_[:]), 16)


def p_ssd_core(ctx):
    nc = ctx.nc
    d = ctx.dram
    q, g, v, s, t = nc.sync, nc.gpsimd, nc.vector, nc.scalar, nc.tensor
    CS = SSD_CS
    with Scope(ctx):
        acT = ctx.sbuf('sc_ac', [32, S], F32)
        rt_ = ctx.sbuf('sc_rt', [32, S], F32)
        bt = ctx.sbuf('sc_bt', [64, S], BF16)
        ct = ctx.sbuf('sc_ct', [64, S], BF16)
        btok = ctx.sbuf('sc_btok', [128, NT * 64], BF16)
        mask = ctx.sbuf('sc_mask', [128, 128], F32)
        sel = ctx.sbuf('sc_sel', [32, 32 * 128], F32)
        drow = ctx.sbuf('sc_drow', [1, 32], F32)
        onesf = ctx.sbuf('sc_onesf', [2, 128], F32)
        eye = ctx.sbuf('sc_eye', [128, 128], F32)
        xtk = [ctx.sbuf(f'sc_x{i}', [128, DM], BF16) for i in range(2)]
        stage = ctx.sbuf('sc_stage', [96, 128], F32)
        stok = ctx.sbuf('sc_stok', [128, 96], F32)
        cdrow = ctx.sbuf('sc_cdrow', [1, 64], F32)
        eac = ctx.sbuf('sc_eac', [128, 32], F32)
        cbt = ctx.sbuf('sc_cbt', [128, 128], BF16)
        xds = ctx.sbuf('sc_xds', [128, DM], BF16)
        expnt = [ctx.sbuf(f'sc_ex{i}', [128, 512], F32) for i in range(2)]
        ebt = [ctx.sbuf(f'sc_eb{i}', [128, 512], BF16) for i in range(2)]
        mt = [ctx.sbuf(f'sc_mt{i}', [128, 512], BF16) for i in range(2)]
        yo = ctx.sbuf('sc_yo', [128, DM], F32)
        y01 = ctx.sbuf('sc_y01', [128, DM], F32)
        xdd = ctx.sbuf('sc_xdd', [128, DM], F32)
        ytok = ctx.sbuf('sc_ytok', [128, DM], BF16)
        dbb = ctx.sbuf('sc_dbb', [128, 32], F32)
        prevf = ctx.sbuf('sc_prevf', [64, DM], F32)
        prevb = ctx.sbuf('sc_prevb', [64, DM], BF16)
        pE = ctx.psum('sc_pE', [128, 512], F32)
        pST = ctx.psum('sc_pST', [128, 512], F32)
        pAc = pE
        pYd = ctx.psum('sc_pYd', [128, 1024], F32)
        pYo = ctx.psum('sc_pYo', [128, 1024], F32)
        pst = ctx.psum('sc_pst', [64, 1024], F32)
        for nm_, sb_ in [('ACT_s', acT), ('RT_s', rt_), ('BT', bt), ('CT', ct),
                         ('ssd_mask', mask), ('sel32', sel), ('ssd_d', drow),
                         ('onesf', onesf), ('eye', eye)]:
            ctx.inc('qs', q.dma_start(out=sb_[:], in_=d[nm_][:, :]), 16)
        for i in range(NT):
            ctx.inc('qg', g.dma_start(out=btok[:, i * 64:(i + 1) * 64],
                                      in_=d['BTOK'][i * 128:(i + 1) * 128, :]), 16)
        ctx.inc('dve', v.memset(prevf[:], 0.0))
        ctx.inc('dve', v.memset(prevb[:], 0.0))
        ctx.barrier([t, v, s, g])
        mm = t.matmul(pE[:, 0:32], onesf[0:1, 0:128], drow[0:1, 0:32], start=True, stop=True)
        ctx.inc('pe', mm)
        ctx.wait(v, 'pe')
        ctx.inc('dve', v.tensor_copy(out=dbb[:], in_=pE[:, 0:32]))
        ld_tok = [0, 0]
        xrel = [0, 0]
        sttok = 0
        ysttok = 0
        cast_tok = ctx.cnt['dve']
        for c in range(SSD_NC):
            b = c % 2
            if c >= 2:
                ctx.wait(q, 'pe', xrel[b])
            ctx.inc('qs', q.dma_start(out=xtk[b][:], in_=d['XTOK'][c * CS:(c + 1) * CS, :]), 16)
            ld_tok[b] = ctx.cnt['qs']
            acl = acT[:, c * CS + CS - 1:c * CS + CS]
            ctx.wait(v, 'pe')
            ctx.inc('dve', v.tensor_copy(out=stage[0:32, :], in_=acT[:, c * CS:(c + 1) * CS]))
            ctx.inc('dve', v.tensor_scalar(out=stage[32:64, :], in0=rt_[:, c * CS:(c + 1) * CS],
                                           scalar1=-1.0, scalar2=None, op0=OP.mult))
            ctx.inc('dve', v.tensor_scalar(out=stage[64:96, :], in0=rt_[:, c * CS:(c + 1) * CS],
                                           scalar1=acl, scalar2=None, op0=OP.subtract))
            v.drain()
            ctx.wait(s, 'dve')
            ctx.inc('act', s.activation(out=stage[64:96, :], in_=stage[64:96, :],
                                        func=AF.Exp, scale=-1.0))
            s.drain()
            ctx.wait(t, 'act')
            ctx.wait(t, 'dve')
            mm = t.transpose(pST[:, 0:96], stage[0:96, :], eye[0:96, 0:96])
            ctx.inc('pe', mm)
            mm = t.transpose(pST[0:1, 96:128], acl, eye[0:32, 0:32])
            ctx.inc('pe', mm)
            ctx.wait(v, 'pe')
            ctx.wait(v, 'act')
            ctx.inc('dve', v.tensor_copy(out=stok[:], in_=pST[:, 0:96]))
            ctx.inc('dve', v.tensor_copy(out=cdrow[0:1, 0:32], in_=pST[0:1, 96:128]))
            v.drain()
            ctx.wait(s, 'dve')
            ctx.inc('act', s.activation(out=cdrow[0:1, 32:64], in_=cdrow[0:1, 0:32],
                                        func=AF.Exp))
            ctx.inc('act', s.activation(out=eac[:], in_=stok[:, 0:32], func=AF.Exp))
            s.drain()
            ctx.wait(t, 'dve')
            mm = t.matmul(pST[:, 288:416], bt[:, c * CS:(c + 1) * CS],
                          ct[:, c * CS:(c + 1) * CS], start=True, stop=True)
            ctx.inc('pe', mm)
            ctx.wait(v, 'pe')
            ctx.inc('dve', v.tensor_copy(out=cbt[:], in_=pST[:, 288:416]))
            cbtok = ctx.cnt['dve']
            ctx.wait(v, 'qs', ld_tok[b])
            ctx.inc('dve', v.tensor_tensor(
                out=xds[:].rearrange('p (h e) -> p h e', h=32),
                in0=xtk[b][:].rearrange('p (h e) -> p h e', h=32),
                in1=stok[:, 64:96].unsqueeze(2).broadcast_to([128, 32, 32]),
                op=OP.mult))
            v.drain()
            ctx.wait(t, 'dve', cast_tok)
            for oh in range(2):
                mm = t.matmul(pYo[:, oh * 512:(oh + 1) * 512], ct[:, c * CS:(c + 1) * CS],
                              prevb[:, oh * 512:(oh + 1) * 512], start=True, stop=True)
                ctx.inc('pe', mm)
            yoff_tok = ctx.cnt['pe']
            for oh in range(2):
                mm = t.matmul(pst[0:64, oh * 512:(oh + 1) * 512],
                              btok[:, c * 64:(c + 1) * 64],
                              xds[:, oh * 512:(oh + 1) * 512], start=True, stop=True)
                ctx.inc('pe', mm)
            pst_tok = ctx.cnt['pe']
            acfree = 0
            exread = [0, 0]
            mtrel = [0, 0]
            ydtok = [0, 0]
            for hg in range(SSD_NH // 4):
                par = hg % 2
                ctx.wait(t, 'dve', acfree)
                for q_ in range(4):
                    h = hg * 4 + q_
                    mm = t.matmul(pAc[:, q_ * 128:(q_ + 1) * 128],
                                  sel[:, h * 128:(h + 1) * 128],
                                  acT[:, c * CS:(c + 1) * CS], start=True, stop=True)
                ctx.inc('pe', mm)
                ctx.wait(v, 'pe')
                ctx.wait(v, 'act', exread[par])
                ctx.inc('dve', v.tensor_tensor(
                    out=expnt[par][:].rearrange('p (g i) -> p g i', g=4),
                    in0=pAc[:, 0:512].rearrange('p (g i) -> p g i', g=4),
                    in1=mask[:].unsqueeze(1).broadcast_to([128, 4, 128]), op=OP.min))
                acfree = ctx.cnt['dve']
                ctx.wait(s, 'dve', acfree)
                ctx.wait(s, 'pool', mtrel[par])
                for q_ in range(4):
                    h = hg * 4 + q_
                    ctx.inc('act', s.activation(out=ebt[par][:, q_ * 128:(q_ + 1) * 128],
                                                in_=expnt[par][:, q_ * 128:(q_ + 1) * 128],
                                                func=AF.Exp,
                                                bias=stok[:, 32 + h:33 + h], scale=1.0))
                exread[par] = ctx.cnt['act']
                ctx.wait(g, 'act')
                ctx.wait(g, 'pe', ydtok[par])
                ctx.inc('pool', g.tensor_tensor(
                    out=mt[par][:].rearrange('p (g i) -> p g i', g=4),
                    in0=ebt[par][:].rearrange('p (g i) -> p g i', g=4),
                    in1=cbt[:].unsqueeze(1).broadcast_to([128, 4, 128]), op=OP.mult))
                mtrel[par] = ctx.cnt['pool']
                ctx.wait(t, 'pool', mtrel[par])
                for q_ in range(4):
                    h = hg * 4 + q_
                    mm = t.matmul(pYd[:, h * 32:(h + 1) * 32],
                                  mt[par][:, q_ * 128:(q_ + 1) * 128],
                                  xtk[b][:, h * 32:(h + 1) * 32],
                                  start=(h % 16 == 0), stop=(h % 16 == 15))
                ctx.inc('pe', mm)
                ydtok[par] = ctx.cnt['pe']
            xrel[b] = ctx.cnt['pe']
            ctx.wait(v, 'pe')
            ctx.wait(v, 'act')
            ctx.inc('dve', v.tensor_tensor(
                out=yo[:].rearrange('p (h e) -> p h e', h=32),
                in0=pYo[:, :].rearrange('p (h e) -> p h e', h=32),
                in1=eac[:].unsqueeze(2).broadcast_to([128, 32, 32]), op=OP.mult))
            v.drain()
            ctx.inc('dve', v.tensor_tensor(out=y01[:], in0=pYd[:, :], in1=yo[:], op=OP.add))
            ctx.inc('dve', v.tensor_tensor(
                out=xdd[:].rearrange('p (h e) -> p h e', h=32),
                in0=xtk[b][:].rearrange('p (h e) -> p h e', h=32),
                in1=dbb[:].unsqueeze(2).broadcast_to([128, 32, 32]), op=OP.mult))
            v.drain()
            ctx.wait(v, 'qs', ysttok)
            ctx.inc('dve', v.tensor_tensor(out=ytok[:], in0=y01[:], in1=xdd[:], op=OP.add))
            ctx.wait(q, 'dve')
            ctx.inc('qs', q.dma_start(out=d['YTOK'][c * CS:(c + 1) * CS, :], in_=ytok[:]), 16)
            ysttok = ctx.cnt['qs']
            ctx.wait(t, 'act')
            mm = t.matmul(pST[0:64, 128:160], onesf[0:1, 0:64], cdrow[0:1, 32:64],
                          start=True, stop=True)
            ctx.inc('pe', mm)
            ctx.wait(v, 'pe')
            ctx.inc('dve', v.tensor_tensor(
                out=prevf[:].rearrange('p (h e) -> p h e', h=32),
                in0=prevf[:].rearrange('p (h e) -> p h e', h=32),
                in1=pST[0:64, 128:160].unsqueeze(2).broadcast_to([64, 32, 32]),
                op=OP.mult))
            v.drain()
            ctx.inc('dve', v.tensor_tensor(out=prevf[:], in0=prevf[:], in1=pst[0:64, :],
                                           op=OP.add))
            v.drain()
            ctx.inc('dve', v.tensor_copy(out=prevb[:], in_=prevf[:]))
            cast_tok = ctx.cnt['dve']


def p_ssd_out(ctx):
    nc = ctx.nc
    d = ctx.dram
    q, g, v, s, t = nc.sync, nc.gpsimd, nc.vector, nc.scalar, nc.tensor
    with Scope(ctx):
        wout = ctx.sbuf('so_w', [128, NF * DM], BF16)
        yt = [ctx.sbuf(f'so_y{i}', [128, DM], BF16) for i in range(2)]
        ytT = ctx.sbuf('so_ytT', [128, NF * 128], BF16)
        eyebf = ctx.sbuf('so_eye', [128, 128], BF16)
        s1 = ctx.sbuf('so_s1', [128, DM], F32)
        ptA = ctx.psum('so_ptA', [128, 1024], BF16)
        pm1 = ctx.psum('so_pm1', [128, 512], F32)
        pm2 = ctx.psum('so_pm2', [128, 512], F32)
        cast_loads(ctx, wout, 'ssd_wout', NF, DM)
        ctx.inc('qs', q.dma_start(out=eyebf[:], in_=d['eyebf'][:, :]), 16)
        ctx.barrier([t])
        ld_tok = [0, 0]
        rel_tok = [0, 0]
        sttok = 0
        for i in range(NT):
            b = i % 2
            if i >= 2:
                ctx.wait(q, 'pe', rel_tok[b])
            ctx.inc('qs', q.dma_start(out=yt[b][:], in_=d['YTOK'][i * 128:(i + 1) * 128, :]), 16)
            ld_tok[b] = ctx.cnt['qs']
            ctx.wait(t, 'qs', ld_tok[b])
            ctx.wait(t, 'dve')
            for k in range(NF):
                mm = t.transpose(ptA[:, k * 128:(k + 1) * 128],
                                 yt[b][:, k * 128:(k + 1) * 128], eyebf[:, 0:128])
            ctx.inc('pe', mm)
            rel_tok[b] = ctx.cnt['pe']
            ctx.wait(v, 'pe')
            ctx.inc('dve', v.tensor_copy(out=ytT[:], in_=ptA[:, 0:1024]))
            tok = ctx.cnt['dve']
            ctx.wait(t, 'dve', tok)
            for oh, bank in [(0, pm1), (1, pm2)]:
                for k in range(NF):
                    mm = t.matmul(bank[:, :], ytT[:, k * 128:(k + 1) * 128],
                                  wout[:, k * DM + oh * 512:k * DM + (oh + 1) * 512],
                                  start=(k == 0), stop=(k == NF - 1))
                ctx.inc('pe', mm)
            ctx.wait(v, 'pe')
            ctx.wait(v, 'qs', sttok)
            ctx.inc('dve', v.tensor_copy(out=s1[:, 0:512], in_=pm1[:, :]))
            ctx.inc('dve', v.tensor_copy(out=s1[:, 512:1024], in_=pm2[:, :]))
            ctx.wait(q, 'dve')
            ctx.inc('qs', q.dma_start(out=d['S1'][i * 128:(i + 1) * 128, :], in_=s1[:]), 16)
            sttok = ctx.cnt['qs']


def p_moe_rt(ctx):
    nc = ctx.nc
    d = ctx.dram
    q, g, v, s, t = nc.sync, nc.gpsimd, nc.vector, nc.scalar, nc.tensor
    with Scope(ctx):
        rt = ctx.sbuf('mr_rt', [128, NF * 8], BF16)
        rtf = ctx.sbuf('mr_rtf', [128, NF * 8], F32)
        rtlo = ctx.sbuf('mr_rtlo', [128, NF * 8], BF16)
        xch = [ctx.sbuf(f'mr_x{i}', [128, 2 * NF * 128], BF16) for i in range(2)]
        eye = ctx.sbuf('mr_eye', [128, 128], F32)
        w_ = ctx.sbuf('mr_w', [128, 24], F32)
        oh1 = ctx.sbuf('mr_oh1', [128, 8], F32)
        oh2 = ctx.sbuf('mr_oh2', [128, 8], F32)
        t1 = ctx.sbuf('mr_t1', [128, 8], F32)
        s2 = ctx.sbuf('mr_s2', [128, 8], F32)
        e1 = ctx.sbuf('mr_e1', [128, 8], F32)
        apl = ctx.sbuf('mr_apl', [128, 8], F32)
        dw = ctx.sbuf('mr_dw', [128, 8], F32)
        dwT = ctx.sbuf('mr_dwT', [8, S], F32)
        pl = ctx.psum('mr_pl', [128, 512], F32)
        ptr = ctx.psum('mr_ptr', [128, 512], F32)
        cast_loads(ctx, rt, 'moe_rt', NF, 8)
        for k in range(NF):
            ctx.inc('qs', q.dma_start(out=rtf[:, k * 8:(k + 1) * 8],
                                      in_=d['moe_rt'][k * 128:(k + 1) * 128, :]), 16)
        ctx.inc('qs', q.dma_start(out=eye[:], in_=d['eye'][:, :]), 16)
        ctx.wait(v, 'qs')
        ctx.wait(v, 'qg')
        ctx.inc('dve', v.tensor_tensor(out=rtlo[:], in0=rtf[:], in1=rt[:], op=OP.subtract))
        ctx.barrier([t])
        ld_tok = [0, 0]
        rel_tok = [0, 0]
        for i in range(NT):
            b = i % 2
            if i >= 2:
                ctx.wait(q, 'pe', rel_tok[b])
            for k in range(NF):
                ctx.inc('qs', q.dma_start(out=xch[b][:, k * 128:(k + 1) * 128],
                                          in_=d['XN1T'][k * 128:(k + 1) * 128,
                                                        1 + i * 128:1 + (i + 1) * 128]), 16)
            for k in range(NF):
                ctx.inc('qs', q.dma_start(out=xch[b][:, (NF + k) * 128:(NF + k + 1) * 128],
                                          in_=d['XN1TLO'][k * 128:(k + 1) * 128,
                                                          1 + i * 128:1 + (i + 1) * 128]), 16)
            ld_tok[b] = ctx.cnt['qs']
            ctx.wait(t, 'qs', ld_tok[b])
            ctx.wait(t, 'dve')
            ctx.wait(t, 'act')
            for k in range(NF):
                t.matmul(pl[:, 0:8], xch[b][:, k * 128:(k + 1) * 128],
                         rt[:, k * 8:(k + 1) * 8], start=(k == 0), stop=False)
            for k in range(NF):
                t.matmul(pl[:, 0:8], xch[b][:, k * 128:(k + 1) * 128],
                         rtlo[:, k * 8:(k + 1) * 8], start=False, stop=False)
            for k in range(NF):
                mm = t.matmul(pl[:, 0:8], xch[b][:, (NF + k) * 128:(NF + k + 1) * 128],
                              rt[:, k * 8:(k + 1) * 8], start=False, stop=(k == NF - 1))
            ctx.inc('pe', mm)
            rel_tok[b] = ctx.cnt['pe']
            ctx.wait(v, 'pe')
            ctx.inc('dve', v.reduce_max(out=w_[:, 0:1], in_=pl[:, 0:8], axis=AX.X))
            v.drain()
            ctx.inc('dve', v.tensor_scalar(out=t1[:], in0=pl[:, 0:8], scalar1=w_[:, 0:1],
                                           scalar2=None, op0=OP.subtract))
            ctx.inc('dve', v.tensor_scalar(out=oh1[:], in0=pl[:, 0:8], scalar1=w_[:, 0:1],
                                           scalar2=None, op0=OP.is_equal))
            ctx.wait(s, 'pe')
            ctx.wait(s, 'dve')
            ctx.inc('act', s.activation(out=apl[:], in_=pl[:, 0:8], func=AF.Abs))
            s.drain()
            ctx.wait(v, 'act')
            ctx.inc('dve', v.tensor_scalar(out=w_[:, 8:16], in0=apl[:], scalar1=w_[:, 0:1],
                                           scalar2=-0.02, op0=OP.max, op1=OP.mult))
            v.drain()
            ctx.inc('dve', v.tensor_tensor(out=w_[:, 8:16], in0=t1[:], in1=w_[:, 8:16],
                                           op=OP.is_ge))
            v.drain()
            ctx.wait(s, 'dve')
            ctx.inc('act', s.activation(out=e1[:], in_=t1[:], func=AF.Exp))
            s.drain()
            ctx.wait(v, 'act')
            ctx.inc('dve', v.tensor_tensor(out=e1[:], in0=e1[:], in1=w_[:, 8:16], op=OP.mult))
            v.drain()
            ctx.inc('dve', v.reduce_sum(out=w_[:, 2:3], in_=e1[:], axis=AX.X))
            v.drain()
            ctx.inc('dve', v.reciprocal(out=w_[:, 2:3], in_=w_[:, 2:3]))
            ctx.inc('dve', v.scalar_tensor_tensor(out=s2[:], in0=oh1[:], scalar=-BIG,
                                                  in1=pl[:, 0:8], op0=OP.mult, op1=OP.add))
            v.drain()
            ctx.inc('dve', v.reduce_max(out=w_[:, 3:4], in_=s2[:], axis=AX.X))
            v.drain()
            ctx.inc('dve', v.tensor_scalar(out=oh2[:], in0=s2[:], scalar1=w_[:, 3:4],
                                           scalar2=None, op0=OP.is_equal))
            ctx.inc('dve', v.tensor_scalar(out=s2[:], in0=s2[:], scalar1=w_[:, 3:4],
                                           scalar2=None, op0=OP.subtract))
            ctx.inc('dve', v.tensor_scalar(out=w_[:, 16:24], in0=apl[:], scalar1=w_[:, 3:4],
                                           scalar2=-0.02, op0=OP.max, op1=OP.mult))
            ctx.inc('dve', v.tensor_scalar(out=t1[:], in0=pl[:, 0:8], scalar1=w_[:, 3:4],
                                           scalar2=None, op0=OP.subtract))
            v.drain()
            ctx.inc('dve', v.tensor_tensor(out=w_[:, 16:24], in0=t1[:], in1=w_[:, 16:24],
                                           op=OP.is_ge))
            v.drain()
            ctx.wait(s, 'dve')
            ctx.inc('act', s.activation(out=s2[:], in_=s2[:], func=AF.Exp))
            s.drain()
            ctx.wait(v, 'act')
            ctx.inc('dve', v.tensor_tensor(out=s2[:], in0=s2[:], in1=w_[:, 16:24], op=OP.mult))
            v.drain()
            ctx.inc('dve', v.reduce_sum(out=w_[:, 5:6], in_=s2[:], axis=AX.X))
            v.drain()
            ctx.inc('dve', v.reciprocal(out=w_[:, 5:6], in_=w_[:, 5:6]))
            v.drain()
            ctx.inc('dve', v.tensor_scalar(out=dw[:], in0=oh1[:], scalar1=w_[:, 2:3],
                                           scalar2=None, op0=OP.mult))
            ctx.inc('dve', v.tensor_scalar(out=oh2[:], in0=oh2[:], scalar1=w_[:, 5:6],
                                           scalar2=None, op0=OP.mult))
            v.drain()
            ctx.inc('dve', v.tensor_tensor(out=dw[:], in0=dw[:], in1=oh2[:], op=OP.add))
            dwtok = ctx.cnt['dve']
            ctx.wait(t, 'dve', dwtok)
            mm = t.transpose(ptr[0:8, 0:128], dw[:, 0:8], eye[0:128, 0:128])
            ctx.inc('pe', mm)
            ctx.wait(v, 'pe')
            ctx.inc('dve', v.tensor_copy(out=dwT[0:8, i * 128:(i + 1) * 128],
                                         in_=ptr[0:8, 0:128]))
        ctx.wait(q, 'dve')
        ctx.inc('qs', q.dma_start(out=d['DWT'][:, :], in_=dwT[:]), 16)


def p_moe_exp(ctx):
    nc = ctx.nc
    d = ctx.dram
    q, g, v, s, t = nc.sync, nc.gpsimd, nc.vector, nc.scalar, nc.tensor
    NM = MOE_F // 128
    with Scope(ctx):
        wg_ = ctx.sbuf('me_wg', [128, NF * MOE_F], BF16)
        wu_ = ctx.sbuf('me_wu', [128, NF * MOE_F], BF16)
        wd_ = ctx.sbuf('me_wd', [128, NM * DM], BF16)
        wg = [wg_, wg_]
        wu = [wu_, wu_]
        wd = [wd_, wd_]
        xch = [ctx.sbuf(f'me_x{i}', [128, NF * 512], BF16) for i in range(2)]
        xdw = ctx.sbuf('me_xdw', [128, NF * 512], BF16)
        dwT = ctx.sbuf('me_dwT', [8, S], F32)
        sel = ctx.sbuf('me_sel', [8, 8 * 128], F32)
        dwb = ctx.sbuf('me_dwb', [128, 512], BF16)
        sg = ctx.sbuf('me_sg', [128, 1024], BF16)
        gu = ctx.sbuf('me_gu', [128, NM * 512], BF16)
        mo = ctx.sbuf('me_mo', [128, DM], F32)
        pga = ctx.psum('me_pga', [128, 512], F32)
        pgb = ctx.psum('me_pgb', [128, 512], F32)
        pua = ctx.psum('me_pua', [128, 512], F32)
        pub = ctx.psum('me_pub', [128, 512], F32)
        pdw = ctx.psum('me_pdw', [128, 512], F32)
        pd1 = ctx.psum('me_pd1', [128, 512], F32)
        pd2 = ctx.psum('me_pd2', [128, 512], F32)
        pd3 = ctx.psum('me_pd3', [128, 512], F32)
        ctx.inc('qs', q.dma_start(out=dwT[:], in_=d['DWT'][:, :]), 16)
        ctx.inc('qs', q.dma_start(out=sel[:], in_=d['sel8'][:, :]), 16)
        wld = [0, 0]
        wrel = [0, 0]
        ld_tok = [0, 0]
        xrel = [0, 0]
        dwbtok = 0
        storetok = 0
        for e in range(MOE_E):
            eb = e % 2
            if e >= 1:
                ctx.wait(g, 'pe', wrel[(e - 1) % 2])
            for k in range(NF):
                ctx.inc('qg', g.dma_start(out=wg[eb][:, k * MOE_F:(k + 1) * MOE_F],
                                          in_=d['moe_wg'][e, k * 128:(k + 1) * 128, :]), 16)
                ctx.inc('qg', g.dma_start(out=wu[eb][:, k * MOE_F:(k + 1) * MOE_F],
                                          in_=d['moe_wu'][e, k * 128:(k + 1) * 128, :]), 16)
            for j in range(NM):
                ctx.inc('qg', g.dma_start(out=wd[eb][:, j * DM:(j + 1) * DM],
                                          in_=d['moe_wd'][e, j * 128:(j + 1) * 128, :]), 16)
            wld[eb] = ctx.cnt['qg']
            for c in range(8):
                cb = c % 2
                if e > 0 or c >= 2:
                    ctx.wait(q, 'pe', xrel[cb])
                for k in range(NF):
                    ctx.inc('qs', q.dma_start(out=xch[cb][:, k * 512:(k + 1) * 512],
                                              in_=d['XN1T'][k * 128:(k + 1) * 128,
                                                            1 + c * 512:1 + (c + 1) * 512]), 16)
                ld_tok[cb] = ctx.cnt['qs']
                ctx.wait(t, 'qs', ld_tok[cb])
                ctx.wait(t, 'dve')
                mm = t.matmul(pdw[:, :], sel[:, e * 128:(e + 1) * 128],
                              dwT[:, c * 512:(c + 1) * 512], start=True, stop=True)
                ctx.inc('pe', mm)
                ctx.wait(v, 'pe')
                ctx.inc('dve', v.tensor_copy(out=dwb[:], in_=pdw[:, :]))
                v.drain()
                for k in range(NF):
                    ctx.inc('dve', v.tensor_tensor(out=xdw[:, k * 512:(k + 1) * 512],
                                                   in0=xch[cb][:, k * 512:(k + 1) * 512],
                                                   in1=dwb[:], op=OP.mult))
                dwbtok = ctx.cnt['dve']
                if c == 0:
                    ctx.wait(t, 'qg', wld[eb])
                ctx.wait(t, 'dve', dwbtok)
                silu_tok = [0, 0]
                gu_tok = [0, 0]
                for m in range(NM):
                    bank_g = [pga, pgb][m % 2]
                    bank_u = [pua, pub][m % 2]
                    ctx.wait(t, 'act', silu_tok[m % 2])
                    for k in range(NF):
                        mm = t.matmul(bank_g[:, :], wg[eb][:, k * MOE_F + m * 128:k * MOE_F + (m + 1) * 128],
                                      xch[cb][:, k * 512:(k + 1) * 512],
                                      start=(k == 0), stop=(k == NF - 1))
                    ctx.inc('pe', mm)
                    ctx.wait(t, 'dve', gu_tok[m % 2])
                    for k in range(NF):
                        mm = t.matmul(bank_u[:, :], wu[eb][:, k * MOE_F + m * 128:k * MOE_F + (m + 1) * 128],
                                      xdw[:, k * 512:(k + 1) * 512],
                                      start=(k == 0), stop=(k == NF - 1))
                    ctx.inc('pe', mm)
                    ctx.wait(s, 'pe')
                    ctx.wait(s, 'dve', gu_tok[m % 2])
                    ctx.inc('act', s.activation(out=sg[:, (m % 2) * 512:(m % 2 + 1) * 512],
                                                in_=bank_g[:, :], func=AF.Silu))
                    silu_tok[m % 2] = ctx.cnt['act']
                    ctx.wait(v, 'act')
                    ctx.inc('dve', v.tensor_tensor(out=gu[:, m * 512:(m + 1) * 512],
                                                   in0=bank_u[:, :],
                                                   in1=sg[:, (m % 2) * 512:(m % 2 + 1) * 512],
                                                   op=OP.mult))
                    gu_tok[m % 2] = ctx.cnt['dve']
                xrel[cb] = ctx.cnt['pe']
                gutok = ctx.cnt['dve']
                ctx.wait(t, 'dve', gutok)
                evactok = [0, 0]
                for tsub in range(4):
                    pdx = [(pd1, pd2), (pd3, pdw)][tsub % 2]
                    ctx.wait(t, 'dve', evactok[tsub % 2])
                    for oh, bank in [(0, pdx[0]), (1, pdx[1])]:
                        for j in range(NM):
                            mm = t.matmul(bank[:, :],
                                          gu[:, j * 512 + tsub * 128:j * 512 + (tsub + 1) * 128],
                                          wd[eb][:, j * DM + oh * 512:j * DM + (oh + 1) * 512],
                                          start=(j == 0), stop=(j == NM - 1))
                        ctx.inc('pe', mm)
                    ctx.wait(v, 'pe')
                    ctx.wait(v, 'qg', storetok)
                    ctx.inc('dve', v.tensor_copy(out=mo[:, 0:512], in_=pdx[0][:, :]))
                    ctx.inc('dve', v.tensor_copy(out=mo[:, 512:1024], in_=pdx[1][:, :]))
                    evactok[tsub % 2] = ctx.cnt['dve']
                    ctx.wait(g, 'dve')
                    row = c * 512 + tsub * 128
                    if e == 0:
                        ctx.inc('qg', g.dma_start(out=d['MOEACC'][row:row + 128, :],
                                                  in_=mo[:]), 16)
                    else:
                        ctx.inc('qg', g.dma_start(out=d['MOEACC'][row:row + 128, :],
                                                  in_=mo[:], accum_op=OP.add), 16)
                    storetok = ctx.cnt['qg']
            wrel[eb] = ctx.cnt['pe']


def _phase_list():
    return [
        ('ln0', ['x', 'eyebf'], ['XNTOK', 'XNT', 'XNF32'], lambda c: p_ln0(c)),
        ('l0aA', ['XNT', 'l0a_Wk', 'l0a_Wv', 'l0a_bk', 'l0a_bv', 'onesbf'],
         ['KVT', 'KDIAG'], lambda c: p_l0a_A(c)),
        ('l0aB', ['XNT', 'XNTOK', 'KVT', 'KDIAG', 'l0a_Wq', 'l0a_Wd', 'l0a_bq_c',
                  'l0a_bd', 'l0a_gamma', 'l0a_beta', 'eye', 'onesbf', 'onesf', 'pairsel'],
         ['S0'], lambda c: p_l0a_B(c)),
        ('fqkv0', ['XNT', 'ff_wqkv', 'ff_qs', 'ff_ks', 'onesf', 'eyebf',
                   'rot_qc', 'rot_qs', 'rot_kc', 'rot_ks'],
         ['QHT', 'KHT', 'VH'], lambda c: p_ffn_qkv(c, 0, 'XNT')),
        ('fatt0', ['QHT', 'KHT', 'VH', 'mask_gen', 'mask_w0', 'eyebf'],
         ['OT'], lambda c: p_ffn_att(c, 0)),
        ('fwo0', ['OT', 'XNF32', 'ff_wo'], ['XIA'], lambda c: p_ffn_wo(c, 0, 'XNF32', True)),
        ('ftr0a', ['XIA', 'eyebf'], ['XITA'], lambda c: p_tr(c, 'XIA', 'XITA')),
        ('fffn0', ['XITA', 'XIA', 'ff_w1', 'ff_b1', 'ff_w2', 'ff_b2', 'ff_lng', 'ff_lnb'],
         ['XIB'], lambda c: p_ffn_ffn(c, 0, 'XIB')),
        ('ftr0b', ['XIB', 'eyebf'], ['XITB'], lambda c: p_tr(c, 'XIB', 'XITB')),
        ('fqkv1', ['XITB', 'ff_wqkv', 'ff_qs', 'ff_ks', 'onesf', 'eyebf',
                   'rot_qc', 'rot_qs', 'rot_kc', 'rot_ks'],
         ['QHT', 'KHT', 'VH'], lambda c: p_ffn_qkv(c, 1, 'XITB')),
        ('fatt1', ['QHT', 'KHT', 'VH', 'mask_gen', 'mask_w0', 'eyebf'],
         ['OT'], lambda c: p_ffn_att(c, 1)),
        ('fwo1', ['OT', 'XIB', 'ff_wo'], ['XIA'], lambda c: p_ffn_wo(c, 1, 'XIB', True)),
        ('ftr1a', ['XIA', 'eyebf'], ['XITA'], lambda c: p_tr(c, 'XIA', 'XITA')),
        ('fffn1', ['XITA', 'XIA', 'ff_w1', 'ff_b1', 'ff_w2', 'ff_b2', 'ff_lng', 'ff_lnb'],
         ['FF'], lambda c: p_ffn_ffn(c, 1, 'FF')),
        ('comb0', ['x', 'S0', 'FF', 'eyebf'],
         ['X1', 'XN1TOK', 'XN1T', 'XN1TLO'], lambda c: p_comb0(c)),
        ('sproj', ['XN1T', 'ssd_wint', 'ssd_dtb', 'ssd_alog'],
         ['XTOK', 'BTOK', 'BT', 'CT', 'ACT_s', 'RT_s'], lambda c: p_ssd_proj(c)),
        ('score', ['XTOK', 'BTOK', 'BT', 'CT', 'ACT_s', 'RT_s', 'ssd_mask', 'sel32',
                   'ssd_d', 'onesf', 'eye'], ['YTOK'], lambda c: p_ssd_core(c)),
        ('sout', ['YTOK', 'ssd_wout', 'eyebf'], ['S1'], lambda c: p_ssd_out(c)),
        ('mrt', ['XN1T', 'XN1TLO', 'moe_rt', 'eye'], ['DWT'], lambda c: p_moe_rt(c)),
        ('mexp', ['XN1T', 'DWT', 'sel8', 'moe_wg', 'moe_wu', 'moe_wd'],
         ['MOEACC'], lambda c: p_moe_exp(c)),
        ('final', ['X1', 'S1', 'MOEACC'], ['out'], lambda c: p_final(c)),
    ]


FUSED = True


def build_fused():
    ctx = Ctx(fused=True)
    nc = ctx.nc
    engines = [nc.tensor, nc.vector, nc.scalar, nc.gpsimd, nc.sync]
    for name, ins, outs, emit in _phase_list():
        emit(ctx)
        ctx.barrier(engines)
    finish(ctx)
    ctx.close()
    return ctx


def kernel(**inputs):
    consts = host_prep(inputs)
    x_all = np.ascontiguousarray(np.asarray(inputs['L0_attn_input_tensor'], np.float32))
    if FUSED:
        ctx = build_fused()
        in_maps = []
        for b in range(BATCH):
            m = dict(consts)
            m['x'] = x_all[b]
            in_maps.append(m)
        res = run_bass_kernel_spmd(ctx.nc, in_maps, core_ids=list(range(BATCH)))
        return np.stack([res.results[b]['out'] for b in range(BATCH)], axis=0)
    percore = []
    for b in range(BATCH):
        vals = dict(consts)
        vals['x'] = x_all[b]
        percore.append(vals)
    for name, ins, outs, emit in _phase_list():
        ctx = Ctx(fused=False, phase_inputs=ins, phase_outputs=outs)
        emit(ctx)
        finish(ctx)
        ctx.close()
        in_maps = [{n: percore[b][n] for n in ins} for b in range(BATCH)]
        res = run_bass_kernel_spmd(ctx.nc, in_maps, core_ids=list(range(BATCH)))
        for b in range(BATCH):
            for n in outs:
                percore[b][n] = res.results[b][n]
    return np.stack([percore[b]['out'] for b in range(BATCH)], axis=0)


# version 08
# speedup vs baseline: 1.0328x; 1.0328x over previous
import math
import numpy as np
import ml_dtypes
import concourse.bass as bass
import concourse.mybir as mybir
from concourse.bass_utils import run_bass_kernel_spmd
from concourse.alu_op_type import AluOpType as OP

F32 = mybir.dt.float32
BF16 = mybir.dt.bfloat16
AF = mybir.ActivationFunctionType
AX = mybir.AxisListType

S = 4096
DM = 1024
NT = S // 128
NF = DM // 128
SP = S + 1
BIG = 1.0e38

L0A_NH, L0A_HD = 16, 64
FF_HEADS, FF_HD, FF_WIN, FF_INNER = 8, 64, 512, 512
FF_HID = 2730
FF_HPAD = 2816
SSD_DS, SSD_HD, SSD_NH = 64, 32, 32
SSD_CS = 128
SSD_NC = S // SSD_CS
MOE_E, MOE_F = 8, 2048


def _rotary_tables():
    n, dim_head, window = S, FF_HD, FF_WIN
    half_idx = np.arange(0, dim_head, 2, dtype=np.float64)
    inv_freq = 1.0 / 10000.0 ** (half_idx / dim_head)
    t = np.arange(n, dtype=np.float64)
    freqs = t[:, None] * inv_freq[None, :]
    freqs = np.concatenate([freqs, freqs], axis=-1)
    base = (half_idx + 0.4 * dim_head) / (1.4 * dim_head)
    power = (t - n // 2) / float(window // 2)
    scale = base[None, :] ** power[:, None]
    scale = np.concatenate([scale, scale], axis=-1)
    cos, sin = np.cos(freqs), np.sin(freqs)
    qc = (cos * scale).astype(np.float32)
    qs = (sin * scale).astype(np.float32)
    kc = (cos / scale).astype(np.float32)
    ks = (sin / scale).astype(np.float32)
    def sgn(tab):
        out = tab.copy()
        out[:, :dim_head // 2] *= -1.0
        return out
    return qc, sgn(qs), kc, sgn(ks)


def _attn_masks():
    ii = np.arange(FF_WIN)[:, None]
    jj = np.arange(2 * FF_WIN)[None, :]
    valid = (jj >= ii) & (jj <= ii + FF_WIN)
    m_gen = np.where(valid, 0.0, -BIG).astype(np.float32)
    valid0 = valid & (jj >= FF_WIN)
    m_w0 = np.where(valid0, 0.0, -BIG).astype(np.float32)
    return m_gen, m_w0


def _ssd_mask():
    i = np.arange(SSD_CS)
    m = np.where(i[None, :] >= i[:, None], BIG, -BIG).astype(np.float32)
    return m


def _selectors():
    ps = np.zeros((16, 8, 128), np.float32)
    for j in range(8):
        ps[2 * j, j, 0:64] = 1.0
        ps[2 * j + 1, j, 64:128] = 1.0
    s32 = np.zeros((32, 32, 128), np.float32)
    for h in range(32):
        s32[h, h, :] = 1.0
    s8 = np.zeros((8, 8, 128), np.float32)
    for e in range(8):
        s8[e, e, :] = 1.0
    return (ps.reshape(16, 8 * 128), s32.reshape(32, 32 * 128),
            s8.reshape(8, 8 * 128))


def host_prep(I):
    d = {}
    f32 = lambda a: np.ascontiguousarray(a, dtype=np.float32)
    d['eye'] = np.eye(128, dtype=np.float32)
    d['eyebf'] = np.eye(128, dtype=np.float32).astype(ml_dtypes.bfloat16)
    d['onesbf'] = np.ones((128, 128), dtype=ml_dtypes.bfloat16)
    d['onesf'] = np.ones((2, 128), dtype=np.float32)
    qc, qs, kc, ks = _rotary_tables()
    d['rot_qc'], d['rot_qs'], d['rot_kc'], d['rot_ks'] = qc, qs, kc, ks
    mg, m0 = _attn_masks()
    d['mask_gen'] = mg.astype(ml_dtypes.bfloat16)
    d['mask_w0'] = m0.astype(ml_dtypes.bfloat16)
    d['ssd_mask'] = _ssd_mask()
    ps, s32, s8 = _selectors()
    d['pairsel'], d['sel32'], d['sel8'] = ps, s32, s8

    for w in ['Wq', 'Wk', 'Wv', 'Wd']:
        d['l0a_' + w] = f32(I['L0_attn_' + w].T)
    for b in ['bq', 'bk', 'bv', 'bd', 'gamma', 'beta']:
        d['l0a_' + b] = f32(I['L0_attn_' + b]).reshape(1, DM)
    d['l0a_bq_c'] = f32(np.asarray(I['L0_attn_bq'], np.float32).reshape(8, 128).T)
    d['ff_wqkv'] = f32(I['L0_ffn_wqkv'])
    d['ff_wo'] = f32(I['L0_ffn_wo'])
    w1 = np.asarray(I['L0_ffn_w1'], np.float32)
    b1 = np.asarray(I['L0_ffn_b1'], np.float32)
    w2 = np.asarray(I['L0_ffn_w2'], np.float32)
    w1p = np.zeros((2, DM, 2 * FF_HPAD), np.float32)
    w1p[:, :, :FF_HID] = w1[:, :, :FF_HID]
    w1p[:, :, FF_HPAD:FF_HPAD + FF_HID] = w1[:, :, FF_HID:]
    b1p = np.zeros((2, 2 * FF_HPAD), np.float32)
    b1p[:, :FF_HID] = b1[:, :FF_HID]
    b1p[:, FF_HPAD:FF_HPAD + FF_HID] = b1[:, FF_HID:]
    w2p = np.zeros((2, FF_HPAD, DM), np.float32)
    w2p[:, :FF_HID, :] = w2
    d['ff_w1'] = f32(w1p)
    d['ff_b1'] = f32(b1p.reshape(2, 1, 2 * FF_HPAD))
    d['ff_w2'] = f32(w2p)
    d['ff_b2'] = f32(np.asarray(I['L0_ffn_b2'], np.float32).reshape(2, 1, DM))
    d['ff_qs'] = f32(I['L0_ffn_q_scale']).reshape(2, 1, 64)
    d['ff_ks'] = f32(I['L0_ffn_k_scale']).reshape(2, 1, 64)
    d['ff_lng'] = f32(I['L0_ffn_ln_g']).reshape(2, 1, DM)
    d['ff_lnb'] = f32(I['L0_ffn_ln_b']).reshape(2, 1, DM)
    d['ssd_wint'] = f32(np.asarray(I['L1_attn_W_in'], np.float32).T)
    d['ssd_wout'] = f32(np.asarray(I['L1_attn_W_out'], np.float32).T)
    d['ssd_dtb'] = f32(I['L1_attn_dt_bias']).reshape(32, 1)
    d['ssd_alog'] = f32(I['L1_attn_A_log']).reshape(32, 1)
    d['ssd_d'] = f32(I['L1_attn_D']).reshape(1, 32)
    d['moe_rt'] = f32(np.asarray(I['L1_ffn_router_w'], np.float32).T)
    d['moe_wg'] = f32(np.transpose(np.asarray(I['L1_ffn_w_gate'], np.float32), (0, 2, 1)))
    d['moe_wu'] = f32(np.transpose(np.asarray(I['L1_ffn_w_up'], np.float32), (0, 2, 1)))
    d['moe_wd'] = f32(np.transpose(np.asarray(I['L1_ffn_w_down'], np.float32), (0, 2, 1)))
    return d


SCRATCH = {
    'XNT':   ([DM, SP], BF16),
    'XNTOK': ([S, DM], BF16),
    'XNF32': ([S, DM], F32),
    'S0':    ([S, DM], F32),
    'XIA':   ([S, DM], F32),
    'XIB':   ([S, DM], F32),
    'XITA':  ([DM, SP], BF16),
    'XITB':  ([DM, SP], BF16),
    'QHT':   ([FF_INNER, S], BF16),
    'KHT':   ([FF_INNER, S], BF16),
    'VH':    ([S, FF_INNER], BF16),
    'OT':    ([FF_INNER, S], BF16),
    'FF':    ([S, DM], F32),
    'X1':    ([S, DM], F32),
    'XN1T':  ([DM, SP], BF16),
    'XN1TLO': ([DM, SP], BF16),
    'XN1TOK': ([S, DM], BF16),
    'XTOK':  ([S, DM], BF16),
    'BT':    ([64, S], BF16),
    'CT':    ([64, S], BF16),
    'BTOK':  ([S, 64], BF16),
    'ACT_s': ([32, S], F32),
    'RT_s':  ([32, S], F32),
    'S1':    ([S, DM], F32),
    'MOEACC': ([S, DM], F32),
    'DWT':   ([8, S], F32),
    'YTOK':  ([S, DM], BF16),
    'KVT':   ([128, L0A_NH * 64], BF16),
    'KDIAG': ([128, 128], BF16),
}

CONSTS = {
    'eye': ([128, 128], F32), 'eyebf': ([128, 128], BF16),
    'onesbf': ([128, 128], BF16), 'onesf': ([2, 128], F32),
    'rot_qc': ([S, 64], F32), 'rot_qs': ([S, 64], F32),
    'rot_kc': ([S, 64], F32), 'rot_ks': ([S, 64], F32),
    'mask_gen': ([FF_WIN, 2 * FF_WIN], BF16), 'mask_w0': ([FF_WIN, 2 * FF_WIN], BF16),
    'ssd_mask': ([SSD_CS, SSD_CS], F32),
    'pairsel': ([16, 8 * 128], F32), 'sel32': ([32, 32 * 128], F32),
    'sel8': ([8, 8 * 128], F32),
    'l0a_Wq': ([DM, DM], F32), 'l0a_Wk': ([DM, DM], F32),
    'l0a_Wv': ([DM, DM], F32), 'l0a_Wd': ([DM, DM], F32),
    'l0a_bq': ([1, DM], F32), 'l0a_bk': ([1, DM], F32),
    'l0a_bv': ([1, DM], F32), 'l0a_bd': ([1, DM], F32),
    'l0a_gamma': ([1, DM], F32), 'l0a_beta': ([1, DM], F32),
    'l0a_bq_c': ([128, 8], F32),
    'ff_wqkv': ([2, DM, 1536], F32), 'ff_wo': ([2, FF_INNER, DM], F32),
    'ff_w1': ([2, DM, 2 * FF_HPAD], F32), 'ff_b1': ([2, 1, 2 * FF_HPAD], F32),
    'ff_w2': ([2, FF_HPAD, DM], F32), 'ff_b2': ([2, 1, DM], F32),
    'ff_qs': ([2, 1, 64], F32), 'ff_ks': ([2, 1, 64], F32),
    'ff_lng': ([2, 1, DM], F32), 'ff_lnb': ([2, 1, DM], F32),
    'ssd_wint': ([DM, 1184], F32), 'ssd_wout': ([DM, DM], F32),
    'ssd_dtb': ([32, 1], F32), 'ssd_alog': ([32, 1], F32), 'ssd_d': ([1, 32], F32),
    'moe_rt': ([DM, 8], F32),
    'moe_wg': ([8, DM, MOE_F], F32), 'moe_wu': ([8, DM, MOE_F], F32),
    'moe_wd': ([8, MOE_F, DM], F32),
}


class Ctx:

    def __init__(self, fused, phase_inputs=None, phase_outputs=None):
        self.nc = bass.Bass("TRN2", target_bir_lowering=False)
        self.fused = fused
        self.dram = {}
        self._stack = []
        nc = self.nc
        if fused:
            self.dram['x'] = nc.dram_tensor('x', [S, DM], F32, kind='ExternalInput').ap()
            self.dram['out'] = nc.dram_tensor('out', [S, DM], F32, kind='ExternalOutput').ap()
            for n, (shp, dt) in CONSTS.items():
                self.dram[n] = nc.dram_tensor(n, shp, dt, kind='ExternalInput').ap()
            for n, (shp, dt) in SCRATCH.items():
                self.dram[n] = nc.dram_tensor(n, shp, dt, kind='Internal').ap()
        else:
            ins = set(phase_inputs)
            outs = set(phase_outputs)
            for n in sorted(ins):
                shp, dt = self._spec(n)
                self.dram[n] = nc.dram_tensor(n, shp, dt, kind='ExternalInput').ap()
            for n in sorted(outs):
                shp, dt = self._spec(n)
                self.dram[n] = nc.dram_tensor(n, shp, dt, kind='ExternalOutput').ap()
        self.sems = {}
        self.cnt = {}
        for name in ['pe', 'dve', 'act', 'pool', 'qs', 'qg', 'qa', 'qv']:
            sem = nc.semaphore(f'sem_{name}')
            self.sems[name] = self._enter(sem)
            self.cnt[name] = 0

    def _spec(self, n):
        if n == 'x':
            return [S, DM], F32
        if n == 'out':
            return [S, DM], F32
        if n in CONSTS:
            return CONSTS[n]
        return SCRATCH[n]

    def _enter(self, cm):
        v = cm.__enter__()
        self._stack.append(cm)
        return v

    _uid = 0

    def sbuf(self, name, shape, dtype):
        Ctx._uid += 1
        return self._enter(self.nc.sbuf_tensor(f'{name}_u{Ctx._uid}', shape, dtype))

    def psum(self, name, shape, dtype=F32):
        Ctx._uid += 1
        return self._enter(self.nc.psum_tensor(f'{name}_u{Ctx._uid}', shape, dtype))

    def close(self):
        while self._stack:
            self._stack.pop().__exit__(None, None, None)

    def inc(self, name, instr, n=1):
        instr.then_inc(self.sems[name], n)
        self.cnt[name] += n
        return self.cnt[name]

    def wait(self, eng, name, val=None):
        if val is None:
            val = self.cnt[name]
        if val > 0:
            eng.wait_ge(self.sems[name], val)

    def barrier(self, engines):
        for e in engines:
            for name in self.sems:
                self.wait(e, name)


BATCH = 8


def to_bf16(a):
    return np.asarray(a).astype(ml_dtypes.bfloat16)


class Scope:
    def __init__(self, ctx):
        self.ctx = ctx
        self.mark = len(ctx._stack)

    def __enter__(self):
        return self

    def __exit__(self, *a):
        st = self.ctx._stack
        while len(st) > self.mark:
            st.pop().__exit__(None, None, None)


def p_ln0(ctx, src='x', dst_tok='XNTOK', dst_t='XNT', dst_f32='XNF32', eps=1e-5):
    nc = ctx.nc
    d = ctx.dram
    with Scope(ctx):
        xt = [ctx.sbuf(f'ln_x{i}', [128, DM], F32) for i in range(2)]
        sq = ctx.sbuf('ln_sq', [128, DM], F32)
        st = ctx.sbuf('ln_st', [128, 8], F32)
        xn = ctx.sbuf('ln_xn', [128, DM], BF16)
        xnf = ctx.sbuf('ln_xnf', [128, DM], F32)
        eyebf = ctx.sbuf('ln_eye', [128, 128], BF16)
        tacc = ctx.sbuf('ln_tacc', [128, NF * SP], BF16)
        tps = ctx.psum('ln_tps', [128, 1024], BF16)
        ld_tok = [0, 0]
        cons_tok = [0, 0]
        st_tok = 0
        tp_tok = 0
        q = nc.sync
        g = nc.gpsimd
        v = nc.vector
        s = nc.scalar
        t = nc.tensor
        ctx.inc('qs', q.dma_start(out=eyebf[:], in_=d['eyebf'][:, :]), 16)
        ctx.wait(v, 'qs')
        for k in range(NF):
            ctx.inc('dve', v.memset(tacc[:, k * SP:k * SP + 1], 0.0))
        for i in range(NT):
            b = i % 2
            if i >= 2:
                ctx.wait(q, 'dve', cons_tok[b])
            ctx.inc('qs', q.dma_start(out=xt[b][:], in_=d[src][i * 128:(i + 1) * 128, :]), 16)
            ld_tok[b] = ctx.cnt['qs']
            x = xt[b]
            ctx.wait(v, 'qs', ld_tok[b])
            ctx.inc('dve', v.reduce_sum(out=st[:, 0:1], in_=x[:], axis=AX.X))
            ctx.wait(s, 'dve')
            ctx.inc('act', s.activation(out=sq[:], in_=x[:], func=AF.Square))
            ctx.wait(v, 'act')
            ctx.inc('dve', v.reduce_sum(out=st[:, 1:2], in_=sq[:], axis=AX.X))
            v.drain()
            ctx.inc('dve', v.tensor_scalar(out=st[:, 2:3], in0=st[:, 0:1],
                                           scalar1=1.0 / DM, scalar2=None, op0=OP.mult))
            ctx.inc('dve', v.tensor_scalar(out=st[:, 3:4], in0=st[:, 1:2],
                                           scalar1=1.0 / DM, scalar2=None, op0=OP.mult))
            v.drain()
            ctx.inc('dve', v.tensor_tensor(out=st[:, 5:6], in0=st[:, 2:3],
                                           in1=st[:, 2:3], op=OP.mult))
            v.drain()
            ctx.inc('dve', v.tensor_tensor(out=st[:, 4:5], in0=st[:, 3:4],
                                           in1=st[:, 5:6], op=OP.subtract))
            v.drain()
            ctx.inc('dve', v.tensor_scalar(out=st[:, 4:5], in0=st[:, 4:5],
                                           scalar1=float(eps), scalar2=None, op0=OP.add))
            ctx.wait(s, 'dve')
            ctx.inc('act', s.activation(out=st[:, 6:7], in_=st[:, 4:5], func=AF.Sqrt))
            s.drain()
            ctx.wait(v, 'act')
            ctx.inc('dve', v.reciprocal(out=st[:, 6:7], in_=st[:, 6:7]))
            v.drain()
            ctx.inc('dve', v.tensor_tensor(out=st[:, 7:8], in0=st[:, 2:3],
                                           in1=st[:, 6:7], op=OP.mult))
            v.drain()
            ctx.inc('dve', v.tensor_scalar(out=st[:, 7:8], in0=st[:, 7:8],
                                           scalar1=-1.0, scalar2=None, op0=OP.mult))
            v.drain()
            ctx.wait(v, 'qs', st_tok)
            ctx.wait(v, 'pe', tp_tok)
            ctx.inc('dve', v.tensor_scalar(out=xnf[:], in0=x[:], scalar1=st[:, 6:7],
                                           scalar2=st[:, 7:8], op0=OP.mult, op1=OP.add))
            v.drain()
            ctx.inc('dve', v.tensor_copy(out=xn[:], in_=xnf[:]))
            cons_tok[b] = ctx.cnt['dve']
            ctx.wait(q, 'dve', cons_tok[b])
            ctx.inc('qs', q.dma_start(out=d[dst_tok][i * 128:(i + 1) * 128, :], in_=xn[:]), 16)
            if dst_f32 is not None:
                ctx.inc('qs', q.dma_start(out=d[dst_f32][i * 128:(i + 1) * 128, :], in_=xnf[:]), 16)
            st_tok = ctx.cnt['qs']
            ctx.wait(t, 'dve')
            for k in range(NF):
                mm = t.transpose(tps[:, k * 128:(k + 1) * 128], xn[:, k * 128:(k + 1) * 128],
                                 eyebf[:, 0:128])
                if k == NF - 1:
                    ctx.inc('pe', mm)
            tp_tok = ctx.cnt['pe']
            ctx.wait(v, 'pe', tp_tok)
            for k in range(NF):
                instr = v.tensor_copy(out=tacc[:, k * SP + 1 + i * 128: k * SP + 1 + (i + 1) * 128],
                                      in_=tps[:, k * 128:(k + 1) * 128])
                ctx.inc('dve', instr)
        ctx.wait(q, 'dve')
        for k in range(NF):
            ctx.inc('qs', q.dma_start(out=d[dst_t][k * 128:(k + 1) * 128, :],
                                      in_=tacc[:, k * SP:(k + 1) * SP]), 16)


def finish(ctx):
    ctx.barrier([ctx.nc.sync, ctx.nc.gpsimd])


PHASE_DEFS = {}


def run_phase_standalone(name, vals_list, inputs, outputs, emit, core_ids=None):
    ctx = Ctx(fused=False, phase_inputs=inputs, phase_outputs=outputs)
    emit(ctx)
    finish(ctx)
    ctx.close()
    in_maps = []
    for vals in vals_list:
        in_maps.append({n: vals[n] for n in inputs})
    if core_ids is None:
        core_ids = list(range(len(vals_list)))
    res = run_bass_kernel_spmd(ctx.nc, in_maps, core_ids=core_ids)
    return [{n: r[n] for n in outputs} for r in res.results]


def p_l0a_A(ctx):
    nc = ctx.nc
    d = ctx.dram
    q, g, v, s, t = nc.sync, nc.gpsimd, nc.vector, nc.scalar, nc.tensor
    with Scope(ctx):
        wk = ctx.sbuf('aA_wk', [128, NF * DM], BF16)
        wv = ctx.sbuf('aA_wv', [128, NF * DM], BF16)
        bk = ctx.sbuf('aA_bk', [1, DM], BF16)
        bv = ctx.sbuf('aA_bv', [1, DM], BF16)
        ones = ctx.sbuf('aA_ones', [128, 128], BF16)
        kt = ctx.sbuf('aA_kt', [128, NT * DM], BF16)
        vt = ctx.sbuf('aA_vt', [128, NT * DM], BF16)
        xc = [ctx.sbuf(f'aA_xc{i}', [128, NF * 128], BF16) for i in range(2)]
        t1 = ctx.sbuf('aA_t1', [128, 512], F32)
        t2 = ctx.sbuf('aA_t2', [128, 512], BF16)
        eb = ctx.sbuf('aA_eb', [128, 512], BF16)
        kvtb = ctx.sbuf('aA_kvtb', [128, L0A_NH * 64], BF16)
        ksf = ctx.sbuf('aA_ksf', [128, 8], F32)
        kdg = ctx.sbuf('aA_kdg', [128, 128], BF16)
        pk = ctx.psum('aA_pk', [128, 512], F32)
        pv = ctx.psum('aA_pv', [128, 512], F32)
        pA = ctx.psum('aA_pA', [128, 512], F32)
        pB = ctx.psum('aA_pB', [128, 512], F32)
        pks = ctx.psum('aA_pks', [128, 512], F32)
        kvbanks = [pk, pv, pA, pB]
        for k in range(NF):
            ctx.inc('qg', g.dma_start(out=wk[:, k * DM:(k + 1) * DM],
                                      in_=d['l0a_Wk'][k * 128:(k + 1) * 128, :]), 16)
            ctx.inc('qg', g.dma_start(out=wv[:, k * DM:(k + 1) * DM],
                                      in_=d['l0a_Wv'][k * 128:(k + 1) * 128, :]), 16)
        ctx.inc('qg', g.dma_start(out=bk[:, :], in_=d['l0a_bk'][:, :]), 16)
        ctx.inc('qg', g.dma_start(out=bv[:, :], in_=d['l0a_bv'][:, :]), 16)
        ctx.inc('qs', q.dma_start(out=ones[:], in_=d['onesbf'][:, :]), 16)
        wtok = ctx.cnt['qg']
        otok = ctx.cnt['qs']
        ld_tok = [0, 0]
        rel_tok = [0, 0]
        ps_rel = 0
        for i in range(NT):
            b = i % 2
            if i >= 2:
                ctx.wait(q, 'pe', rel_tok[b])
            for k in range(NF):
                ctx.inc('qs', q.dma_start(out=xc[b][:, k * 128:(k + 1) * 128],
                                          in_=d['XNT'][k * 128:(k + 1) * 128,
                                                       1 + i * 128:1 + (i + 1) * 128]), 16)
            ld_tok[b] = ctx.cnt['qs']
            ctx.wait(t, 'qs', ld_tok[b])
            if i == 0:
                ctx.wait(t, 'qg', wtok)
                ctx.wait(t, 'qs', otok)
            for half in range(2):
                ctx.wait(t, 'dve', ps_rel)
                for k in range(NF):
                    t.matmul(pk[:, :], xc[b][:, k * 128:(k + 1) * 128],
                             wk[:, k * DM + half * 512:k * DM + (half + 1) * 512],
                             start=(k == 0), stop=False)
                t.matmul(pk[:, :], ones[0:1, 0:128], bk[0:1, half * 512:(half + 1) * 512],
                         start=False, stop=True)
                for k in range(NF):
                    t.matmul(pv[:, :], xc[b][:, k * 128:(k + 1) * 128],
                             wv[:, k * DM + half * 512:k * DM + (half + 1) * 512],
                             start=(k == 0), stop=False)
                mm = t.matmul(pv[:, :], ones[0:1, 0:128], bv[0:1, half * 512:(half + 1) * 512],
                              start=False, stop=True)
                ctx.inc('pe', mm)
                if half == 1:
                    rel_tok[b] = ctx.cnt['pe']
                ctx.wait(v, 'pe')
                cbase = i * DM + half * 512
                ctx.wait(v, 'act')
                ctx.inc('dve', v.tensor_scalar(out=t1[:], in0=pk[:, :], scalar1=0.0,
                                               scalar2=None, op0=OP.min))
                ctx.wait(s, 'dve')
                ctx.inc('act', s.activation(out=eb[:], in_=t1[:], func=AF.Exp))
                ctx.inc('dve', v.tensor_scalar(out=t2[:], in0=pk[:, :], scalar1=0.0,
                                               scalar2=None, op0=OP.max))
                v.drain()
                ctx.wait(v, 'act')
                ctx.inc('dve', v.tensor_tensor(out=kt[:, cbase:cbase + 512],
                                               in0=eb[:], in1=t2[:], op=OP.add))
                ctx.inc('dve', v.tensor_copy(out=vt[:, cbase:cbase + 512], in_=pv[:, :]))
                ps_rel = ctx.cnt['dve']
        ctx.wait(t, 'dve')
        evac_tok = {}
        for h in range(L0A_NH):
            bank = kvbanks[h % 4]
            if h >= 4:
                ctx.wait(t, 'dve', evac_tok[h % 4])
            for i in range(NT):
                mm = t.matmul(bank[0:64, 0:64],
                              kt[:, i * DM + h * 64:i * DM + (h + 1) * 64],
                              vt[:, i * DM + h * 64:i * DM + (h + 1) * 64],
                              start=(i == 0), stop=(i == NT - 1))
            ctx.inc('pe', mm)
            ctx.wait(v, 'pe')
            ctx.inc('dve', v.tensor_copy(out=kvtb[0:64, h * 64:(h + 1) * 64],
                                         in_=bank[0:64, 0:64]))
            ctx.inc('dve', v.tensor_copy(out=kvtb[64:128, h * 64:(h + 1) * 64],
                                         in_=bank[0:64, 0:64]))
            evac_tok[h % 4] = ctx.cnt['dve']
        for kti in range(8):
            for i in range(NT):
                mm = t.matmul(pks[:, kti:kti + 1],
                              kt[:, i * DM + kti * 128:i * DM + (kti + 1) * 128],
                              ones[:, 0:1], start=(i == 0), stop=(i == NT - 1))
            ctx.inc('pe', mm)
        ctx.wait(v, 'pe')
        ctx.inc('dve', v.tensor_scalar(out=ksf[:], in0=pks[:, 0:8], scalar1=1e-6,
                                       scalar2=None, op0=OP.add))
        ctx.inc('dve', v.memset(kdg[:], 0.0))
        v.drain()
        for kti in range(8):
            for hh in range(2):
                h = 2 * kti + hh
                ctx.inc('dve', v.tensor_copy(out=kdg[hh * 64:(hh + 1) * 64,
                                                     kti * 16 + h:kti * 16 + h + 1],
                                             in_=ksf[hh * 64:(hh + 1) * 64, kti:kti + 1]))
        v.drain()
        ctx.wait(q, 'dve')
        ctx.inc('qs', q.dma_start(out=d['KVT'][:, :], in_=kvtb[:]), 16)
        ctx.inc('qs', q.dma_start(out=d['KDIAG'][:, :], in_=kdg[:]), 16)


def emit_lnstats(ctx, x_ap, st, sq, eps, width=DM):
    nc = ctx.nc
    v, s = nc.vector, nc.scalar
    ctx.inc('dve', v.reduce_sum(out=st[:, 0:1], in_=x_ap, axis=AX.X))
    ctx.wait(s, 'dve')
    ctx.inc('act', s.activation(out=sq[:, 0:width], in_=x_ap, func=AF.Square))
    ctx.wait(v, 'act')
    ctx.inc('dve', v.reduce_sum(out=st[:, 1:2], in_=sq[:, 0:width], axis=AX.X))
    v.drain()
    ctx.inc('dve', v.tensor_scalar(out=st[:, 2:3], in0=st[:, 0:1],
                                   scalar1=1.0 / width, scalar2=None, op0=OP.mult))
    ctx.inc('dve', v.tensor_scalar(out=st[:, 3:4], in0=st[:, 1:2],
                                   scalar1=1.0 / width, scalar2=None, op0=OP.mult))
    v.drain()
    ctx.inc('dve', v.tensor_tensor(out=st[:, 5:6], in0=st[:, 2:3],
                                   in1=st[:, 2:3], op=OP.mult))
    v.drain()
    ctx.inc('dve', v.tensor_tensor(out=st[:, 4:5], in0=st[:, 3:4],
                                   in1=st[:, 5:6], op=OP.subtract))
    v.drain()
    ctx.inc('dve', v.tensor_scalar(out=st[:, 4:5], in0=st[:, 4:5],
                                   scalar1=float(eps), scalar2=None, op0=OP.add))
    ctx.wait(s, 'dve')
    ctx.inc('act', s.activation(out=st[:, 6:7], in_=st[:, 4:5], func=AF.Sqrt))
    s.drain()
    ctx.wait(v, 'act')
    ctx.inc('dve', v.reciprocal(out=st[:, 6:7], in_=st[:, 6:7]))
    v.drain()
    ctx.inc('dve', v.tensor_tensor(out=st[:, 7:8], in0=st[:, 2:3],
                                   in1=st[:, 6:7], op=OP.mult))
    v.drain()
    ctx.inc('dve', v.tensor_scalar(out=st[:, 7:8], in0=st[:, 7:8],
                                   scalar1=-1.0, scalar2=None, op0=OP.mult))
    v.drain()


def cast_loads(ctx, dst_sb, src_name, nrow_tiles, cols, dst_stride=None):
    g = ctx.nc.gpsimd
    d = ctx.dram
    if dst_stride is None:
        dst_stride = cols
    for k in range(nrow_tiles):
        ctx.inc('qg', g.dma_start(out=dst_sb[:, k * dst_stride:k * dst_stride + cols],
                                  in_=d[src_name][k * 128:(k + 1) * 128, :]), 16)


def p_l0a_B(ctx):
    nc = ctx.nc
    d = ctx.dram
    q, g, v, s, t = nc.sync, nc.gpsimd, nc.vector, nc.scalar, nc.tensor
    with Scope(ctx):
        wq = ctx.sbuf('aB_wq', [128, NF * DM], BF16)
        wd = ctx.sbuf('aB_wd', [128, NF * DM], BF16)
        bqc = ctx.sbuf('aB_bqc', [128, 8], F32)
        bdr = ctx.sbuf('aB_bdr', [1, DM], BF16)
        kvt = ctx.sbuf('aB_kvt', [128, L0A_NH * 64], BF16)
        kdg = ctx.sbuf('aB_kdg', [128, 128], BF16)
        eye = ctx.sbuf('aB_eye', [128, 128], F32)
        ones = ctx.sbuf('aB_ones', [128, 128], BF16)
        onesf = ctx.sbuf('aB_onesf', [2, 128], F32)
        gam = ctx.sbuf('aB_gam', [1, DM], F32)
        bet = ctx.sbuf('aB_bet', [1, DM], F32)
        psel = ctx.sbuf('aB_psel', [16, 8 * 128], F32)
        gb = ctx.sbuf('aB_gb', [128, DM], F32)
        bb = ctx.sbuf('aB_bb', [128, DM], F32)
        xch = [ctx.sbuf(f'aB_xch{i}', [128, NF * 512], BF16) for i in range(2)]
        qh = ctx.sbuf('aB_qh', [128, NF * 512], BF16)
        qz = ctx.sbuf('aB_qz', [128, NF * 512], BF16)
        ctxT = ctx.sbuf('aB_ctxT', [128, NF * 512], BF16)
        t1 = ctx.sbuf('aB_t1', [128, 512], F32)
        t2 = ctx.sbuf('aB_t2', [128, 512], BF16)
        eb = ctx.sbuf('aB_eb', [128, 512], BF16)
        zt = ctx.sbuf('aB_zt', [128, 16], F32)
        ztr = ctx.sbuf('aB_ztr', [16, 512], F32)
        xtk = ctx.sbuf('aB_xtk', [128, DM], BF16)
        hs = ctx.sbuf('aB_hs', [128, DM], F32)
        hsn = ctx.sbuf('aB_hsn', [128, DM], F32)
        s0t = ctx.sbuf('aB_s0t', [128, DM], F32)
        sq = ctx.sbuf('aB_sq', [128, DM], F32)
        stt = ctx.sbuf('aB_stt', [128, 8], F32)
        pba = ctx.psum('aB_pa', [128, 512], F32)
        pbb = ctx.psum('aB_pb', [128, 512], F32)
        pbc = ctx.psum('aB_pc', [128, 512], F32)
        pbd = ctx.psum('aB_pd', [128, 512], F32)
        pbe = ctx.psum('aB_pe', [128, 512], F32)

        cast_loads(ctx, wq, 'l0a_Wq', NF, DM)
        cast_loads(ctx, wd, 'l0a_Wd', NF, DM)
        ctx.inc('qg', g.dma_start(out=bdr[:, :], in_=d['l0a_bd'][:, :]), 16)
        for nm_, sb_ in [('l0a_bq_c', bqc), ('KVT', kvt), ('KDIAG', kdg), ('eye', eye),
                         ('onesbf', ones), ('onesf', onesf), ('l0a_gamma', gam),
                         ('l0a_beta', bet), ('pairsel', psel)]:
            ctx.inc('qs', q.dma_start(out=sb_[:], in_=d[nm_][:, :]), 16)
        ctx.barrier([t])
        for oh in range(2):
            t.matmul(pba[:, :], onesf[0:1, 0:128], gam[0:1, oh * 512:(oh + 1) * 512],
                     start=True, stop=True)
            mm = t.matmul(pbb[:, :], onesf[0:1, 0:128], bet[0:1, oh * 512:(oh + 1) * 512],
                          start=True, stop=True)
            ctx.inc('pe', mm)
            ctx.wait(v, 'pe')
            ctx.inc('dve', v.tensor_copy(out=gb[:, oh * 512:(oh + 1) * 512], in_=pba[:, :]))
            ctx.inc('dve', v.tensor_copy(out=bb[:, oh * 512:(oh + 1) * 512], in_=pbb[:, :]))
            ctx.wait(t, 'dve')

        ld_tok = [0, 0]
        rel_tok = [0, 0]
        qrel = [0, 0]
        denrel = 0
        zbrel = 0
        ctxrel = [0, 0]
        wdrel = [0, 0]
        sttok = 0
        for c in range(8):
            b = c % 2
            if c >= 2:
                ctx.wait(q, 'pe', rel_tok[b])
            for k in range(NF):
                ctx.inc('qs', q.dma_start(out=xch[b][:, k * 512:(k + 1) * 512],
                                          in_=d['XNT'][k * 128:(k + 1) * 128,
                                                       1 + c * 512:1 + (c + 1) * 512]), 16)
            ld_tok[b] = ctx.cnt['qs']
            ctx.wait(t, 'qs', ld_tok[b])
            for o in range(NF):
                bank = [pba, pbb][o % 2]
                ctx.wait(t, 'dve', qrel[o % 2])
                for k in range(NF):
                    mm = t.matmul(bank[:, :], wq[:, k * DM + o * 128:k * DM + (o + 1) * 128],
                                  xch[b][:, k * 512:(k + 1) * 512],
                                  start=(k == 0), stop=(k == NF - 1))
                ctx.inc('pe', mm)
                if o == NF - 1:
                    rel_tok[b] = ctx.cnt['pe']
                ctx.wait(v, 'pe')
                ctx.wait(v, 'act')
                ctx.inc('dve', v.tensor_scalar(out=t1[:], in0=bank[:, :],
                                               scalar1=bqc[:, o:o + 1], scalar2=0.0,
                                               op0=OP.add, op1=OP.min))
                ctx.wait(s, 'dve')
                ctx.inc('act', s.activation(out=eb[:], in_=t1[:], func=AF.Exp))
                ctx.inc('dve', v.tensor_scalar(out=t2[:], in0=bank[:, :],
                                               scalar1=bqc[:, o:o + 1], scalar2=0.0,
                                               op0=OP.add, op1=OP.max))
                v.drain()
                ctx.wait(v, 'act')
                ctx.inc('dve', v.tensor_tensor(out=qh[:, o * 512:(o + 1) * 512],
                                               in0=eb[:], in1=t2[:], op=OP.add))
                qrel[o % 2] = ctx.cnt['dve']
            ctx.wait(t, 'dve')
            for j in range(4):
                ctx.wait(t, 'dve', denrel)
                for o in range(NF):
                    mm = t.matmul(pbc[:, 0:16],
                                  qh[:, o * 512 + j * 128:o * 512 + (j + 1) * 128],
                                  kdg[:, o * 16:(o + 1) * 16],
                                  start=(o == 0), stop=(o == NF - 1))
                ctx.inc('pe', mm)
                ctx.wait(v, 'pe')
                ctx.inc('dve', v.reciprocal(out=zt[:, 0:16], in_=pbc[:, 0:16]))
                denrel = ctx.cnt['dve']
                ctx.wait(t, 'dve', denrel)
                mm = t.transpose(pbd[0:16, j * 128:(j + 1) * 128], zt[:, 0:16],
                                 eye[:, 0:128])
                ctx.inc('pe', mm)
            ctx.wait(v, 'pe')
            ctx.inc('dve', v.tensor_copy(out=ztr[:, :], in_=pbd[0:16, 0:512]))
            ztok = ctx.cnt['dve']
            for jj in range(NF):
                ctx.wait(t, 'dve', max(zbrel, ztok))
                mm = t.matmul(pbe[:, :], psel[:, jj * 128:(jj + 1) * 128],
                              ztr[:, 0:512], start=True, stop=True)
                ctx.inc('pe', mm)
                ctx.wait(v, 'pe')
                ctx.inc('dve', v.tensor_tensor(out=qz[:, jj * 512:(jj + 1) * 512],
                                               in0=qh[:, jj * 512:(jj + 1) * 512],
                                               in1=pbe[:, :], op=OP.mult))
                zbrel = ctx.cnt['dve']
            ctx.wait(t, 'dve')
            for h in range(L0A_NH):
                bank = [pba, pbb][h % 2]
                ctx.wait(t, 'dve', ctxrel[h % 2])
                hb = (h % 2) * 64
                mm = t.matmul(bank[0:64, 0:512], kvt[hb:hb + 64, h * 64:(h + 1) * 64],
                              qz[hb:hb + 64, (h // 2) * 512:(h // 2 + 1) * 512],
                              start=True, stop=True)
                ctx.inc('pe', mm)
                ctx.wait(v, 'pe')
                ctx.inc('dve', v.tensor_copy(out=ctxT[(h % 2) * 64:(h % 2) * 64 + 64,
                                                      (h // 2) * 512:(h // 2 + 1) * 512],
                                             in_=bank[0:64, 0:512]))
                ctxrel[h % 2] = ctx.cnt['dve']
            ctx.wait(t, 'dve')
            for j in range(4):
                ctx.wait(q, 'dve')
                ctx.inc('qs', q.dma_start(out=xtk[:],
                                          in_=d['XNTOK'][c * 512 + j * 128:c * 512 + (j + 1) * 128, :]), 16)
                xload = ctx.cnt['qs']
                for oh in range(2):
                    bank = [pbc, pbd][oh]
                    ctx.wait(t, 'dve', wdrel[oh])
                    for m in range(NF):
                        t.matmul(bank[:, :], ctxT[:, m * 512 + j * 128:m * 512 + (j + 1) * 128],
                                 wd[:, m * DM + oh * 512:m * DM + (oh + 1) * 512],
                                 start=(m == 0), stop=False)
                    mm = t.matmul(bank[:, :], ones[0:1, 0:128],
                                  bdr[0:1, oh * 512:(oh + 1) * 512], start=False, stop=True)
                    ctx.inc('pe', mm)
                    ctx.wait(v, 'pe')
                    ctx.wait(v, 'qs', xload)
                    ctx.inc('dve', v.tensor_tensor(out=hs[:, oh * 512:(oh + 1) * 512],
                                                   in0=pbc[:, :] if oh == 0 else pbd[:, :],
                                                   in1=xtk[:, oh * 512:(oh + 1) * 512],
                                                   op=OP.add))
                    wdrel[oh] = ctx.cnt['dve']
                v.drain()
                emit_lnstats(ctx, hs[:, 0:DM], stt, sq, 1e-12)
                ctx.wait(v, 'qs', sttok)
                ctx.inc('dve', v.tensor_scalar(out=hsn[:], in0=hs[:], scalar1=stt[:, 6:7],
                                               scalar2=stt[:, 7:8], op0=OP.mult, op1=OP.add))
                v.drain()
                ctx.inc('dve', v.tensor_tensor(out=s0t[:], in0=hsn[:], in1=gb[:], op=OP.mult))
                v.drain()
                ctx.inc('dve', v.tensor_tensor(out=s0t[:], in0=s0t[:], in1=bb[:], op=OP.add))
                ctx.wait(q, 'dve')
                ctx.inc('qs', q.dma_start(out=d['S0'][c * 512 + j * 128:c * 512 + (j + 1) * 128, :],
                                          in_=s0t[:]), 16)
                sttok = ctx.cnt['qs']


def p_tr(ctx, src, dst):
    nc = ctx.nc
    d = ctx.dram
    q, g, v, t = nc.sync, nc.gpsimd, nc.vector, nc.tensor
    with Scope(ctx):
        xb = [ctx.sbuf(f'tr_x{i}', [128, DM], BF16) for i in range(2)]
        eyebf = ctx.sbuf('tr_eye', [128, 128], BF16)
        tacc = ctx.sbuf('tr_tacc', [128, NF * SP], BF16)
        tps = ctx.psum('tr_tps', [128, 1024], BF16)
        ctx.inc('qs', q.dma_start(out=eyebf[:], in_=d['eyebf'][:, :]), 16)
        ctx.wait(v, 'qs')
        for k in range(NF):
            ctx.inc('dve', v.memset(tacc[:, k * SP:k * SP + 1], 0.0))
        ld_tok = [0, 0]
        rel_tok = [0, 0]
        for i in range(NT):
            b = i % 2
            if i >= 2:
                ctx.wait(g, 'pe', rel_tok[b])
            ctx.inc('qg', g.dma_start(out=xb[b][:], in_=d[src][i * 128:(i + 1) * 128, :]), 16)
            ld_tok[b] = ctx.cnt['qg']
            ctx.wait(t, 'qg', ld_tok[b])
            ctx.wait(t, 'dve')
            for k in range(NF):
                mm = t.transpose(tps[:, k * 128:(k + 1) * 128],
                                 xb[b][:, k * 128:(k + 1) * 128], eyebf[:, 0:128])
            ctx.inc('pe', mm)
            rel_tok[b] = ctx.cnt['pe']
            ctx.wait(v, 'pe', rel_tok[b])
            for k in range(NF):
                ctx.inc('dve', v.tensor_copy(out=tacc[:, k * SP + 1 + i * 128:k * SP + 1 + (i + 1) * 128],
                                             in_=tps[:, k * 128:(k + 1) * 128]))
        ctx.wait(q, 'dve')
        for k in range(NF):
            ctx.inc('qs', q.dma_start(out=d[dst][k * 128:(k + 1) * 128, :],
                                      in_=tacc[:, k * SP:(k + 1) * SP]), 16)


def p_ffn_qkv(ctx, l, src_t):
    nc = ctx.nc
    d = ctx.dram
    q, g, v, s, t = nc.sync, nc.gpsimd, nc.vector, nc.scalar, nc.tensor
    with Scope(ctx):
        wq = ctx.sbuf('fq_w', [128, NF * 1536], BF16)
        hch = [ctx.sbuf(f'fq_h{i}', [128, NF * 128], BF16) for i in range(2)]
        taq = ctx.sbuf('fq_taq', [128, NT * 64], F32)
        tsq = ctx.sbuf('fq_tsq', [128, NT * 64], F32)
        tak = ctx.sbuf('fq_tak', [128, NT * 64], F32)
        tsk = ctx.sbuf('fq_tsk', [128, NT * 64], F32)
        rot = ctx.sbuf('fq_rot', [128, 64], F32)
        qsb = ctx.sbuf('fq_qsb', [128, 128], F32)
        qsw = ctx.sbuf('fq_qsw', [128, 128], F32)
        qsr = ctx.sbuf('fq_qsr', [1, 128], F32)
        onesf = ctx.sbuf('fq_onesf', [2, 128], F32)
        eyebf = ctx.sbuf('fq_eye', [128, 128], BF16)
        sq = ctx.sbuf('fq_sq', [128, 512], F32)
        nsq = ctx.sbuf('fq_nsq', [128, 16], F32)
        q0 = ctx.sbuf('fq_q0', [128, 512], F32)
        qa = ctx.sbuf('fq_qa', [128, 512], F32)
        qr = ctx.sbuf('fq_qr', [128, 512], F32)
        qf = ctx.sbuf('fq_qf', [128, 512], BF16)
        vh = ctx.sbuf('fq_vh', [128, 512], BF16)
        qht = ctx.sbuf('fq_qht', [128, 4 * S], BF16)
        kht = ctx.sbuf('fq_kht', [128, 4 * S], BF16)
        pq = ctx.psum('fq_pq', [128, 512], F32)
        pk = ctx.psum('fq_pk', [128, 512], F32)
        pv = ctx.psum('fq_pv', [128, 512], F32)
        ptr = ctx.psum('fq_ptr', [128, 512], BF16)
        for k in range(NF):
            ctx.inc('qg', g.dma_start(out=wq[:, k * 1536:(k + 1) * 1536],
                                      in_=d['ff_wqkv'][l, k * 128:(k + 1) * 128, :]), 16)
        ctx.inc('qs', q.dma_start(out=qsr[0:1, 0:64], in_=d['ff_qs'][l, :, :]), 16)
        ctx.inc('qs', q.dma_start(out=qsr[0:1, 64:128], in_=d['ff_ks'][l, :, :]), 16)
        ctx.inc('qs', q.dma_start(out=onesf[:], in_=d['onesf'][:, :]), 16)
        ctx.inc('qs', q.dma_start(out=eyebf[:], in_=d['eyebf'][:, :]), 16)
        ctx.barrier([t])
        t.matmul(pq[:, 0:64], onesf[0:1, 0:128], qsr[0:1, 0:64], start=True, stop=True)
        mm = t.matmul(pq[:, 64:128], onesf[0:1, 0:128], qsr[0:1, 64:128], start=True, stop=True)
        ctx.inc('pe', mm)
        ctx.wait(v, 'pe')
        ctx.inc('dve', v.tensor_copy(out=qsb[:, :], in_=pq[:, 0:128]))
        v.drain()
        ctx.inc('dve', v.tensor_scalar(out=qsb[:, 0:64], in0=qsb[:, 0:64],
                                       scalar1=float(8.0), scalar2=None, op0=OP.mult))
        v.drain()
        for base in (0, 64):
            ctx.inc('dve', v.tensor_copy(out=qsw[:, base:base + 32],
                                         in_=qsb[:, base + 32:base + 64]))
            ctx.inc('dve', v.tensor_copy(out=qsw[:, base + 32:base + 64],
                                         in_=qsb[:, base:base + 32]))
        v.drain()
        for i in range(NT):
            for (nm_, dst_, scb, sc_) in [('rot_qc', taq, qsb, 0), ('rot_qs', tsq, qsw, 0),
                                          ('rot_kc', tak, qsb, 64), ('rot_ks', tsk, qsw, 64)]:
                ctx.wait(q, 'dve')
                ctx.inc('qs', q.dma_start(out=rot[:, :],
                                          in_=d[nm_][i * 128:(i + 1) * 128, :]), 16)
                ctx.wait(v, 'qs')
                ctx.inc('dve', v.tensor_tensor(out=dst_[:, i * 64:(i + 1) * 64],
                                               in0=rot[:, :],
                                               in1=scb[:, sc_:sc_ + 64], op=OP.mult))
        qkvrel = [0, 0]
        ld_tok = [0, 0]
        tp_tok = 0
        vh_tok = [0]
        for i in range(NT):
            b = i % 2
            if i >= 2:
                ctx.wait(q, 'pe', qkvrel[b])
            for k in range(NF):
                off = 1 if k < 4 else 0
                ctx.inc('qs', q.dma_start(out=hch[b][:, k * 128:(k + 1) * 128],
                                          in_=d[src_t][k * 128:(k + 1) * 128,
                                                       off + i * 128:off + (i + 1) * 128]), 16)
            ld_tok[b] = ctx.cnt['qs']
            ctx.wait(t, 'qs', ld_tok[b])
            ctx.wait(t, 'dve')
            for p, bank in [(0, pq), (1, pk), (2, pv)]:
                for k in range(NF):
                    mm = t.matmul(bank[:, :], hch[b][:, k * 128:(k + 1) * 128],
                                  wq[:, k * 1536 + p * 512:k * 1536 + (p + 1) * 512],
                                  start=(k == 0), stop=(k == NF - 1))
            ctx.inc('pe', mm)
            qkvrel[b] = ctx.cnt['pe']
            ctx.wait(v, 'pe')
            ctx.wait(v, 'qs', vh_tok[0])
            ctx.inc('dve', v.tensor_copy(out=vh[:], in_=pv[:, :]))
            ctx.wait(q, 'dve')
            ctx.inc('qs', q.dma_start(out=d['VH'][i * 128:(i + 1) * 128, :], in_=vh[:]), 16)
            vh_tok[0] = ctx.cnt['qs']
            for (bank, tc, tss, scoff, acc) in [(pq, taq, tsq, 0, qht), (pk, tak, tsk, 8, kht)]:
                ctx.wait(s, 'pe')
                ctx.wait(s, 'dve')
                ctx.inc('act', s.activation(out=sq[:], in_=bank[:, :], func=AF.Square))
                ctx.wait(v, 'act')
                ctx.inc('dve', v.tensor_reduce(out=nsq[:, scoff:scoff + 8].unsqueeze(2),
                                               in_=sq[:].rearrange('p (h e) -> p h e', h=8),
                                               axis=AX.X, op=OP.add))
                v.drain()
                ctx.wait(s, 'dve')
                ctx.inc('act', s.activation(out=nsq[:, scoff:scoff + 8],
                                            in_=nsq[:, scoff:scoff + 8], func=AF.Sqrt))
                s.drain()
                ctx.wait(v, 'act')
                ctx.inc('dve', v.tensor_scalar(out=nsq[:, scoff:scoff + 8],
                                               in0=nsq[:, scoff:scoff + 8],
                                               scalar1=1e-12, scalar2=None, op0=OP.max))
                v.drain()
                ctx.inc('dve', v.reciprocal(out=nsq[:, scoff:scoff + 8],
                                            in_=nsq[:, scoff:scoff + 8]))
                v.drain()
                rn3 = nsq[:, scoff:scoff + 8].unsqueeze(2).broadcast_to([128, 8, 64])
                ctx.inc('dve', v.tensor_tensor(out=q0[:].rearrange('p (h e) -> p h e', h=8),
                                               in0=bank[:, :].rearrange('p (h e) -> p h e', h=8),
                                               in1=rn3, op=OP.mult))
                v.drain()
                dvtok = ctx.cnt['dve']
                tc3 = tc[:, i * 64:(i + 1) * 64].unsqueeze(1).broadcast_to([128, 8, 64])
                ctx.wait(g, 'dve', dvtok)
                ctx.inc('pool', g.tensor_tensor(out=qa[:].rearrange('p (h e) -> p h e', h=8),
                                                in0=q0[:].rearrange('p (h e) -> p h e', h=8),
                                                in1=tc3, op=OP.mult))
                tlo = tss[:, i * 64:i * 64 + 32].unsqueeze(1).broadcast_to([128, 8, 32])
                thi = tss[:, i * 64 + 32:(i + 1) * 64].unsqueeze(1).broadcast_to([128, 8, 32])
                q03 = q0[:].rearrange('p (h e) -> p h e', h=8)
                qr3 = qr[:].rearrange('p (h e) -> p h e', h=8)
                ctx.inc('pool', g.tensor_tensor(out=qr3[:, :, 0:32], in0=q03[:, :, 32:64],
                                                in1=tlo, op=OP.mult))
                ctx.inc('pool', g.tensor_tensor(out=qr3[:, :, 32:64], in0=q03[:, :, 0:32],
                                                in1=thi, op=OP.mult))
                g.drain()
                ptok = ctx.cnt['pool']
                ctx.wait(g, 'pe', tp_tok)
                ctx.inc('pool', g.tensor_tensor(out=qf[:], in0=qa[:], in1=qr[:], op=OP.add))
                ptok = ctx.cnt['pool']
                ctx.wait(t, 'pool', ptok)
                for db in range(4):
                    mm = t.transpose(ptr[:, db * 128:(db + 1) * 128],
                                     qf[:, db * 128:(db + 1) * 128], eyebf[:, 0:128])
                ctx.inc('pe', mm)
                tp_tok = ctx.cnt['pe']
                ctx.wait(v, 'pe', tp_tok)
                for db in range(4):
                    ctx.inc('dve', v.tensor_copy(out=acc[:, db * S + i * 128:db * S + (i + 1) * 128],
                                                 in_=ptr[:, db * 128:(db + 1) * 128]))
        ctx.wait(q, 'dve')
        for db in range(4):
            ctx.inc('qs', q.dma_start(out=d['QHT'][db * 128:(db + 1) * 128, :],
                                      in_=qht[:, db * S:(db + 1) * S]), 16)
            ctx.inc('qs', q.dma_start(out=d['KHT'][db * 128:(db + 1) * 128, :],
                                      in_=kht[:, db * S:(db + 1) * S]), 16)


def p_ffn_att(ctx, l):
    nc = ctx.nc
    d = ctx.dram
    q, g, v, s, t = nc.sync, nc.gpsimd, nc.vector, nc.scalar, nc.tensor
    with Scope(ctx):
        qht = ctx.sbuf('fa_qht', [128, 4 * S], BF16)
        kht = ctx.sbuf('fa_kht', [128, 4 * S], BF16)
        vhs = ctx.sbuf('fa_vh', [128, NT * FF_INNER], BF16)
        otacc = ctx.sbuf('fa_ot', [128, 4 * S], BF16)
        mgen = ctx.sbuf('fa_mgen', [128, 4 * 640], BF16)
        mw0 = ctx.sbuf('fa_mw0', [128, 4 * 512], BF16)
        eyebf = ctx.sbuf('fa_eye', [128, 128], BF16)
        mcol = [ctx.sbuf(f'fa_mc{i}', [128, 4], F32) for i in range(2)]
        ebuf = [ctx.sbuf(f'fa_e{i}', [128, 640], BF16) for i in range(2)]
        att = [ctx.sbuf(f'fa_at{i}', [128, 640], BF16) for i in range(2)]
        atT = [ctx.sbuf(f'fa_atT{i}', [128, 5 * 128], BF16) for i in range(2)]
        ovec = [ctx.sbuf(f'fa_ov{i}', [64, 512], BF16) for i in range(2)]
        psim = [ctx.psum(f'fa_psim{i}', [128, 1024], F32) for i in range(2)]
        ptr = [ctx.psum(f'fa_ptr{i}', [128, 640], BF16) for i in range(2)]
        po = [ctx.psum(f'fa_po{i}', [64, 512], F32) for i in range(2)]
        for db in range(4):
            ctx.inc('qs', q.dma_start(out=qht[:, db * S:(db + 1) * S],
                                      in_=d['QHT'][db * 128:(db + 1) * 128, :]), 16)
            ctx.inc('qs', q.dma_start(out=kht[:, db * S:(db + 1) * S],
                                      in_=d['KHT'][db * 128:(db + 1) * 128, :]), 16)
        for i in range(NT):
            ctx.inc('qg', g.dma_start(out=vhs[:, i * FF_INNER:(i + 1) * FF_INNER],
                                      in_=d['VH'][i * 128:(i + 1) * 128, :]), 16)
        for isub in range(4):
            ctx.inc('qs', q.dma_start(out=mgen[:, isub * 640:(isub + 1) * 640],
                                      in_=d['mask_gen'][isub * 128:(isub + 1) * 128,
                                                        isub * 128:isub * 128 + 640]), 16)
            ctx.inc('qs', q.dma_start(out=mw0[:, isub * 512:(isub + 1) * 512],
                                      in_=d['mask_w0'][isub * 128:(isub + 1) * 128,
                                                       512:1024]), 16)
        ctx.inc('qs', q.dma_start(out=eyebf[:], in_=d['eyebf'][:, :]), 16)
        ctx.barrier([t, v, s, g])
        units = []
        for h in range(FF_HEADS):
            for w in range(8):
                for isub in range(4):
                    units.append((h, w, isub))
        NU = len(units)

        def geom(u):
            h, w, isub = units[u]
            hb = (h % 2) * 64
            db = h // 2
            if w == 0:
                W = (isub + 1) * 128
                kcol0 = 0
                mask_ap = mw0[:, isub * 512:isub * 512 + W]
            else:
                W = 640
                kcol0 = (w - 1) * FF_WIN + isub * 128
                mask_ap = mgen[:, isub * 640:isub * 640 + W]
            return h, w, isub, hb, db, W, kcol0, mask_ap

        smtok = [0, 0]
        simtok = [0, 0]
        expdone = [0, 0]
        atttok = [0, 0]
        trtok = [0, 0]
        evactok = [0, 0]
        otok = [0, 0]
        ostok = [0, 0]
        for u in range(NU + 2):
            par = u % 2
            if u < NU:
                h, w, isub, hb, db, W, kcol0, mask_ap = geom(u)
                ctx.wait(t, 'act', expdone[par])
                ctx.wait(t, 'dve', smtok[par])
                lhs = qht[hb:hb + 64, db * S + w * FF_WIN + isub * 128:
                          db * S + w * FF_WIN + (isub + 1) * 128]
                done = 0
                while done < W:
                    nn = min(512, W - done)
                    t.matmul(psim[par][:, done:done + nn], lhs,
                             kht[hb:hb + 64, db * S + kcol0 + done:
                                 db * S + kcol0 + done + nn],
                             start=True, stop=False)
                    mm = t.matmul(psim[par][:, done:done + nn], eyebf[:, 0:128],
                                  mask_ap[:, done:done + nn], start=False, stop=True)
                    done += nn
                ctx.inc('pe', mm)
                simtok[par] = ctx.cnt['pe']
                ctx.wait(v, 'pe', simtok[par])
                ctx.wait(v, 'act', expdone[par])
                ctx.inc('dve', v.reduce_max(out=mcol[par][:, 0:1], in_=psim[par][:, 0:W],
                                            axis=AX.X, negate=True))
                v.drain()
                smtok[par] = ctx.cnt['dve']
                ctx.wait(s, 'dve', smtok[par])
                ctx.wait(s, 'dve', atttok[par])
                ctx.inc('act', s.activation(out=ebuf[par][:, 0:W], in_=psim[par][:, 0:W],
                                            func=AF.Exp, bias=mcol[par][:, 0:1], scale=1.0,
                                            accum_out=mcol[par][:, 2:3]))
                s.drain()
                expdone[par] = ctx.cnt['act']
            if 1 <= u < NU + 1:
                u1 = u - 1
                p1 = u1 % 2
                h, w, isub, hb, db, W, kcol0, mask_ap = geom(u1)
                ctx.wait(v, 'act', expdone[p1])
                ctx.inc('dve', v.reciprocal(out=mcol[p1][:, 3:4], in_=mcol[p1][:, 2:3]))
                v.drain()
                ctx.wait(v, 'pe', trtok[p1])
                ctx.inc('dve', v.tensor_scalar(out=att[p1][:, 0:W], in0=ebuf[p1][:, 0:W],
                                               scalar1=mcol[p1][:, 3:4], scalar2=None,
                                               op0=OP.mult))
                atttok[p1] = ctx.cnt['dve']
                ctx.wait(t, 'dve', atttok[p1])
                ctx.wait(t, 'act', evactok[p1])
                nb = W // 128
                for bbk in range(nb):
                    mm = t.transpose(ptr[p1][:, bbk * 128:(bbk + 1) * 128],
                                     att[p1][:, bbk * 128:(bbk + 1) * 128], eyebf[:, 0:128])
                ctx.inc('pe', mm)
                trtok[p1] = ctx.cnt['pe']
            if u >= 2:
                u2 = u - 2
                p2 = u2 % 2
                h, w, isub, hb, db, W, kcol0, mask_ap = geom(u2)
                hwpar = (h * 8 + w) % 2
                nb = W // 128
                ctx.wait(s, 'pe', trtok[p2])
                ctx.wait(s, 'pe', otok[p2])
                ctx.inc('act', s.copy(out=atT[p2][:, 0:nb * 128], in_=ptr[p2][:, 0:nb * 128]))
                s.drain()
                evactok[p2] = ctx.cnt['act']
                ctx.wait(t, 'act', evactok[p2])
                if isub == 0:
                    ctx.wait(t, 'qs', ostok[hwpar])
                for bi in range(nb):
                    ktok = kcol0 + bi * 128
                    vtile = ktok // 128
                    mm = t.matmul(po[hwpar][0:64, isub * 128:(isub + 1) * 128],
                                  vhs[:, vtile * FF_INNER + h * 64:
                                      vtile * FF_INNER + (h + 1) * 64],
                                  atT[p2][:, bi * 128:(bi + 1) * 128],
                                  start=(bi == 0), stop=(bi == nb - 1))
                ctx.inc('pe', mm)
                otok[p2] = ctx.cnt['pe']
                if isub == 3:
                    ctx.wait(v, 'pe', otok[p2])
                    ctx.wait(v, 'qs', ostok[hwpar])
                    ctx.inc('dve', v.tensor_copy(out=ovec[hwpar][:, :], in_=po[hwpar][0:64, 0:512]))
                    ctx.wait(q, 'dve')
                    ctx.inc('qs', q.dma_start(
                        out=d['OT'].rearrange('(a p) s -> a p s', p=128)[db, hb:hb + 64,
                            w * FF_WIN:(w + 1) * FF_WIN],
                        in_=ovec[hwpar][:, :]), 16)
                    ostok[hwpar] = ctx.cnt['qs']


def p_ffn_wo(ctx, l, res_src, res_f32):
    nc = ctx.nc
    d = ctx.dram
    q, g, v, s, t = nc.sync, nc.gpsimd, nc.vector, nc.scalar, nc.tensor
    with Scope(ctx):
        wo = ctx.sbuf('fw_wo', [128, 4 * DM], BF16)
        och = [ctx.sbuf(f'fw_o{i}', [128, 4 * 128], BF16) for i in range(2)]
        res = ctx.sbuf('fw_res', [128, DM], F32 if res_f32 else BF16)
        xo = ctx.sbuf('fw_xo', [128, DM], F32)
        p1 = ctx.psum('fw_p1', [128, 512], F32)
        p2 = ctx.psum('fw_p2', [128, 512], F32)
        for k in range(4):
            ctx.inc('qg', g.dma_start(out=wo[:, k * DM:(k + 1) * DM],
                                      in_=d['ff_wo'][l, k * 128:(k + 1) * 128, :]), 16)
        ctx.barrier([t])
        ld_tok = [0, 0]
        rel_tok = [0, 0]
        st_tok = 0
        for i in range(NT):
            b = i % 2
            if i >= 2:
                ctx.wait(q, 'pe', rel_tok[b])
            for k in range(4):
                ctx.inc('qs', q.dma_start(out=och[b][:, k * 128:(k + 1) * 128],
                                          in_=d['OT'][k * 128:(k + 1) * 128,
                                                      i * 128:(i + 1) * 128]), 16)
            ctx.wait(q, 'dve')
            ctx.inc('qs', q.dma_start(out=res[:], in_=d[res_src][i * 128:(i + 1) * 128, :]), 16)
            ld_tok[b] = ctx.cnt['qs']
            ctx.wait(t, 'qs', ld_tok[b])
            ctx.wait(t, 'dve')
            for oh, bank in [(0, p1), (1, p2)]:
                for k in range(4):
                    mm = t.matmul(bank[:, :], och[b][:, k * 128:(k + 1) * 128],
                                  wo[:, k * DM + oh * 512:k * DM + (oh + 1) * 512],
                                  start=(k == 0), stop=(k == 3))
                ctx.inc('pe', mm)
            rel_tok[b] = ctx.cnt['pe']
            ctx.wait(v, 'pe')
            ctx.wait(v, 'qs', st_tok)
            for oh, bank in [(0, p1), (1, p2)]:
                ctx.inc('dve', v.tensor_tensor(out=xo[:, oh * 512:(oh + 1) * 512],
                                               in0=bank[:, :],
                                               in1=res[:, oh * 512:(oh + 1) * 512], op=OP.add))
            ctx.wait(q, 'dve')
            ctx.inc('qs', q.dma_start(out=d['XIA'][i * 128:(i + 1) * 128, :], in_=xo[:]), 16)
            st_tok = ctx.cnt['qs']


def p_ffn_ffn(ctx, l, dst):
    nc = ctx.nc
    d = ctx.dram
    q, g, v, s, t = nc.sync, nc.gpsimd, nc.vector, nc.scalar, nc.tensor
    NJ = FF_HPAD // 128
    with Scope(ctx):
        hch0 = ctx.sbuf('fn_h0', [128, NF * 512], BF16)
        hch = [hch0, hch0]
        lnh = ctx.sbuf('fn_lnh', [128, NF * 512], BF16)
        lnh1 = ctx.sbuf('fn_lnh1', [128, 512], BF16)
        w1s = ctx.sbuf('fn_w1s', [128, NF * 2 * FF_HPAD], BF16)
        w2b = ctx.sbuf('fn_w2b', [128, NJ * DM], BF16)
        hsb = ctx.sbuf('fn_hsb', [128, NJ * 512], BF16)
        ggb = ctx.sbuf('fn_ggb', [128, 1024], BF16)
        b1r = ctx.sbuf('fn_b1r', [1, 2 * FF_HPAD], BF16)
        b2r = ctx.sbuf('fn_b2r', [1, DM], BF16)
        gamr = ctx.sbuf('fn_gamr', [1, DM], F32)
        betr = ctx.sbuf('fn_betr', [1, DM], F32)
        onesb = ctx.sbuf('fn_onesb', [128, 512], BF16)
        onesf = ctx.sbuf('fn_onesf', [1, 512], F32)
        srow = ctx.sbuf('fn_srow', [1, 4 * 512], F32)
        xi = ctx.sbuf('fn_xi', [128, 512], F32)
        pls = [ctx.psum(f'fn_p{i}', [128, 512], F32) for i in range(8)]
        pS, pS2, pAb, pBb, pa, pg_, pw1, pw2 = pls
        W1W = 2 * FF_HPAD
        for k in range(NF):
            ctx.inc('qg', g.dma_start(out=w1s[:, k * W1W:(k + 1) * W1W],
                                      in_=d['ff_w1'][l, k * 128:(k + 1) * 128, :]), 16)
        for j in range(NJ):
            ctx.inc('qg', g.dma_start(out=w2b[:, j * DM:(j + 1) * DM],
                                      in_=d['ff_w2'][l, j * 128:(j + 1) * 128, :]), 16)
        ctx.inc('qg', g.dma_start(out=b1r[:, :], in_=d['ff_b1'][l, :, :]), 16)
        ctx.inc('qg', g.dma_start(out=b2r[:, :], in_=d['ff_b2'][l, :, :]), 16)
        ctx.inc('qg', g.memset(onesb[:], 1.0), 1)
        ctx.inc('qg', g.memset(onesf[:], 1.0), 1)
        ctx.inc('qs', q.dma_start(out=gamr[:, :], in_=d['ff_lng'][l, :, :]), 16)
        ctx.inc('qs', q.dma_start(out=betr[:, :], in_=d['ff_lnb'][l, :, :]), 16)
        ctx.barrier([t])
        ld_tok = [0, 0]
        hrel = [0, 0]
        st_tok = 0
        w1ld = [0, 0]
        w1rel = [0, 0]
        for c in range(8):
            b = c % 2
            if c >= 2:
                ctx.wait(q, 'dve', hrel[b])
            for k in range(NF):
                off = 1 if k < 4 else 0
                ctx.inc('qs', q.dma_start(out=hch[b][:, k * 512:(k + 1) * 512],
                                          in_=d['XITA'][k * 128:(k + 1) * 128,
                                                        off + c * 512:off + (c + 1) * 512]), 16)
            ld_tok[b] = ctx.cnt['qs']
            ctx.wait(t, 'qs', ld_tok[b])
            ctx.wait(t, 'dve')
            ctx.wait(t, 'act')
            for k in range(NF):
                mm = t.matmul(pS[0:1, :], onesb[:, 0:1], hch[b][:, k * 512:(k + 1) * 512],
                              start=(k == 0), stop=(k == NF - 1))
            ctx.inc('pe', mm)
            sqtok = 0
            sqb = ggb[:, 0:512]
            for k in range(NF):
                ctx.wait(s, 'qs', ld_tok[b])
                if k == 0:
                    ctx.wait(s, 'dve')
                if k >= 1:
                    ctx.wait(s, 'pe', sqtok)
                ctx.inc('act', s.activation(out=sqb, in_=hch[b][:, k * 512:(k + 1) * 512],
                                            func=AF.Square))
                ctx.wait(t, 'act')
                mm = t.matmul(pS2[0:1, :], onesb[:, 0:1], sqb,
                              start=(k == 0), stop=(k == NF - 1))
                ctx.inc('pe', mm)
                sqtok = ctx.cnt['pe']
            ctx.wait(v, 'pe')
            ctx.inc('dve', v.tensor_scalar(out=srow[:, 0:512], in0=pS[0:1, 0:512],
                                           scalar1=1.0 / DM, scalar2=None, op0=OP.mult))
            ctx.inc('dve', v.tensor_scalar(out=srow[:, 512:1024], in0=pS2[0:1, 0:512],
                                           scalar1=1.0 / DM, scalar2=None, op0=OP.mult))
            v.drain()
            ctx.inc('dve', v.tensor_tensor(out=srow[:, 1024:1536], in0=srow[:, 0:512],
                                           in1=srow[:, 0:512], op=OP.mult))
            v.drain()
            ctx.inc('dve', v.tensor_tensor(out=srow[:, 1536:2048], in0=srow[:, 512:1024],
                                           in1=srow[:, 1024:1536], op=OP.subtract))
            v.drain()
            ctx.inc('dve', v.tensor_scalar(out=srow[:, 1536:2048], in0=srow[:, 1536:2048],
                                           scalar1=1e-5, scalar2=None, op0=OP.add))
            ctx.wait(s, 'dve')
            ctx.inc('act', s.activation(out=srow[:, 512:1024], in_=srow[:, 1536:2048],
                                        func=AF.Sqrt))
            s.drain()
            ctx.wait(v, 'act')
            ctx.inc('dve', v.reciprocal(out=srow[:, 512:1024], in_=srow[:, 512:1024]))
            v.drain()
            ctx.inc('dve', v.tensor_tensor(out=srow[:, 1024:1536], in0=srow[:, 0:512],
                                           in1=srow[:, 512:1024], op=OP.mult))
            v.drain()
            ctx.inc('dve', v.tensor_scalar(out=srow[:, 1024:1536], in0=srow[:, 1024:1536],
                                           scalar1=-1.0, scalar2=None, op0=OP.mult))
            v.drain()
            for kf in range(NF):
                ctx.wait(t, 'dve')
                t.matmul(pAb[:, :], gamr[0:1, kf * 128:(kf + 1) * 128],
                         srow[0:1, 512:1024], start=True, stop=True)
                t.matmul(pBb[:, :], gamr[0:1, kf * 128:(kf + 1) * 128],
                         srow[0:1, 1024:1536], start=True, stop=False)
                mm = t.matmul(pBb[:, :], betr[0:1, kf * 128:(kf + 1) * 128],
                              onesf[0:1, 0:512], start=False, stop=True)
                ctx.inc('pe', mm)
                ctx.wait(v, 'pe')
                ctx.wait(v, 'qs', st_tok)
                ctx.inc('dve', v.tensor_tensor(out=lnh1[:], in0=hch[b][:, kf * 512:(kf + 1) * 512],
                                               in1=pAb[:, :], op=OP.mult))
                v.drain()
                ctx.inc('dve', v.tensor_tensor(out=lnh[:, kf * 512:(kf + 1) * 512],
                                               in0=lnh1[:], in1=pBb[:, :], op=OP.add))
            hrel[b] = ctx.cnt['dve']
            if c == 0:
                ctx.wait(t, 'qg')
            ctx.wait(t, 'dve')
            jtt = [0, 0]
            jgg = [0, 0]
            for j in range(NJ):
                pa_, pg2 = [(pS, pS2), (pAb, pBb)][j % 2]
                ctx.wait(t, 'dve', jtt[j % 2])
                ctx.wait(t, 'act', jgg[j % 2])
                for k in range(NF):
                    t.matmul(pa_[:, :], w1s[:, k * W1W + j * 128:k * W1W + (j + 1) * 128],
                             lnh[:, k * 512:(k + 1) * 512], start=(k == 0), stop=False)
                t.matmul(pa_[:, :], b1r[0:1, j * 128:(j + 1) * 128], onesb[0:1, 0:512],
                         start=False, stop=True)
                for k in range(NF):
                    t.matmul(pg2[:, :], w1s[:, k * W1W + FF_HPAD + j * 128:k * W1W + FF_HPAD + (j + 1) * 128],
                             lnh[:, k * 512:(k + 1) * 512], start=(k == 0), stop=False)
                mm = t.matmul(pg2[:, :], b1r[0:1, FF_HPAD + j * 128:FF_HPAD + (j + 1) * 128],
                              onesb[0:1, 0:512], start=False, stop=True)
                ctx.inc('pe', mm)
                ctx.wait(s, 'pe')
                ctx.wait(s, 'dve', jtt[j % 2])
                ctx.inc('act', s.activation(out=ggb[:, (j % 2) * 512:(j % 2 + 1) * 512],
                                            in_=pg2[:, :], func=AF.Gelu))
                jgg[j % 2] = ctx.cnt['act']
                ctx.wait(v, 'act')
                ctx.inc('dve', v.tensor_tensor(out=hsb[:, j * 512:(j + 1) * 512],
                                               in0=pa_[:, :],
                                               in1=ggb[:, (j % 2) * 512:(j % 2 + 1) * 512],
                                               op=OP.mult))
                jtt[j % 2] = ctx.cnt['dve']
            for tsub in range(4):
                ctx.wait(t, 'dve')
                for oh, bank in [(0, pw1), (1, pw2)]:
                    for j in range(NJ):
                        t.matmul(bank[:, :], hsb[:, j * 512 + tsub * 128:j * 512 + (tsub + 1) * 128],
                                 w2b[:, j * DM + oh * 512:j * DM + (oh + 1) * 512],
                                 start=(j == 0), stop=False)
                    mm = t.matmul(bank[:, :], onesb[0:1, 0:128],
                                  b2r[0:1, oh * 512:(oh + 1) * 512], start=False, stop=True)
                    ctx.inc('pe', mm)
                ctx.wait(v, 'pe')
                for oh, bank in [(0, pw1), (1, pw2)]:
                    ctx.wait(q, 'dve')
                    ctx.inc('qs', q.dma_start(
                        out=xi[:, 0:512],
                        in_=d['XIA'][c * 512 + tsub * 128:c * 512 + (tsub + 1) * 128,
                                     oh * 512:(oh + 1) * 512]), 16)
                    xitok = ctx.cnt['qs']
                    ctx.wait(v, 'qs', xitok)
                    ctx.wait(v, 'qs', st_tok)
                    ov = lnh[:, 0:1024].bitcast(F32)
                    ctx.inc('dve', v.tensor_tensor(out=ov,
                                                   in0=bank[:, :],
                                                   in1=xi[:, 0:512], op=OP.add))
                    ctx.wait(q, 'dve')
                    ctx.inc('qs', q.dma_start(
                        out=d[dst][c * 512 + tsub * 128:c * 512 + (tsub + 1) * 128,
                                   oh * 512:(oh + 1) * 512],
                        in_=ov), 16)
                    st_tok = ctx.cnt['qs']


def p_comb0(ctx):
    nc = ctx.nc
    d = ctx.dram
    q, g, v, s, t = nc.sync, nc.gpsimd, nc.vector, nc.scalar, nc.tensor
    with Scope(ctx):
        xt = [ctx.sbuf(f'cb_x{i}', [128, DM], F32) for i in range(2)]
        st_ = [ctx.sbuf(f'cb_s{i}', [128, DM], F32) for i in range(2)]
        gt = [ctx.sbuf(f'cb_g{i}', [128, DM], F32) for i in range(2)]
        sg = ctx.sbuf('cb_sg', [128, DM], F32)
        x1 = ctx.sbuf('cb_xr', [128, DM], F32)
        sq = ctx.sbuf('cb_sq', [128, DM], F32)
        stt = ctx.sbuf('cb_st', [128, 8], F32)
        xnf = ctx.sbuf('cb_xnf', [128, DM], F32)
        xn = ctx.sbuf('cb_xn', [128, DM], BF16)
        xnlo = ctx.sbuf('cb_xnlo', [128, DM], BF16)
        eyebf = ctx.sbuf('cb_eye', [128, 128], BF16)
        tacc = ctx.sbuf('cb_tacc', [128, NF * SP], BF16)
        tacc2 = ctx.sbuf('cb_tacc2', [128, NF * SP], BF16)
        tps = ctx.psum('cb_tps', [128, 1024], BF16)
        tps2 = ctx.psum('cb_tps2', [128, 1024], BF16)
        ctx.inc('qs', q.dma_start(out=eyebf[:], in_=d['eyebf'][:, :]), 16)
        ctx.wait(v, 'qs')
        for k in range(NF):
            ctx.inc('dve', v.memset(tacc[:, k * SP:k * SP + 1], 0.0))
            ctx.inc('dve', v.memset(tacc2[:, k * SP:k * SP + 1], 0.0))
        ld_tok = [0, 0]
        cons_tok = [0, 0]
        st_tok = 0
        tp_tok = 0
        for i in range(NT):
            b = i % 2
            if i >= 2:
                ctx.wait(q, 'dve', cons_tok[b])
                ctx.wait(g, 'dve', cons_tok[b])
            ctx.inc('qs', q.dma_start(out=xt[b][:], in_=d['x'][i * 128:(i + 1) * 128, :]), 16)
            ctx.inc('qs', q.dma_start(out=st_[b][:], in_=d['S0'][i * 128:(i + 1) * 128, :]), 16)
            ctx.inc('qg', g.dma_start(out=gt[b][:], in_=d['FF'][i * 128:(i + 1) * 128, :]), 16)
            ld_tok[b] = (ctx.cnt['qs'], ctx.cnt['qg'])
            ctx.wait(s, 'qs', ld_tok[b][0])
            ctx.wait(s, 'qg', ld_tok[b][1])
            ctx.wait(s, 'dve')
            ctx.inc('act', s.activation(out=gt[b][:], in_=gt[b][:], func=AF.Silu))
            s.drain()
            ctx.wait(v, 'act')
            ctx.wait(v, 'qs', ld_tok[b][0])
            ctx.inc('dve', v.tensor_tensor(out=sg[:], in0=st_[b][:], in1=gt[b][:], op=OP.mult))
            v.drain()
            ctx.wait(v, 'qs', st_tok)
            ctx.inc('dve', v.tensor_tensor(out=x1[:], in0=xt[b][:], in1=sg[:], op=OP.add))
            cons_tok[b] = ctx.cnt['dve']
            v.drain()
            ctx.wait(q, 'dve', cons_tok[b])
            ctx.inc('qs', q.dma_start(out=d['X1'][i * 128:(i + 1) * 128, :], in_=x1[:]), 16)
            emit_lnstats(ctx, x1[:, 0:DM], stt, sq, 1e-5)
            ctx.wait(v, 'pe', tp_tok)
            ctx.inc('dve', v.tensor_scalar(out=xnf[:], in0=x1[:], scalar1=stt[:, 6:7],
                                           scalar2=stt[:, 7:8], op0=OP.mult, op1=OP.add))
            v.drain()
            ctx.inc('dve', v.tensor_copy(out=xn[:], in_=xnf[:]))
            v.drain()
            ctx.inc('dve', v.tensor_tensor(out=xnlo[:], in0=xnf[:], in1=xn[:], op=OP.subtract))
            v.drain()
            ctx.wait(q, 'dve')
            ctx.inc('qs', q.dma_start(out=d['XN1TOK'][i * 128:(i + 1) * 128, :], in_=xn[:]), 16)
            st_tok = ctx.cnt['qs']
            ctx.wait(t, 'dve')
            for k in range(NF):
                t.transpose(tps[:, k * 128:(k + 1) * 128], xn[:, k * 128:(k + 1) * 128],
                            eyebf[:, 0:128])
                mm = t.transpose(tps2[:, k * 128:(k + 1) * 128], xnlo[:, k * 128:(k + 1) * 128],
                                 eyebf[:, 0:128])
            ctx.inc('pe', mm)
            tp_tok = ctx.cnt['pe']
            ctx.wait(v, 'pe', tp_tok)
            for k in range(NF):
                ctx.inc('dve', v.tensor_copy(out=tacc[:, k * SP + 1 + i * 128:k * SP + 1 + (i + 1) * 128],
                                             in_=tps[:, k * 128:(k + 1) * 128]))
                ctx.inc('dve', v.tensor_copy(out=tacc2[:, k * SP + 1 + i * 128:k * SP + 1 + (i + 1) * 128],
                                             in_=tps2[:, k * 128:(k + 1) * 128]))
        ctx.wait(q, 'dve')
        for k in range(NF):
            ctx.inc('qs', q.dma_start(out=d['XN1T'][k * 128:(k + 1) * 128, :],
                                      in_=tacc[:, k * SP:(k + 1) * SP]), 16)
            ctx.inc('qs', q.dma_start(out=d['XN1TLO'][k * 128:(k + 1) * 128, :],
                                      in_=tacc2[:, k * SP:(k + 1) * SP]), 16)


def p_final(ctx):
    nc = ctx.nc
    d = ctx.dram
    q, g, v, s, t = nc.sync, nc.gpsimd, nc.vector, nc.scalar, nc.tensor
    with Scope(ctx):
        xt = [ctx.sbuf(f'fi_x{i}', [128, DM], F32) for i in range(2)]
        st_ = [ctx.sbuf(f'fi_s{i}', [128, DM], F32) for i in range(2)]
        gt = [ctx.sbuf(f'fi_g{i}', [128, DM], F32) for i in range(2)]
        sg = ctx.sbuf('fi_sg', [128, DM], F32)
        x1 = ctx.sbuf('fi_xr', [128, DM], F32)
        ld_tok = [0, 0]
        cons_tok = [0, 0]
        st_tok = 0
        for i in range(NT):
            b = i % 2
            if i >= 2:
                ctx.wait(q, 'dve', cons_tok[b])
                ctx.wait(g, 'dve', cons_tok[b])
            ctx.inc('qs', q.dma_start(out=xt[b][:], in_=d['X1'][i * 128:(i + 1) * 128, :]), 16)
            ctx.inc('qs', q.dma_start(out=st_[b][:], in_=d['S1'][i * 128:(i + 1) * 128, :]), 16)
            ctx.inc('qg', g.dma_start(out=gt[b][:], in_=d['MOEACC'][i * 128:(i + 1) * 128, :]), 16)
            ld_tok[b] = (ctx.cnt['qs'], ctx.cnt['qg'])
            ctx.wait(s, 'qs', ld_tok[b][0])
            ctx.wait(s, 'qg', ld_tok[b][1])
            ctx.wait(s, 'dve')
            ctx.inc('act', s.activation(out=gt[b][:], in_=gt[b][:], func=AF.Silu))
            s.drain()
            ctx.wait(v, 'act')
            ctx.wait(v, 'qs', ld_tok[b][0])
            ctx.inc('dve', v.tensor_tensor(out=sg[:], in0=st_[b][:], in1=gt[b][:], op=OP.mult))
            v.drain()
            ctx.wait(v, 'qs', st_tok)
            ctx.inc('dve', v.tensor_tensor(out=x1[:], in0=xt[b][:], in1=sg[:], op=OP.add))
            cons_tok[b] = ctx.cnt['dve']
            ctx.wait(q, 'dve', cons_tok[b])
            ctx.inc('qs', q.dma_start(out=d['out'][i * 128:(i + 1) * 128, :], in_=x1[:]), 16)
            st_tok = ctx.cnt['qs']


def p_ssd_proj(ctx):
    nc = ctx.nc
    d = ctx.dram
    q, g, v, s, t = nc.sync, nc.gpsimd, nc.vector, nc.scalar, nc.tensor
    with Scope(ctx):
        wint = ctx.sbuf('sp_w', [128, NF * 1184], BF16)
        xch = [ctx.sbuf(f'sp_x{i}', [128, NF * 512], BF16) for i in range(2)]
        btok = ctx.sbuf('sp_btok', [128, 64], BF16)
        bcsb = ctx.sbuf('sp_bcsb', [128, 512], BF16)
        dtr = ctx.sbuf('sp_dtr', [32, S], F32)
        dtb = ctx.sbuf('sp_dtb', [32, 1], F32)
        alog = ctx.sbuf('sp_alog', [32, 2], F32)
        zrow = ctx.sbuf('sp_z', [32, S], F32)
        tmp = ctx.sbuf('sp_tmp', [32, S], F32)
        dt_ = ctx.sbuf('sp_dt', [32, S], F32)
        at_ = ctx.sbuf('sp_at', [32, S], F32)
        act_ = ctx.sbuf('sp_act', [32, S], F32)
        rt_ = ctx.sbuf('sp_rt', [32, S], F32)
        onec = ctx.sbuf('sp_one', [32, 1], F32)
        xo = ctx.sbuf('sp_xo', [128, DM], BF16)
        pxa = ctx.psum('sp_pa', [128, 512], F32)
        pxb = ctx.psum('sp_pb', [128, 512], F32)
        pxc = ctx.psum('sp_pc', [128, 512], F32)
        for k in range(NF):
            ctx.inc('qg', g.dma_start(out=wint[:, k * 1184:(k + 1) * 1184],
                                      in_=d['ssd_wint'][k * 128:(k + 1) * 128, :]), 16)
        ctx.inc('qs', q.dma_start(out=dtb[:], in_=d['ssd_dtb'][:, :]), 16)
        ctx.inc('qs', q.dma_start(out=alog[:, 0:1], in_=d['ssd_alog'][:, :]), 16)
        ctx.inc('qg', g.memset(onec[:], 1.0), 1)
        ctx.barrier([t])
        ld_tok = [0, 0]
        rel_tok = [0, 0]
        evtok = 0
        xstok = 0
        for i in range(NT):
            b = i % 2
            if i >= 2:
                ctx.wait(q, 'pe', rel_tok[b])
            for k in range(NF):
                ctx.inc('qs', q.dma_start(out=xch[b][:, k * 512:k * 512 + 128],
                                          in_=d['XN1T'][k * 128:(k + 1) * 128,
                                                        1 + i * 128:1 + (i + 1) * 128]), 16)
            ld_tok[b] = ctx.cnt['qs']
            ctx.wait(t, 'qs', ld_tok[b])
            ctx.wait(t, 'dve', evtok)
            for oh, bank in [(0, pxa), (1, pxb)]:
                for k in range(NF):
                    mm = t.matmul(bank[:, :], xch[b][:, k * 512:k * 512 + 128],
                                  wint[:, k * 1184 + oh * 512:k * 1184 + (oh + 1) * 512],
                                  start=(k == 0), stop=(k == NF - 1))
                ctx.inc('pe', mm)
            for k in range(NF):
                mm = t.matmul(pxc[:, 0:64], xch[b][:, k * 512:k * 512 + 128],
                              wint[:, k * 1184 + 1024:k * 1184 + 1088],
                              start=(k == 0), stop=(k == NF - 1))
            ctx.inc('pe', mm)
            rel_tok[b] = ctx.cnt['pe']
            ctx.wait(v, 'pe')
            ctx.wait(v, 'qs', xstok)
            ctx.inc('dve', v.tensor_copy(out=xo[:, 0:512], in_=pxa[:, :]))
            ctx.inc('dve', v.tensor_copy(out=xo[:, 512:1024], in_=pxb[:, :]))
            ctx.inc('dve', v.tensor_copy(out=btok[:, :], in_=pxc[:, 0:64]))
            evtok = ctx.cnt['dve']
            ctx.wait(q, 'dve', evtok)
            ctx.inc('qs', q.dma_start(out=d['XTOK'][i * 128:(i + 1) * 128, :], in_=xo[:]), 16)
            ctx.inc('qs', q.dma_start(out=d['BTOK'][i * 128:(i + 1) * 128, :], in_=btok[:]), 16)
            xstok = ctx.cnt['qs']
        bcrel = 0
        bcst = 0
        for c in range(8):
            b = c % 2
            ctx.wait(q, 'pe', rel_tok[b])
            for k in range(NF):
                ctx.inc('qs', q.dma_start(out=xch[b][:, k * 512:(k + 1) * 512],
                                          in_=d['XN1T'][k * 128:(k + 1) * 128,
                                                        1 + c * 512:1 + (c + 1) * 512]), 16)
            ld_tok[b] = ctx.cnt['qs']
            ctx.wait(t, 'qs', ld_tok[b])
            ctx.wait(t, 'dve', bcrel)
            for k in range(NF):
                mm = t.matmul(pxa[:, :], wint[:, k * 1184 + 1024:k * 1184 + 1152],
                              xch[b][:, k * 512:(k + 1) * 512],
                              start=(k == 0), stop=(k == NF - 1))
            ctx.inc('pe', mm)
            for k in range(NF):
                mm = t.matmul(pxb[0:32, :], wint[:, k * 1184 + 1152:k * 1184 + 1184],
                              xch[b][:, k * 512:(k + 1) * 512],
                              start=(k == 0), stop=(k == NF - 1))
            ctx.inc('pe', mm)
            rel_tok[b] = ctx.cnt['pe']
            ctx.wait(v, 'pe')
            ctx.wait(v, 'qs', bcst)
            ctx.inc('dve', v.tensor_copy(out=bcsb[:, :], in_=pxa[:, :]))
            ctx.inc('dve', v.tensor_copy(out=dtr[0:32, c * 512:(c + 1) * 512],
                                         in_=pxb[0:32, :]))
            bcrel = ctx.cnt['dve']
            ctx.wait(q, 'dve', bcrel)
            ctx.inc('qs', q.dma_start(out=d['BT'][:, c * 512:(c + 1) * 512],
                                      in_=bcsb[0:64, :]), 16)
            ctx.inc('qs', q.dma_start(out=d['CT'][:, c * 512:(c + 1) * 512],
                                      in_=bcsb[64:128, :]), 16)
            bcst = ctx.cnt['qs']
        v.drain()
        ctx.inc('dve', v.tensor_scalar(out=zrow[:], in0=dtr[:], scalar1=dtb[:, 0:1],
                                       scalar2=None, op0=OP.add))
        v.drain()
        ctx.wait(s, 'dve')
        ctx.inc('act', s.activation(out=tmp[:], in_=zrow[:], func=AF.Abs))
        s.drain()
        ctx.inc('act', s.activation(out=tmp[:], in_=tmp[:], func=AF.Exp, scale=-1.0))
        s.drain()
        ctx.inc('act', s.activation(out=tmp[:], in_=tmp[:], func=AF.Ln, bias=1.0))
        s.drain()
        ctx.wait(v, 'act')
        ctx.inc('dve', v.tensor_scalar(out=dt_[:], in0=zrow[:], scalar1=0.0,
                                       scalar2=None, op0=OP.max))
        v.drain()
        ctx.inc('dve', v.tensor_tensor(out=dt_[:], in0=dt_[:], in1=tmp[:], op=OP.add))
        v.drain()
        ctx.wait(s, 'dve')
        ctx.inc('act', s.activation(out=alog[:, 1:2], in_=alog[:, 0:1], func=AF.Exp))
        s.drain()
        ctx.wait(v, 'act')
        ctx.inc('dve', v.tensor_scalar(out=alog[:, 1:2], in0=alog[:, 1:2],
                                       scalar1=-1.0, scalar2=None, op0=OP.mult))
        v.drain()
        ctx.inc('dve', v.tensor_scalar(out=at_[:], in0=dt_[:], scalar1=alog[:, 1:2],
                                       scalar2=None, op0=OP.mult))
        v.drain()
        for c in range(SSD_NC):
            ctx.inc('dve', v.tensor_tensor_scan(
                out=act_[:, c * SSD_CS:(c + 1) * SSD_CS],
                data0=onec[:, 0:1].broadcast_to([32, SSD_CS]),
                data1=at_[:, c * SSD_CS:(c + 1) * SSD_CS],
                initial=0.0, op0=OP.mult, op1=OP.add))
        v.drain()
        ctx.wait(s, 'dve')
        ctx.inc('act', s.activation(out=tmp[:], in_=dt_[:], func=AF.Ln))
        s.drain()
        ctx.wait(v, 'act')
        ctx.inc('dve', v.tensor_tensor(out=rt_[:], in0=act_[:], in1=tmp[:], op=OP.subtract))
        v.drain()
        ctx.wait(q, 'dve')
        ctx.inc('qs', q.dma_start(out=d['ACT_s'][:, :], in_=act_[:]), 16)
        ctx.inc('qs', q.dma_start(out=d['RT_s'][:, :], in_=rt_[:]), 16)


def p_ssd_core(ctx):
    nc = ctx.nc
    d = ctx.dram
    q, g, v, s, t = nc.sync, nc.gpsimd, nc.vector, nc.scalar, nc.tensor
    CS = SSD_CS
    with Scope(ctx):
        acT = ctx.sbuf('sc_ac', [32, S], F32)
        rt_ = ctx.sbuf('sc_rt', [32, S], F32)
        bt = ctx.sbuf('sc_bt', [64, S], BF16)
        ct = ctx.sbuf('sc_ct', [64, S], BF16)
        btok = ctx.sbuf('sc_btok', [128, NT * 64], BF16)
        mask = ctx.sbuf('sc_mask', [128, 128], F32)
        sel = ctx.sbuf('sc_sel', [32, 32 * 128], F32)
        drow = ctx.sbuf('sc_drow', [1, 32], F32)
        onesf = ctx.sbuf('sc_onesf', [2, 128], F32)
        eye = ctx.sbuf('sc_eye', [128, 128], F32)
        xtk = [ctx.sbuf(f'sc_x{i}', [128, DM], BF16) for i in range(2)]
        stage = ctx.sbuf('sc_stage', [96, 128], F32)
        stok = ctx.sbuf('sc_stok', [128, 96], F32)
        cdrow = ctx.sbuf('sc_cdrow', [1, 64], F32)
        eac = ctx.sbuf('sc_eac', [128, 32], F32)
        cbt = ctx.sbuf('sc_cbt', [128, 128], BF16)
        xds = ctx.sbuf('sc_xds', [128, DM], BF16)
        expnt = [ctx.sbuf(f'sc_ex{i}', [128, 512], F32) for i in range(2)]
        ebt = [ctx.sbuf(f'sc_eb{i}', [128, 512], BF16) for i in range(2)]
        mt = [ctx.sbuf(f'sc_mt{i}', [128, 512], BF16) for i in range(2)]
        yo = ctx.sbuf('sc_yo', [128, DM], F32)
        y01 = ctx.sbuf('sc_y01', [128, DM], F32)
        xdd = ctx.sbuf('sc_xdd', [128, DM], F32)
        ytok = ctx.sbuf('sc_ytok', [128, DM], BF16)
        dbb = ctx.sbuf('sc_dbb', [128, 32], F32)
        prevf = ctx.sbuf('sc_prevf', [64, DM], F32)
        prevb = ctx.sbuf('sc_prevb', [64, DM], BF16)
        pE = ctx.psum('sc_pE', [128, 512], F32)
        pST = ctx.psum('sc_pST', [128, 512], F32)
        pAc = pE
        pYd = ctx.psum('sc_pYd', [128, 1024], F32)
        pYo = ctx.psum('sc_pYo', [128, 1024], F32)
        pst = ctx.psum('sc_pst', [64, 1024], F32)
        for nm_, sb_ in [('ACT_s', acT), ('RT_s', rt_), ('BT', bt), ('CT', ct),
                         ('ssd_mask', mask), ('sel32', sel), ('ssd_d', drow),
                         ('onesf', onesf), ('eye', eye)]:
            ctx.inc('qs', q.dma_start(out=sb_[:], in_=d[nm_][:, :]), 16)
        for i in range(NT):
            ctx.inc('qg', g.dma_start(out=btok[:, i * 64:(i + 1) * 64],
                                      in_=d['BTOK'][i * 128:(i + 1) * 128, :]), 16)
        ctx.inc('dve', v.memset(prevf[:], 0.0))
        ctx.inc('dve', v.memset(prevb[:], 0.0))
        ctx.barrier([t, v, s, g])
        mm = t.matmul(pE[:, 0:32], onesf[0:1, 0:128], drow[0:1, 0:32], start=True, stop=True)
        ctx.inc('pe', mm)
        ctx.wait(v, 'pe')
        ctx.inc('dve', v.tensor_copy(out=dbb[:], in_=pE[:, 0:32]))
        ld_tok = [0, 0]
        xrel = [0, 0]
        sttok = 0
        ysttok = 0
        cast_tok = ctx.cnt['dve']
        for c in range(SSD_NC):
            b = c % 2
            if c >= 2:
                ctx.wait(q, 'pe', xrel[b])
            ctx.inc('qs', q.dma_start(out=xtk[b][:], in_=d['XTOK'][c * CS:(c + 1) * CS, :]), 16)
            ld_tok[b] = ctx.cnt['qs']
            acl = acT[:, c * CS + CS - 1:c * CS + CS]
            ctx.wait(v, 'pe')
            ctx.inc('dve', v.tensor_copy(out=stage[0:32, :], in_=acT[:, c * CS:(c + 1) * CS]))
            ctx.inc('dve', v.tensor_scalar(out=stage[32:64, :], in0=rt_[:, c * CS:(c + 1) * CS],
                                           scalar1=-1.0, scalar2=None, op0=OP.mult))
            ctx.inc('dve', v.tensor_scalar(out=stage[64:96, :], in0=rt_[:, c * CS:(c + 1) * CS],
                                           scalar1=acl, scalar2=None, op0=OP.subtract))
            v.drain()
            ctx.wait(s, 'dve')
            ctx.inc('act', s.activation(out=stage[64:96, :], in_=stage[64:96, :],
                                        func=AF.Exp, scale=-1.0))
            s.drain()
            ctx.wait(t, 'act')
            ctx.wait(t, 'dve')
            mm = t.transpose(pST[:, 0:96], stage[0:96, :], eye[0:96, 0:96])
            ctx.inc('pe', mm)
            mm = t.transpose(pST[0:1, 96:128], acl, eye[0:32, 0:32])
            ctx.inc('pe', mm)
            ctx.wait(v, 'pe')
            ctx.wait(v, 'act')
            ctx.inc('dve', v.tensor_copy(out=stok[:], in_=pST[:, 0:96]))
            ctx.inc('dve', v.tensor_copy(out=cdrow[0:1, 0:32], in_=pST[0:1, 96:128]))
            v.drain()
            ctx.wait(s, 'dve')
            ctx.inc('act', s.activation(out=cdrow[0:1, 32:64], in_=cdrow[0:1, 0:32],
                                        func=AF.Exp))
            ctx.inc('act', s.activation(out=eac[:], in_=stok[:, 0:32], func=AF.Exp))
            s.drain()
            ctx.wait(t, 'dve')
            mm = t.matmul(pST[:, 288:416], bt[:, c * CS:(c + 1) * CS],
                          ct[:, c * CS:(c + 1) * CS], start=True, stop=True)
            ctx.inc('pe', mm)
            ctx.wait(v, 'pe')
            ctx.inc('dve', v.tensor_copy(out=cbt[:], in_=pST[:, 288:416]))
            cbtok = ctx.cnt['dve']
            ctx.wait(v, 'qs', ld_tok[b])
            ctx.inc('dve', v.tensor_tensor(
                out=xds[:].rearrange('p (h e) -> p h e', h=32),
                in0=xtk[b][:].rearrange('p (h e) -> p h e', h=32),
                in1=stok[:, 64:96].unsqueeze(2).broadcast_to([128, 32, 32]),
                op=OP.mult))
            v.drain()
            ctx.wait(t, 'dve', cast_tok)
            for oh in range(2):
                mm = t.matmul(pYo[:, oh * 512:(oh + 1) * 512], ct[:, c * CS:(c + 1) * CS],
                              prevb[:, oh * 512:(oh + 1) * 512], start=True, stop=True)
                ctx.inc('pe', mm)
            yoff_tok = ctx.cnt['pe']
            for oh in range(2):
                mm = t.matmul(pst[0:64, oh * 512:(oh + 1) * 512],
                              btok[:, c * 64:(c + 1) * 64],
                              xds[:, oh * 512:(oh + 1) * 512], start=True, stop=True)
                ctx.inc('pe', mm)
            pst_tok = ctx.cnt['pe']
            acfree = 0
            exread = [0, 0]
            mtrel = [0, 0]
            ydtok = [0, 0]
            for hg in range(SSD_NH // 4):
                par = hg % 2
                ctx.wait(t, 'dve', acfree)
                for q_ in range(4):
                    h = hg * 4 + q_
                    mm = t.matmul(pAc[:, q_ * 128:(q_ + 1) * 128],
                                  sel[:, h * 128:(h + 1) * 128],
                                  acT[:, c * CS:(c + 1) * CS], start=True, stop=True)
                ctx.inc('pe', mm)
                ctx.wait(v, 'pe')
                ctx.wait(v, 'act', exread[par])
                ctx.inc('dve', v.tensor_tensor(
                    out=expnt[par][:].rearrange('p (g i) -> p g i', g=4),
                    in0=pAc[:, 0:512].rearrange('p (g i) -> p g i', g=4),
                    in1=mask[:].unsqueeze(1).broadcast_to([128, 4, 128]), op=OP.min))
                acfree = ctx.cnt['dve']
                ctx.wait(s, 'dve', acfree)
                ctx.wait(s, 'pool', mtrel[par])
                for q_ in range(4):
                    h = hg * 4 + q_
                    ctx.inc('act', s.activation(out=ebt[par][:, q_ * 128:(q_ + 1) * 128],
                                                in_=expnt[par][:, q_ * 128:(q_ + 1) * 128],
                                                func=AF.Exp,
                                                bias=stok[:, 32 + h:33 + h], scale=1.0))
                exread[par] = ctx.cnt['act']
                ctx.wait(g, 'act')
                ctx.wait(g, 'pe', ydtok[par])
                ctx.inc('pool', g.tensor_tensor(
                    out=mt[par][:].rearrange('p (g i) -> p g i', g=4),
                    in0=ebt[par][:].rearrange('p (g i) -> p g i', g=4),
                    in1=cbt[:].unsqueeze(1).broadcast_to([128, 4, 128]), op=OP.mult))
                mtrel[par] = ctx.cnt['pool']
                ctx.wait(t, 'pool', mtrel[par])
                for q_ in range(4):
                    h = hg * 4 + q_
                    mm = t.matmul(pYd[:, h * 32:(h + 1) * 32],
                                  mt[par][:, q_ * 128:(q_ + 1) * 128],
                                  xtk[b][:, h * 32:(h + 1) * 32],
                                  start=(h % 16 == 0), stop=(h % 16 == 15))
                ctx.inc('pe', mm)
                ydtok[par] = ctx.cnt['pe']
            xrel[b] = ctx.cnt['pe']
            ctx.wait(v, 'pe')
            ctx.wait(v, 'act')
            ctx.inc('dve', v.tensor_tensor(
                out=yo[:].rearrange('p (h e) -> p h e', h=32),
                in0=pYo[:, :].rearrange('p (h e) -> p h e', h=32),
                in1=eac[:].unsqueeze(2).broadcast_to([128, 32, 32]), op=OP.mult))
            v.drain()
            ctx.inc('dve', v.tensor_tensor(out=y01[:], in0=pYd[:, :], in1=yo[:], op=OP.add))
            ctx.inc('dve', v.tensor_tensor(
                out=xdd[:].rearrange('p (h e) -> p h e', h=32),
                in0=xtk[b][:].rearrange('p (h e) -> p h e', h=32),
                in1=dbb[:].unsqueeze(2).broadcast_to([128, 32, 32]), op=OP.mult))
            v.drain()
            ctx.wait(v, 'qs', ysttok)
            ctx.inc('dve', v.tensor_tensor(out=ytok[:], in0=y01[:], in1=xdd[:], op=OP.add))
            ctx.wait(q, 'dve')
            ctx.inc('qs', q.dma_start(out=d['YTOK'][c * CS:(c + 1) * CS, :], in_=ytok[:]), 16)
            ysttok = ctx.cnt['qs']
            ctx.wait(t, 'act')
            mm = t.matmul(pST[0:64, 128:160], onesf[0:1, 0:64], cdrow[0:1, 32:64],
                          start=True, stop=True)
            ctx.inc('pe', mm)
            ctx.wait(v, 'pe')
            ctx.inc('dve', v.tensor_tensor(
                out=prevf[:].rearrange('p (h e) -> p h e', h=32),
                in0=prevf[:].rearrange('p (h e) -> p h e', h=32),
                in1=pST[0:64, 128:160].unsqueeze(2).broadcast_to([64, 32, 32]),
                op=OP.mult))
            v.drain()
            ctx.inc('dve', v.tensor_tensor(out=prevf[:], in0=prevf[:], in1=pst[0:64, :],
                                           op=OP.add))
            v.drain()
            ctx.inc('dve', v.tensor_copy(out=prevb[:], in_=prevf[:]))
            cast_tok = ctx.cnt['dve']


def p_ssd_out(ctx):
    nc = ctx.nc
    d = ctx.dram
    q, g, v, s, t = nc.sync, nc.gpsimd, nc.vector, nc.scalar, nc.tensor
    with Scope(ctx):
        wout = ctx.sbuf('so_w', [128, NF * DM], BF16)
        yt = [ctx.sbuf(f'so_y{i}', [128, DM], BF16) for i in range(2)]
        ytT = ctx.sbuf('so_ytT', [128, NF * 128], BF16)
        eyebf = ctx.sbuf('so_eye', [128, 128], BF16)
        s1 = ctx.sbuf('so_s1', [128, DM], F32)
        ptA = ctx.psum('so_ptA', [128, 1024], BF16)
        pm1 = ctx.psum('so_pm1', [128, 512], F32)
        pm2 = ctx.psum('so_pm2', [128, 512], F32)
        cast_loads(ctx, wout, 'ssd_wout', NF, DM)
        ctx.inc('qs', q.dma_start(out=eyebf[:], in_=d['eyebf'][:, :]), 16)
        ctx.barrier([t])
        ld_tok = [0, 0]
        rel_tok = [0, 0]
        sttok = 0
        for i in range(NT):
            b = i % 2
            if i >= 2:
                ctx.wait(q, 'pe', rel_tok[b])
            ctx.inc('qs', q.dma_start(out=yt[b][:], in_=d['YTOK'][i * 128:(i + 1) * 128, :]), 16)
            ld_tok[b] = ctx.cnt['qs']
            ctx.wait(t, 'qs', ld_tok[b])
            ctx.wait(t, 'dve')
            for k in range(NF):
                mm = t.transpose(ptA[:, k * 128:(k + 1) * 128],
                                 yt[b][:, k * 128:(k + 1) * 128], eyebf[:, 0:128])
            ctx.inc('pe', mm)
            rel_tok[b] = ctx.cnt['pe']
            ctx.wait(v, 'pe')
            ctx.inc('dve', v.tensor_copy(out=ytT[:], in_=ptA[:, 0:1024]))
            tok = ctx.cnt['dve']
            ctx.wait(t, 'dve', tok)
            for oh, bank in [(0, pm1), (1, pm2)]:
                for k in range(NF):
                    mm = t.matmul(bank[:, :], ytT[:, k * 128:(k + 1) * 128],
                                  wout[:, k * DM + oh * 512:k * DM + (oh + 1) * 512],
                                  start=(k == 0), stop=(k == NF - 1))
                ctx.inc('pe', mm)
            ctx.wait(v, 'pe')
            ctx.wait(v, 'qs', sttok)
            ctx.inc('dve', v.tensor_copy(out=s1[:, 0:512], in_=pm1[:, :]))
            ctx.inc('dve', v.tensor_copy(out=s1[:, 512:1024], in_=pm2[:, :]))
            ctx.wait(q, 'dve')
            ctx.inc('qs', q.dma_start(out=d['S1'][i * 128:(i + 1) * 128, :], in_=s1[:]), 16)
            sttok = ctx.cnt['qs']


def p_moe_rt(ctx):
    nc = ctx.nc
    d = ctx.dram
    q, g, v, s, t = nc.sync, nc.gpsimd, nc.vector, nc.scalar, nc.tensor
    with Scope(ctx):
        rt = ctx.sbuf('mr_rt', [128, NF * 8], BF16)
        rtf = ctx.sbuf('mr_rtf', [128, NF * 8], F32)
        rtlo = ctx.sbuf('mr_rtlo', [128, NF * 8], BF16)
        xch = [ctx.sbuf(f'mr_x{i}', [128, 2 * NF * 128], BF16) for i in range(2)]
        eye = ctx.sbuf('mr_eye', [128, 128], F32)
        w_ = ctx.sbuf('mr_w', [128, 24], F32)
        oh1 = ctx.sbuf('mr_oh1', [128, 8], F32)
        oh2 = ctx.sbuf('mr_oh2', [128, 8], F32)
        t1 = ctx.sbuf('mr_t1', [128, 8], F32)
        s2 = ctx.sbuf('mr_s2', [128, 8], F32)
        e1 = ctx.sbuf('mr_e1', [128, 8], F32)
        apl = ctx.sbuf('mr_apl', [128, 8], F32)
        dw = ctx.sbuf('mr_dw', [128, 8], F32)
        dwT = ctx.sbuf('mr_dwT', [8, S], F32)
        pl = ctx.psum('mr_pl', [128, 512], F32)
        ptr = ctx.psum('mr_ptr', [128, 512], F32)
        cast_loads(ctx, rt, 'moe_rt', NF, 8)
        for k in range(NF):
            ctx.inc('qs', q.dma_start(out=rtf[:, k * 8:(k + 1) * 8],
                                      in_=d['moe_rt'][k * 128:(k + 1) * 128, :]), 16)
        ctx.inc('qs', q.dma_start(out=eye[:], in_=d['eye'][:, :]), 16)
        ctx.wait(v, 'qs')
        ctx.wait(v, 'qg')
        ctx.inc('dve', v.tensor_tensor(out=rtlo[:], in0=rtf[:], in1=rt[:], op=OP.subtract))
        ctx.barrier([t])
        ld_tok = [0, 0]
        rel_tok = [0, 0]
        for i in range(NT):
            b = i % 2
            if i >= 2:
                ctx.wait(q, 'pe', rel_tok[b])
            for k in range(NF):
                ctx.inc('qs', q.dma_start(out=xch[b][:, k * 128:(k + 1) * 128],
                                          in_=d['XN1T'][k * 128:(k + 1) * 128,
                                                        1 + i * 128:1 + (i + 1) * 128]), 16)
            for k in range(NF):
                ctx.inc('qs', q.dma_start(out=xch[b][:, (NF + k) * 128:(NF + k + 1) * 128],
                                          in_=d['XN1TLO'][k * 128:(k + 1) * 128,
                                                          1 + i * 128:1 + (i + 1) * 128]), 16)
            ld_tok[b] = ctx.cnt['qs']
            ctx.wait(t, 'qs', ld_tok[b])
            ctx.wait(t, 'dve')
            ctx.wait(t, 'act')
            for k in range(NF):
                t.matmul(pl[:, 0:8], xch[b][:, k * 128:(k + 1) * 128],
                         rt[:, k * 8:(k + 1) * 8], start=(k == 0), stop=False)
            for k in range(NF):
                t.matmul(pl[:, 0:8], xch[b][:, k * 128:(k + 1) * 128],
                         rtlo[:, k * 8:(k + 1) * 8], start=False, stop=False)
            for k in range(NF):
                mm = t.matmul(pl[:, 0:8], xch[b][:, (NF + k) * 128:(NF + k + 1) * 128],
                              rt[:, k * 8:(k + 1) * 8], start=False, stop=(k == NF - 1))
            ctx.inc('pe', mm)
            rel_tok[b] = ctx.cnt['pe']
            ctx.wait(v, 'pe')
            ctx.inc('dve', v.reduce_max(out=w_[:, 0:1], in_=pl[:, 0:8], axis=AX.X))
            v.drain()
            ctx.inc('dve', v.tensor_scalar(out=t1[:], in0=pl[:, 0:8], scalar1=w_[:, 0:1],
                                           scalar2=None, op0=OP.subtract))
            ctx.inc('dve', v.tensor_scalar(out=oh1[:], in0=pl[:, 0:8], scalar1=w_[:, 0:1],
                                           scalar2=None, op0=OP.is_equal))
            ctx.wait(s, 'pe')
            ctx.wait(s, 'dve')
            ctx.inc('act', s.activation(out=apl[:], in_=pl[:, 0:8], func=AF.Abs))
            s.drain()
            ctx.wait(v, 'act')
            ctx.inc('dve', v.tensor_scalar(out=w_[:, 8:16], in0=apl[:], scalar1=w_[:, 0:1],
                                           scalar2=-0.02, op0=OP.max, op1=OP.mult))
            v.drain()
            ctx.inc('dve', v.tensor_tensor(out=w_[:, 8:16], in0=t1[:], in1=w_[:, 8:16],
                                           op=OP.is_ge))
            v.drain()
            ctx.wait(s, 'dve')
            ctx.inc('act', s.activation(out=e1[:], in_=t1[:], func=AF.Exp))
            s.drain()
            ctx.wait(v, 'act')
            ctx.inc('dve', v.tensor_tensor(out=e1[:], in0=e1[:], in1=w_[:, 8:16], op=OP.mult))
            v.drain()
            ctx.inc('dve', v.reduce_sum(out=w_[:, 2:3], in_=e1[:], axis=AX.X))
            v.drain()
            ctx.inc('dve', v.reciprocal(out=w_[:, 2:3], in_=w_[:, 2:3]))
            ctx.inc('dve', v.scalar_tensor_tensor(out=s2[:], in0=oh1[:], scalar=-BIG,
                                                  in1=pl[:, 0:8], op0=OP.mult, op1=OP.add))
            v.drain()
            ctx.inc('dve', v.reduce_max(out=w_[:, 3:4], in_=s2[:], axis=AX.X))
            v.drain()
            ctx.inc('dve', v.tensor_scalar(out=oh2[:], in0=s2[:], scalar1=w_[:, 3:4],
                                           scalar2=None, op0=OP.is_equal))
            ctx.inc('dve', v.tensor_scalar(out=s2[:], in0=s2[:], scalar1=w_[:, 3:4],
                                           scalar2=None, op0=OP.subtract))
            ctx.inc('dve', v.tensor_scalar(out=w_[:, 16:24], in0=apl[:], scalar1=w_[:, 3:4],
                                           scalar2=-0.02, op0=OP.max, op1=OP.mult))
            ctx.inc('dve', v.tensor_scalar(out=t1[:], in0=pl[:, 0:8], scalar1=w_[:, 3:4],
                                           scalar2=None, op0=OP.subtract))
            v.drain()
            ctx.inc('dve', v.tensor_tensor(out=w_[:, 16:24], in0=t1[:], in1=w_[:, 16:24],
                                           op=OP.is_ge))
            v.drain()
            ctx.wait(s, 'dve')
            ctx.inc('act', s.activation(out=s2[:], in_=s2[:], func=AF.Exp))
            s.drain()
            ctx.wait(v, 'act')
            ctx.inc('dve', v.tensor_tensor(out=s2[:], in0=s2[:], in1=w_[:, 16:24], op=OP.mult))
            v.drain()
            ctx.inc('dve', v.reduce_sum(out=w_[:, 5:6], in_=s2[:], axis=AX.X))
            v.drain()
            ctx.inc('dve', v.reciprocal(out=w_[:, 5:6], in_=w_[:, 5:6]))
            v.drain()
            ctx.inc('dve', v.tensor_scalar(out=dw[:], in0=oh1[:], scalar1=w_[:, 2:3],
                                           scalar2=None, op0=OP.mult))
            ctx.inc('dve', v.tensor_scalar(out=oh2[:], in0=oh2[:], scalar1=w_[:, 5:6],
                                           scalar2=None, op0=OP.mult))
            v.drain()
            ctx.inc('dve', v.tensor_tensor(out=dw[:], in0=dw[:], in1=oh2[:], op=OP.add))
            dwtok = ctx.cnt['dve']
            ctx.wait(t, 'dve', dwtok)
            mm = t.transpose(ptr[0:8, 0:128], dw[:, 0:8], eye[0:128, 0:128])
            ctx.inc('pe', mm)
            ctx.wait(v, 'pe')
            ctx.inc('dve', v.tensor_copy(out=dwT[0:8, i * 128:(i + 1) * 128],
                                         in_=ptr[0:8, 0:128]))
        ctx.wait(q, 'dve')
        ctx.inc('qs', q.dma_start(out=d['DWT'][:, :], in_=dwT[:]), 16)


def p_moe_exp(ctx):
    nc = ctx.nc
    d = ctx.dram
    q, g, v, s, t = nc.sync, nc.gpsimd, nc.vector, nc.scalar, nc.tensor
    NM = MOE_F // 128
    with Scope(ctx):
        wg_ = ctx.sbuf('me_wg', [128, NF * MOE_F], BF16)
        wu_ = ctx.sbuf('me_wu', [128, NF * MOE_F], BF16)
        wd_ = ctx.sbuf('me_wd', [128, NM * DM], BF16)
        wg = [wg_, wg_]
        wu = [wu_, wu_]
        wd = [wd_, wd_]
        xch = [ctx.sbuf(f'me_x{i}', [128, NF * 512], BF16) for i in range(2)]
        xdw = ctx.sbuf('me_xdw', [128, NF * 512], BF16)
        dwT = ctx.sbuf('me_dwT', [8, S], F32)
        sel = ctx.sbuf('me_sel', [8, 8 * 128], F32)
        dwb = ctx.sbuf('me_dwb', [128, 512], BF16)
        sg = ctx.sbuf('me_sg', [128, 1024], BF16)
        gu = ctx.sbuf('me_gu', [128, NM * 512], BF16)
        mo = ctx.sbuf('me_mo', [128, DM], F32)
        pga = ctx.psum('me_pga', [128, 512], F32)
        pgb = ctx.psum('me_pgb', [128, 512], F32)
        pua = ctx.psum('me_pua', [128, 512], F32)
        pub = ctx.psum('me_pub', [128, 512], F32)
        pdw = ctx.psum('me_pdw', [128, 512], F32)
        pd1 = ctx.psum('me_pd1', [128, 512], F32)
        pd2 = ctx.psum('me_pd2', [128, 512], F32)
        pd3 = ctx.psum('me_pd3', [128, 512], F32)
        ctx.inc('qs', q.dma_start(out=dwT[:], in_=d['DWT'][:, :]), 16)
        ctx.inc('qs', q.dma_start(out=sel[:], in_=d['sel8'][:, :]), 16)
        wld = [0, 0]
        wrel = [0, 0]
        ld_tok = [0, 0]
        xrel = [0, 0]
        dwbtok = 0
        storetok = 0
        for e in range(MOE_E):
            eb = e % 2
            if e >= 1:
                ctx.wait(g, 'pe', wrel[(e - 1) % 2])
            for k in range(NF):
                ctx.inc('qg', g.dma_start(out=wg[eb][:, k * MOE_F:(k + 1) * MOE_F],
                                          in_=d['moe_wg'][e, k * 128:(k + 1) * 128, :]), 16)
                ctx.inc('qg', g.dma_start(out=wu[eb][:, k * MOE_F:(k + 1) * MOE_F],
                                          in_=d['moe_wu'][e, k * 128:(k + 1) * 128, :]), 16)
            for j in range(NM):
                ctx.inc('qg', g.dma_start(out=wd[eb][:, j * DM:(j + 1) * DM],
                                          in_=d['moe_wd'][e, j * 128:(j + 1) * 128, :]), 16)
            wld[eb] = ctx.cnt['qg']
            for c in range(8):
                cb = c % 2
                if e > 0 or c >= 2:
                    ctx.wait(q, 'pe', xrel[cb])
                for k in range(NF):
                    ctx.inc('qs', q.dma_start(out=xch[cb][:, k * 512:(k + 1) * 512],
                                              in_=d['XN1T'][k * 128:(k + 1) * 128,
                                                            1 + c * 512:1 + (c + 1) * 512]), 16)
                ld_tok[cb] = ctx.cnt['qs']
                ctx.wait(t, 'qs', ld_tok[cb])
                ctx.wait(t, 'dve')
                mm = t.matmul(pdw[:, :], sel[:, e * 128:(e + 1) * 128],
                              dwT[:, c * 512:(c + 1) * 512], start=True, stop=True)
                ctx.inc('pe', mm)
                ctx.wait(v, 'pe')
                ctx.inc('dve', v.tensor_copy(out=dwb[:], in_=pdw[:, :]))
                v.drain()
                for k in range(NF):
                    ctx.inc('dve', v.tensor_tensor(out=xdw[:, k * 512:(k + 1) * 512],
                                                   in0=xch[cb][:, k * 512:(k + 1) * 512],
                                                   in1=dwb[:], op=OP.mult))
                dwbtok = ctx.cnt['dve']
                if c == 0:
                    ctx.wait(t, 'qg', wld[eb])
                ctx.wait(t, 'dve', dwbtok)
                silu_tok = [0, 0]
                gu_tok = [0, 0]
                for m in range(NM):
                    bank_g = [pga, pgb][m % 2]
                    bank_u = [pua, pub][m % 2]
                    ctx.wait(t, 'act', silu_tok[m % 2])
                    for k in range(NF):
                        mm = t.matmul(bank_g[:, :], wg[eb][:, k * MOE_F + m * 128:k * MOE_F + (m + 1) * 128],
                                      xch[cb][:, k * 512:(k + 1) * 512],
                                      start=(k == 0), stop=(k == NF - 1))
                    ctx.inc('pe', mm)
                    ctx.wait(t, 'dve', gu_tok[m % 2])
                    for k in range(NF):
                        mm = t.matmul(bank_u[:, :], wu[eb][:, k * MOE_F + m * 128:k * MOE_F + (m + 1) * 128],
                                      xdw[:, k * 512:(k + 1) * 512],
                                      start=(k == 0), stop=(k == NF - 1))
                    ctx.inc('pe', mm)
                    ctx.wait(s, 'pe')
                    ctx.wait(s, 'dve', gu_tok[m % 2])
                    ctx.inc('act', s.activation(out=sg[:, (m % 2) * 512:(m % 2 + 1) * 512],
                                                in_=bank_g[:, :], func=AF.Silu))
                    silu_tok[m % 2] = ctx.cnt['act']
                    ctx.wait(v, 'act')
                    ctx.inc('dve', v.tensor_tensor(out=gu[:, m * 512:(m + 1) * 512],
                                                   in0=bank_u[:, :],
                                                   in1=sg[:, (m % 2) * 512:(m % 2 + 1) * 512],
                                                   op=OP.mult))
                    gu_tok[m % 2] = ctx.cnt['dve']
                xrel[cb] = ctx.cnt['pe']
                gutok = ctx.cnt['dve']
                ctx.wait(t, 'dve', gutok)
                evactok = [0, 0]
                for tsub in range(4):
                    pdx = [(pd1, pd2), (pd3, pdw)][tsub % 2]
                    ctx.wait(t, 'dve', evactok[tsub % 2])
                    for oh, bank in [(0, pdx[0]), (1, pdx[1])]:
                        for j in range(NM):
                            mm = t.matmul(bank[:, :],
                                          gu[:, j * 512 + tsub * 128:j * 512 + (tsub + 1) * 128],
                                          wd[eb][:, j * DM + oh * 512:j * DM + (oh + 1) * 512],
                                          start=(j == 0), stop=(j == NM - 1))
                        ctx.inc('pe', mm)
                    ctx.wait(v, 'pe')
                    ctx.wait(v, 'qg', storetok)
                    ctx.inc('dve', v.tensor_copy(out=mo[:, 0:512], in_=pdx[0][:, :]))
                    ctx.inc('dve', v.tensor_copy(out=mo[:, 512:1024], in_=pdx[1][:, :]))
                    evactok[tsub % 2] = ctx.cnt['dve']
                    ctx.wait(g, 'dve')
                    row = c * 512 + tsub * 128
                    if e == 0:
                        ctx.inc('qg', g.dma_start(out=d['MOEACC'][row:row + 128, :],
                                                  in_=mo[:]), 16)
                    else:
                        ctx.inc('qg', g.dma_start(out=d['MOEACC'][row:row + 128, :],
                                                  in_=mo[:], accum_op=OP.add), 16)
                    storetok = ctx.cnt['qg']
            wrel[eb] = ctx.cnt['pe']


def _phase_list():
    return [
        ('ln0', ['x', 'eyebf'], ['XNTOK', 'XNT', 'XNF32'], lambda c: p_ln0(c)),
        ('l0aA', ['XNT', 'l0a_Wk', 'l0a_Wv', 'l0a_bk', 'l0a_bv', 'onesbf'],
         ['KVT', 'KDIAG'], lambda c: p_l0a_A(c)),
        ('l0aB', ['XNT', 'XNTOK', 'KVT', 'KDIAG', 'l0a_Wq', 'l0a_Wd', 'l0a_bq_c',
                  'l0a_bd', 'l0a_gamma', 'l0a_beta', 'eye', 'onesbf', 'onesf', 'pairsel'],
         ['S0'], lambda c: p_l0a_B(c)),
        ('fqkv0', ['XNT', 'ff_wqkv', 'ff_qs', 'ff_ks', 'onesf', 'eyebf',
                   'rot_qc', 'rot_qs', 'rot_kc', 'rot_ks'],
         ['QHT', 'KHT', 'VH'], lambda c: p_ffn_qkv(c, 0, 'XNT')),
        ('fatt0', ['QHT', 'KHT', 'VH', 'mask_gen', 'mask_w0', 'eyebf'],
         ['OT'], lambda c: p_ffn_att(c, 0)),
        ('fwo0', ['OT', 'XNF32', 'ff_wo'], ['XIA'], lambda c: p_ffn_wo(c, 0, 'XNF32', True)),
        ('ftr0a', ['XIA', 'eyebf'], ['XITA'], lambda c: p_tr(c, 'XIA', 'XITA')),
        ('fffn0', ['XITA', 'XIA', 'ff_w1', 'ff_b1', 'ff_w2', 'ff_b2', 'ff_lng', 'ff_lnb'],
         ['XIB'], lambda c: p_ffn_ffn(c, 0, 'XIB')),
        ('ftr0b', ['XIB', 'eyebf'], ['XITB'], lambda c: p_tr(c, 'XIB', 'XITB')),
        ('fqkv1', ['XITB', 'ff_wqkv', 'ff_qs', 'ff_ks', 'onesf', 'eyebf',
                   'rot_qc', 'rot_qs', 'rot_kc', 'rot_ks'],
         ['QHT', 'KHT', 'VH'], lambda c: p_ffn_qkv(c, 1, 'XITB')),
        ('fatt1', ['QHT', 'KHT', 'VH', 'mask_gen', 'mask_w0', 'eyebf'],
         ['OT'], lambda c: p_ffn_att(c, 1)),
        ('fwo1', ['OT', 'XIB', 'ff_wo'], ['XIA'], lambda c: p_ffn_wo(c, 1, 'XIB', True)),
        ('ftr1a', ['XIA', 'eyebf'], ['XITA'], lambda c: p_tr(c, 'XIA', 'XITA')),
        ('fffn1', ['XITA', 'XIA', 'ff_w1', 'ff_b1', 'ff_w2', 'ff_b2', 'ff_lng', 'ff_lnb'],
         ['FF'], lambda c: p_ffn_ffn(c, 1, 'FF')),
        ('comb0', ['x', 'S0', 'FF', 'eyebf'],
         ['X1', 'XN1TOK', 'XN1T', 'XN1TLO'], lambda c: p_comb0(c)),
        ('sproj', ['XN1T', 'ssd_wint', 'ssd_dtb', 'ssd_alog'],
         ['XTOK', 'BTOK', 'BT', 'CT', 'ACT_s', 'RT_s'], lambda c: p_ssd_proj(c)),
        ('score', ['XTOK', 'BTOK', 'BT', 'CT', 'ACT_s', 'RT_s', 'ssd_mask', 'sel32',
                   'ssd_d', 'onesf', 'eye'], ['YTOK'], lambda c: p_ssd_core(c)),
        ('sout', ['YTOK', 'ssd_wout', 'eyebf'], ['S1'], lambda c: p_ssd_out(c)),
        ('mrt', ['XN1T', 'XN1TLO', 'moe_rt', 'eye'], ['DWT'], lambda c: p_moe_rt(c)),
        ('mexp', ['XN1T', 'DWT', 'sel8', 'moe_wg', 'moe_wu', 'moe_wd'],
         ['MOEACC'], lambda c: p_moe_exp(c)),
        ('final', ['X1', 'S1', 'MOEACC'], ['out'], lambda c: p_final(c)),
    ]


FUSED = True


def build_fused():
    ctx = Ctx(fused=True)
    nc = ctx.nc
    engines = [nc.tensor, nc.vector, nc.scalar, nc.gpsimd, nc.sync]
    for name, ins, outs, emit in _phase_list():
        emit(ctx)
        ctx.barrier(engines)
    finish(ctx)
    ctx.close()
    return ctx


def kernel(**inputs):
    consts = host_prep(inputs)
    x_all = np.ascontiguousarray(np.asarray(inputs['L0_attn_input_tensor'], np.float32))
    if FUSED:
        ctx = build_fused()
        in_maps = []
        for b in range(BATCH):
            m = dict(consts)
            m['x'] = x_all[b]
            in_maps.append(m)
        res = run_bass_kernel_spmd(ctx.nc, in_maps, core_ids=list(range(BATCH)))
        return np.stack([res.results[b]['out'] for b in range(BATCH)], axis=0)
    percore = []
    for b in range(BATCH):
        vals = dict(consts)
        vals['x'] = x_all[b]
        percore.append(vals)
    for name, ins, outs, emit in _phase_list():
        ctx = Ctx(fused=False, phase_inputs=ins, phase_outputs=outs)
        emit(ctx)
        finish(ctx)
        ctx.close()
        in_maps = [{n: percore[b][n] for n in ins} for b in range(BATCH)]
        res = run_bass_kernel_spmd(ctx.nc, in_maps, core_ids=list(range(BATCH)))
        for b in range(BATCH):
            for n in outs:
                percore[b][n] = res.results[b][n]
    return np.stack([percore[b]['out'] for b in range(BATCH)], axis=0)


# version 09
# speedup vs baseline: 1.0411x; 1.0081x over previous
import math
import numpy as np
import ml_dtypes
import concourse.bass as bass
import concourse.mybir as mybir
from concourse.bass_utils import run_bass_kernel_spmd
from concourse.alu_op_type import AluOpType as OP

F32 = mybir.dt.float32
BF16 = mybir.dt.bfloat16
AF = mybir.ActivationFunctionType
AX = mybir.AxisListType

S = 4096
DM = 1024
NT = S // 128
NF = DM // 128
SP = S + 1
BIG = 1.0e38

L0A_NH, L0A_HD = 16, 64
FF_HEADS, FF_HD, FF_WIN, FF_INNER = 8, 64, 512, 512
FF_HID = 2730
FF_HPAD = 2816
SSD_DS, SSD_HD, SSD_NH = 64, 32, 32
SSD_CS = 128
SSD_NC = S // SSD_CS
MOE_E, MOE_F = 8, 2048


def _rotary_tables():
    n, dim_head, window = S, FF_HD, FF_WIN
    half_idx = np.arange(0, dim_head, 2, dtype=np.float64)
    inv_freq = 1.0 / 10000.0 ** (half_idx / dim_head)
    t = np.arange(n, dtype=np.float64)
    freqs = t[:, None] * inv_freq[None, :]
    freqs = np.concatenate([freqs, freqs], axis=-1)
    base = (half_idx + 0.4 * dim_head) / (1.4 * dim_head)
    power = (t - n // 2) / float(window // 2)
    scale = base[None, :] ** power[:, None]
    scale = np.concatenate([scale, scale], axis=-1)
    cos, sin = np.cos(freqs), np.sin(freqs)
    qc = (cos * scale).astype(np.float32)
    qs = (sin * scale).astype(np.float32)
    kc = (cos / scale).astype(np.float32)
    ks = (sin / scale).astype(np.float32)
    def sgn(tab):
        out = tab.copy()
        out[:, :dim_head // 2] *= -1.0
        return out
    return qc, sgn(qs), kc, sgn(ks)


def _attn_masks():
    ii = np.arange(FF_WIN)[:, None]
    jj = np.arange(2 * FF_WIN)[None, :]
    valid = (jj >= ii) & (jj <= ii + FF_WIN)
    m_gen = np.where(valid, 0.0, -BIG).astype(np.float32)
    valid0 = valid & (jj >= FF_WIN)
    m_w0 = np.where(valid0, 0.0, -BIG).astype(np.float32)
    return m_gen, m_w0


def _ssd_mask():
    i = np.arange(SSD_CS)
    m = np.where(i[None, :] >= i[:, None], BIG, -BIG).astype(np.float32)
    return m


def _selectors():
    ps = np.zeros((16, 8, 128), np.float32)
    for j in range(8):
        ps[2 * j, j, 0:64] = 1.0
        ps[2 * j + 1, j, 64:128] = 1.0
    s32 = np.zeros((32, 32, 128), np.float32)
    for h in range(32):
        s32[h, h, :] = 1.0
    s8 = np.zeros((8, 8, 128), np.float32)
    for e in range(8):
        s8[e, e, :] = 1.0
    return (ps.reshape(16, 8 * 128), s32.reshape(32, 32 * 128),
            s8.reshape(8, 8 * 128))


def host_prep(I):
    d = {}
    f32 = lambda a: np.ascontiguousarray(a, dtype=np.float32)
    d['eye'] = np.eye(128, dtype=np.float32)
    d['eyebf'] = np.eye(128, dtype=np.float32).astype(ml_dtypes.bfloat16)
    d['onesbf'] = np.ones((128, 128), dtype=ml_dtypes.bfloat16)
    d['onesf'] = np.ones((2, 128), dtype=np.float32)
    qc, qs, kc, ks = _rotary_tables()
    d['rot_qc'], d['rot_qs'], d['rot_kc'], d['rot_ks'] = qc, qs, kc, ks
    mg, m0 = _attn_masks()
    d['mask_gen'] = mg.astype(ml_dtypes.bfloat16)
    d['mask_w0'] = m0.astype(ml_dtypes.bfloat16)
    d['ssd_mask'] = _ssd_mask()
    ps, s32, s8 = _selectors()
    d['pairsel'], d['sel32'], d['sel8'] = ps, s32, s8

    for w in ['Wq', 'Wk', 'Wv', 'Wd']:
        d['l0a_' + w] = f32(I['L0_attn_' + w].T)
    for b in ['bq', 'bk', 'bv', 'bd', 'gamma', 'beta']:
        d['l0a_' + b] = f32(I['L0_attn_' + b]).reshape(1, DM)
    d['l0a_bq_c'] = f32(np.asarray(I['L0_attn_bq'], np.float32).reshape(8, 128).T)
    d['ff_wqkv'] = f32(I['L0_ffn_wqkv'])
    d['ff_wo'] = f32(I['L0_ffn_wo'])
    w1 = np.asarray(I['L0_ffn_w1'], np.float32)
    b1 = np.asarray(I['L0_ffn_b1'], np.float32)
    w2 = np.asarray(I['L0_ffn_w2'], np.float32)
    w1p = np.zeros((2, DM, 2 * FF_HPAD), np.float32)
    w1p[:, :, :FF_HID] = w1[:, :, :FF_HID]
    w1p[:, :, FF_HPAD:FF_HPAD + FF_HID] = w1[:, :, FF_HID:]
    b1p = np.zeros((2, 2 * FF_HPAD), np.float32)
    b1p[:, :FF_HID] = b1[:, :FF_HID]
    b1p[:, FF_HPAD:FF_HPAD + FF_HID] = b1[:, FF_HID:]
    w2p = np.zeros((2, FF_HPAD, DM), np.float32)
    w2p[:, :FF_HID, :] = w2
    d['ff_w1'] = f32(w1p)
    d['ff_b1'] = f32(b1p.reshape(2, 1, 2 * FF_HPAD))
    d['ff_w2'] = f32(w2p)
    d['ff_b2'] = f32(np.asarray(I['L0_ffn_b2'], np.float32).reshape(2, 1, DM))
    d['ff_qs'] = f32(I['L0_ffn_q_scale']).reshape(2, 1, 64)
    d['ff_ks'] = f32(I['L0_ffn_k_scale']).reshape(2, 1, 64)
    d['ff_lng'] = f32(I['L0_ffn_ln_g']).reshape(2, 1, DM)
    d['ff_lnb'] = f32(I['L0_ffn_ln_b']).reshape(2, 1, DM)
    d['ssd_wint'] = f32(np.asarray(I['L1_attn_W_in'], np.float32).T)
    d['ssd_wout'] = f32(np.asarray(I['L1_attn_W_out'], np.float32).T)
    d['ssd_dtb'] = f32(I['L1_attn_dt_bias']).reshape(32, 1)
    d['ssd_alog'] = f32(I['L1_attn_A_log']).reshape(32, 1)
    d['ssd_d'] = f32(I['L1_attn_D']).reshape(1, 32)
    d['moe_rt'] = f32(np.asarray(I['L1_ffn_router_w'], np.float32).T)
    d['moe_wg'] = f32(np.transpose(np.asarray(I['L1_ffn_w_gate'], np.float32), (0, 2, 1)))
    d['moe_wu'] = f32(np.transpose(np.asarray(I['L1_ffn_w_up'], np.float32), (0, 2, 1)))
    d['moe_wd'] = f32(np.transpose(np.asarray(I['L1_ffn_w_down'], np.float32), (0, 2, 1)))
    return d


SCRATCH = {
    'XNT':   ([DM, SP], BF16),
    'XNTOK': ([S, DM], BF16),
    'XNF32': ([S, DM], F32),
    'S0':    ([S, DM], F32),
    'XIA':   ([S, DM], F32),
    'XIB':   ([S, DM], F32),
    'XITA':  ([DM, SP], BF16),
    'XITB':  ([DM, SP], BF16),
    'QHT':   ([FF_INNER, S], BF16),
    'KHT':   ([FF_INNER, S], BF16),
    'VH':    ([S, FF_INNER], BF16),
    'OT':    ([FF_INNER, S], BF16),
    'FF':    ([S, DM], F32),
    'X1':    ([S, DM], F32),
    'XN1T':  ([DM, SP], BF16),
    'XN1TLO': ([DM, SP], BF16),
    'XN1TOK': ([S, DM], BF16),
    'XTOK':  ([S, DM], BF16),
    'BT':    ([64, S], BF16),
    'CT':    ([64, S], BF16),
    'BTOK':  ([S, 64], BF16),
    'ACT_s': ([32, S], F32),
    'RT_s':  ([32, S], F32),
    'S1':    ([S, DM], F32),
    'MOEACC': ([S, DM], F32),
    'DWT':   ([8, S], F32),
    'YTOK':  ([S, DM], BF16),
    'KVT':   ([128, L0A_NH * 64], BF16),
    'KDIAG': ([128, 128], BF16),
}

CONSTS = {
    'eye': ([128, 128], F32), 'eyebf': ([128, 128], BF16),
    'onesbf': ([128, 128], BF16), 'onesf': ([2, 128], F32),
    'rot_qc': ([S, 64], F32), 'rot_qs': ([S, 64], F32),
    'rot_kc': ([S, 64], F32), 'rot_ks': ([S, 64], F32),
    'mask_gen': ([FF_WIN, 2 * FF_WIN], BF16), 'mask_w0': ([FF_WIN, 2 * FF_WIN], BF16),
    'ssd_mask': ([SSD_CS, SSD_CS], F32),
    'pairsel': ([16, 8 * 128], F32), 'sel32': ([32, 32 * 128], F32),
    'sel8': ([8, 8 * 128], F32),
    'l0a_Wq': ([DM, DM], F32), 'l0a_Wk': ([DM, DM], F32),
    'l0a_Wv': ([DM, DM], F32), 'l0a_Wd': ([DM, DM], F32),
    'l0a_bq': ([1, DM], F32), 'l0a_bk': ([1, DM], F32),
    'l0a_bv': ([1, DM], F32), 'l0a_bd': ([1, DM], F32),
    'l0a_gamma': ([1, DM], F32), 'l0a_beta': ([1, DM], F32),
    'l0a_bq_c': ([128, 8], F32),
    'ff_wqkv': ([2, DM, 1536], F32), 'ff_wo': ([2, FF_INNER, DM], F32),
    'ff_w1': ([2, DM, 2 * FF_HPAD], F32), 'ff_b1': ([2, 1, 2 * FF_HPAD], F32),
    'ff_w2': ([2, FF_HPAD, DM], F32), 'ff_b2': ([2, 1, DM], F32),
    'ff_qs': ([2, 1, 64], F32), 'ff_ks': ([2, 1, 64], F32),
    'ff_lng': ([2, 1, DM], F32), 'ff_lnb': ([2, 1, DM], F32),
    'ssd_wint': ([DM, 1184], F32), 'ssd_wout': ([DM, DM], F32),
    'ssd_dtb': ([32, 1], F32), 'ssd_alog': ([32, 1], F32), 'ssd_d': ([1, 32], F32),
    'moe_rt': ([DM, 8], F32),
    'moe_wg': ([8, DM, MOE_F], F32), 'moe_wu': ([8, DM, MOE_F], F32),
    'moe_wd': ([8, MOE_F, DM], F32),
}


class Ctx:

    def __init__(self, fused, phase_inputs=None, phase_outputs=None):
        self.nc = bass.Bass("TRN2", target_bir_lowering=False)
        self.fused = fused
        self.dram = {}
        self._stack = []
        nc = self.nc
        if fused:
            self.dram['x'] = nc.dram_tensor('x', [S, DM], F32, kind='ExternalInput').ap()
            self.dram['out'] = nc.dram_tensor('out', [S, DM], F32, kind='ExternalOutput').ap()
            for n, (shp, dt) in CONSTS.items():
                self.dram[n] = nc.dram_tensor(n, shp, dt, kind='ExternalInput').ap()
            for n, (shp, dt) in SCRATCH.items():
                self.dram[n] = nc.dram_tensor(n, shp, dt, kind='Internal').ap()
        else:
            ins = set(phase_inputs)
            outs = set(phase_outputs)
            for n in sorted(ins):
                shp, dt = self._spec(n)
                self.dram[n] = nc.dram_tensor(n, shp, dt, kind='ExternalInput').ap()
            for n in sorted(outs):
                shp, dt = self._spec(n)
                self.dram[n] = nc.dram_tensor(n, shp, dt, kind='ExternalOutput').ap()
        self.sems = {}
        self.cnt = {}
        for name in ['pe', 'dve', 'act', 'pool', 'qs', 'qg', 'qa', 'qv']:
            sem = nc.semaphore(f'sem_{name}')
            self.sems[name] = self._enter(sem)
            self.cnt[name] = 0

    def _spec(self, n):
        if n == 'x':
            return [S, DM], F32
        if n == 'out':
            return [S, DM], F32
        if n in CONSTS:
            return CONSTS[n]
        return SCRATCH[n]

    def _enter(self, cm):
        v = cm.__enter__()
        self._stack.append(cm)
        return v

    _uid = 0

    def sbuf(self, name, shape, dtype):
        Ctx._uid += 1
        return self._enter(self.nc.sbuf_tensor(f'{name}_u{Ctx._uid}', shape, dtype))

    def psum(self, name, shape, dtype=F32):
        Ctx._uid += 1
        return self._enter(self.nc.psum_tensor(f'{name}_u{Ctx._uid}', shape, dtype))

    def close(self):
        while self._stack:
            self._stack.pop().__exit__(None, None, None)

    def inc(self, name, instr, n=1):
        instr.then_inc(self.sems[name], n)
        self.cnt[name] += n
        return self.cnt[name]

    def wait(self, eng, name, val=None):
        if val is None:
            val = self.cnt[name]
        if val > 0:
            eng.wait_ge(self.sems[name], val)

    def barrier(self, engines):
        for e in engines:
            for name in self.sems:
                self.wait(e, name)


BATCH = 8


def to_bf16(a):
    return np.asarray(a).astype(ml_dtypes.bfloat16)


class Scope:
    def __init__(self, ctx):
        self.ctx = ctx
        self.mark = len(ctx._stack)

    def __enter__(self):
        return self

    def __exit__(self, *a):
        st = self.ctx._stack
        while len(st) > self.mark:
            st.pop().__exit__(None, None, None)


def p_ln0(ctx, src='x', dst_tok='XNTOK', dst_t='XNT', dst_f32='XNF32', eps=1e-5):
    nc = ctx.nc
    d = ctx.dram
    with Scope(ctx):
        xt = [ctx.sbuf(f'ln_x{i}', [128, DM], F32) for i in range(2)]
        sq = ctx.sbuf('ln_sq', [128, DM], F32)
        st = ctx.sbuf('ln_st', [128, 8], F32)
        xn = ctx.sbuf('ln_xn', [128, DM], BF16)
        xnf = ctx.sbuf('ln_xnf', [128, DM], F32)
        eyebf = ctx.sbuf('ln_eye', [128, 128], BF16)
        tacc = ctx.sbuf('ln_tacc', [128, NF * SP], BF16)
        tps = ctx.psum('ln_tps', [128, 1024], BF16)
        ld_tok = [0, 0]
        cons_tok = [0, 0]
        st_tok = 0
        tp_tok = 0
        q = nc.sync
        g = nc.gpsimd
        v = nc.vector
        s = nc.scalar
        t = nc.tensor
        ctx.inc('qs', q.dma_start(out=eyebf[:], in_=d['eyebf'][:, :]), 16)
        ctx.wait(v, 'qs')
        for k in range(NF):
            ctx.inc('dve', v.memset(tacc[:, k * SP:k * SP + 1], 0.0))
        for i in range(NT):
            b = i % 2
            if i >= 2:
                ctx.wait(q, 'dve', cons_tok[b])
            ctx.inc('qs', q.dma_start(out=xt[b][:], in_=d[src][i * 128:(i + 1) * 128, :]), 16)
            ld_tok[b] = ctx.cnt['qs']
            x = xt[b]
            ctx.wait(v, 'qs', ld_tok[b])
            ctx.inc('dve', v.reduce_sum(out=st[:, 0:1], in_=x[:], axis=AX.X))
            ctx.wait(s, 'dve')
            ctx.inc('act', s.activation(out=sq[:], in_=x[:], func=AF.Square))
            ctx.wait(v, 'act')
            ctx.inc('dve', v.reduce_sum(out=st[:, 1:2], in_=sq[:], axis=AX.X))
            v.drain()
            ctx.inc('dve', v.tensor_scalar(out=st[:, 2:3], in0=st[:, 0:1],
                                           scalar1=1.0 / DM, scalar2=None, op0=OP.mult))
            ctx.inc('dve', v.tensor_scalar(out=st[:, 3:4], in0=st[:, 1:2],
                                           scalar1=1.0 / DM, scalar2=None, op0=OP.mult))
            v.drain()
            ctx.inc('dve', v.tensor_tensor(out=st[:, 5:6], in0=st[:, 2:3],
                                           in1=st[:, 2:3], op=OP.mult))
            v.drain()
            ctx.inc('dve', v.tensor_tensor(out=st[:, 4:5], in0=st[:, 3:4],
                                           in1=st[:, 5:6], op=OP.subtract))
            v.drain()
            ctx.inc('dve', v.tensor_scalar(out=st[:, 4:5], in0=st[:, 4:5],
                                           scalar1=float(eps), scalar2=None, op0=OP.add))
            ctx.wait(s, 'dve')
            ctx.inc('act', s.activation(out=st[:, 6:7], in_=st[:, 4:5], func=AF.Sqrt))
            s.drain()
            ctx.wait(v, 'act')
            ctx.inc('dve', v.reciprocal(out=st[:, 6:7], in_=st[:, 6:7]))
            v.drain()
            ctx.inc('dve', v.tensor_tensor(out=st[:, 7:8], in0=st[:, 2:3],
                                           in1=st[:, 6:7], op=OP.mult))
            v.drain()
            ctx.inc('dve', v.tensor_scalar(out=st[:, 7:8], in0=st[:, 7:8],
                                           scalar1=-1.0, scalar2=None, op0=OP.mult))
            v.drain()
            ctx.wait(v, 'qs', st_tok)
            ctx.wait(v, 'pe', tp_tok)
            ctx.inc('dve', v.tensor_scalar(out=xnf[:], in0=x[:], scalar1=st[:, 6:7],
                                           scalar2=st[:, 7:8], op0=OP.mult, op1=OP.add))
            v.drain()
            ctx.inc('dve', v.tensor_copy(out=xn[:], in_=xnf[:]))
            cons_tok[b] = ctx.cnt['dve']
            ctx.wait(q, 'dve', cons_tok[b])
            ctx.inc('qs', q.dma_start(out=d[dst_tok][i * 128:(i + 1) * 128, :], in_=xn[:]), 16)
            if dst_f32 is not None:
                ctx.inc('qs', q.dma_start(out=d[dst_f32][i * 128:(i + 1) * 128, :], in_=xnf[:]), 16)
            st_tok = ctx.cnt['qs']
            ctx.wait(t, 'dve')
            for k in range(NF):
                mm = t.transpose(tps[:, k * 128:(k + 1) * 128], xn[:, k * 128:(k + 1) * 128],
                                 eyebf[:, 0:128])
                if k == NF - 1:
                    ctx.inc('pe', mm)
            tp_tok = ctx.cnt['pe']
            ctx.wait(v, 'pe', tp_tok)
            for k in range(NF):
                instr = v.tensor_copy(out=tacc[:, k * SP + 1 + i * 128: k * SP + 1 + (i + 1) * 128],
                                      in_=tps[:, k * 128:(k + 1) * 128])
                ctx.inc('dve', instr)
        ctx.wait(q, 'dve')
        for k in range(NF):
            ctx.inc('qs', q.dma_start(out=d[dst_t][k * 128:(k + 1) * 128, :],
                                      in_=tacc[:, k * SP:(k + 1) * SP]), 16)


def finish(ctx):
    ctx.barrier([ctx.nc.sync, ctx.nc.gpsimd])


PHASE_DEFS = {}


def run_phase_standalone(name, vals_list, inputs, outputs, emit, core_ids=None):
    ctx = Ctx(fused=False, phase_inputs=inputs, phase_outputs=outputs)
    emit(ctx)
    finish(ctx)
    ctx.close()
    in_maps = []
    for vals in vals_list:
        in_maps.append({n: vals[n] for n in inputs})
    if core_ids is None:
        core_ids = list(range(len(vals_list)))
    res = run_bass_kernel_spmd(ctx.nc, in_maps, core_ids=core_ids)
    return [{n: r[n] for n in outputs} for r in res.results]


def p_l0a_A(ctx):
    nc = ctx.nc
    d = ctx.dram
    q, g, v, s, t = nc.sync, nc.gpsimd, nc.vector, nc.scalar, nc.tensor
    with Scope(ctx):
        wk = ctx.sbuf('aA_wk', [128, NF * DM], BF16)
        wv = ctx.sbuf('aA_wv', [128, NF * DM], BF16)
        bk = ctx.sbuf('aA_bk', [1, DM], BF16)
        bv = ctx.sbuf('aA_bv', [1, DM], BF16)
        ones = ctx.sbuf('aA_ones', [128, 128], BF16)
        kt = ctx.sbuf('aA_kt', [128, NT * DM], BF16)
        vt = ctx.sbuf('aA_vt', [128, NT * DM], BF16)
        xc = [ctx.sbuf(f'aA_xc{i}', [128, NF * 128], BF16) for i in range(2)]
        t1 = ctx.sbuf('aA_t1', [128, 512], F32)
        t2 = ctx.sbuf('aA_t2', [128, 512], BF16)
        eb = ctx.sbuf('aA_eb', [128, 512], BF16)
        kvtb = ctx.sbuf('aA_kvtb', [128, L0A_NH * 64], BF16)
        ksf = ctx.sbuf('aA_ksf', [128, 8], F32)
        kdg = ctx.sbuf('aA_kdg', [128, 128], BF16)
        pk = ctx.psum('aA_pk', [128, 512], F32)
        pv = ctx.psum('aA_pv', [128, 512], F32)
        pA = ctx.psum('aA_pA', [128, 512], F32)
        pB = ctx.psum('aA_pB', [128, 512], F32)
        pks = ctx.psum('aA_pks', [128, 512], F32)
        kvbanks = [pk, pv, pA, pB]
        for k in range(NF):
            ctx.inc('qg', g.dma_start(out=wk[:, k * DM:(k + 1) * DM],
                                      in_=d['l0a_Wk'][k * 128:(k + 1) * 128, :]), 16)
            ctx.inc('qg', g.dma_start(out=wv[:, k * DM:(k + 1) * DM],
                                      in_=d['l0a_Wv'][k * 128:(k + 1) * 128, :]), 16)
        ctx.inc('qg', g.dma_start(out=bk[:, :], in_=d['l0a_bk'][:, :]), 16)
        ctx.inc('qg', g.dma_start(out=bv[:, :], in_=d['l0a_bv'][:, :]), 16)
        ctx.inc('qs', q.dma_start(out=ones[:], in_=d['onesbf'][:, :]), 16)
        wtok = ctx.cnt['qg']
        otok = ctx.cnt['qs']
        ld_tok = [0, 0]
        rel_tok = [0, 0]
        ps_rel = 0
        for i in range(NT):
            b = i % 2
            if i >= 2:
                ctx.wait(q, 'pe', rel_tok[b])
            for k in range(NF):
                ctx.inc('qs', q.dma_start(out=xc[b][:, k * 128:(k + 1) * 128],
                                          in_=d['XNT'][k * 128:(k + 1) * 128,
                                                       1 + i * 128:1 + (i + 1) * 128]), 16)
            ld_tok[b] = ctx.cnt['qs']
            ctx.wait(t, 'qs', ld_tok[b])
            if i == 0:
                ctx.wait(t, 'qg', wtok)
                ctx.wait(t, 'qs', otok)
            for half in range(2):
                ctx.wait(t, 'dve', ps_rel)
                for k in range(NF):
                    t.matmul(pk[:, :], xc[b][:, k * 128:(k + 1) * 128],
                             wk[:, k * DM + half * 512:k * DM + (half + 1) * 512],
                             start=(k == 0), stop=False)
                t.matmul(pk[:, :], ones[0:1, 0:128], bk[0:1, half * 512:(half + 1) * 512],
                         start=False, stop=True)
                for k in range(NF):
                    t.matmul(pv[:, :], xc[b][:, k * 128:(k + 1) * 128],
                             wv[:, k * DM + half * 512:k * DM + (half + 1) * 512],
                             start=(k == 0), stop=False)
                mm = t.matmul(pv[:, :], ones[0:1, 0:128], bv[0:1, half * 512:(half + 1) * 512],
                              start=False, stop=True)
                ctx.inc('pe', mm)
                if half == 1:
                    rel_tok[b] = ctx.cnt['pe']
                ctx.wait(v, 'pe')
                cbase = i * DM + half * 512
                ctx.wait(v, 'act')
                ctx.inc('dve', v.tensor_scalar(out=t1[:], in0=pk[:, :], scalar1=0.0,
                                               scalar2=None, op0=OP.min))
                ctx.wait(s, 'dve')
                ctx.inc('act', s.activation(out=eb[:], in_=t1[:], func=AF.Exp))
                ctx.inc('dve', v.tensor_scalar(out=t2[:], in0=pk[:, :], scalar1=0.0,
                                               scalar2=None, op0=OP.max))
                v.drain()
                ctx.wait(v, 'act')
                ctx.inc('dve', v.tensor_tensor(out=kt[:, cbase:cbase + 512],
                                               in0=eb[:], in1=t2[:], op=OP.add))
                ctx.inc('dve', v.tensor_copy(out=vt[:, cbase:cbase + 512], in_=pv[:, :]))
                ps_rel = ctx.cnt['dve']
        ctx.wait(t, 'dve')
        evac_tok = {}
        for h in range(L0A_NH):
            bank = kvbanks[h % 4]
            if h >= 4:
                ctx.wait(t, 'dve', evac_tok[h % 4])
            for i in range(NT):
                mm = t.matmul(bank[0:64, 0:64],
                              kt[:, i * DM + h * 64:i * DM + (h + 1) * 64],
                              vt[:, i * DM + h * 64:i * DM + (h + 1) * 64],
                              start=(i == 0), stop=(i == NT - 1))
            ctx.inc('pe', mm)
            ctx.wait(v, 'pe')
            ctx.inc('dve', v.tensor_copy(out=kvtb[0:64, h * 64:(h + 1) * 64],
                                         in_=bank[0:64, 0:64]))
            ctx.inc('dve', v.tensor_copy(out=kvtb[64:128, h * 64:(h + 1) * 64],
                                         in_=bank[0:64, 0:64]))
            evac_tok[h % 4] = ctx.cnt['dve']
        for kti in range(8):
            for i in range(NT):
                mm = t.matmul(pks[:, kti:kti + 1],
                              kt[:, i * DM + kti * 128:i * DM + (kti + 1) * 128],
                              ones[:, 0:1], start=(i == 0), stop=(i == NT - 1))
            ctx.inc('pe', mm)
        ctx.wait(v, 'pe')
        ctx.inc('dve', v.tensor_scalar(out=ksf[:], in0=pks[:, 0:8], scalar1=1e-6,
                                       scalar2=None, op0=OP.add))
        ctx.inc('dve', v.memset(kdg[:], 0.0))
        v.drain()
        for kti in range(8):
            for hh in range(2):
                h = 2 * kti + hh
                ctx.inc('dve', v.tensor_copy(out=kdg[hh * 64:(hh + 1) * 64,
                                                     kti * 16 + h:kti * 16 + h + 1],
                                             in_=ksf[hh * 64:(hh + 1) * 64, kti:kti + 1]))
        v.drain()
        ctx.wait(q, 'dve')
        ctx.inc('qs', q.dma_start(out=d['KVT'][:, :], in_=kvtb[:]), 16)
        ctx.inc('qs', q.dma_start(out=d['KDIAG'][:, :], in_=kdg[:]), 16)


def emit_lnstats(ctx, x_ap, st, sq, eps, width=DM):
    nc = ctx.nc
    v, s = nc.vector, nc.scalar
    ctx.inc('dve', v.reduce_sum(out=st[:, 0:1], in_=x_ap, axis=AX.X))
    ctx.wait(s, 'dve')
    ctx.inc('act', s.activation(out=sq[:, 0:width], in_=x_ap, func=AF.Square))
    ctx.wait(v, 'act')
    ctx.inc('dve', v.reduce_sum(out=st[:, 1:2], in_=sq[:, 0:width], axis=AX.X))
    v.drain()
    ctx.inc('dve', v.tensor_scalar(out=st[:, 2:3], in0=st[:, 0:1],
                                   scalar1=1.0 / width, scalar2=None, op0=OP.mult))
    ctx.inc('dve', v.tensor_scalar(out=st[:, 3:4], in0=st[:, 1:2],
                                   scalar1=1.0 / width, scalar2=None, op0=OP.mult))
    v.drain()
    ctx.inc('dve', v.tensor_tensor(out=st[:, 5:6], in0=st[:, 2:3],
                                   in1=st[:, 2:3], op=OP.mult))
    v.drain()
    ctx.inc('dve', v.tensor_tensor(out=st[:, 4:5], in0=st[:, 3:4],
                                   in1=st[:, 5:6], op=OP.subtract))
    v.drain()
    ctx.inc('dve', v.tensor_scalar(out=st[:, 4:5], in0=st[:, 4:5],
                                   scalar1=float(eps), scalar2=None, op0=OP.add))
    ctx.wait(s, 'dve')
    ctx.inc('act', s.activation(out=st[:, 6:7], in_=st[:, 4:5], func=AF.Sqrt))
    s.drain()
    ctx.wait(v, 'act')
    ctx.inc('dve', v.reciprocal(out=st[:, 6:7], in_=st[:, 6:7]))
    v.drain()
    ctx.inc('dve', v.tensor_tensor(out=st[:, 7:8], in0=st[:, 2:3],
                                   in1=st[:, 6:7], op=OP.mult))
    v.drain()
    ctx.inc('dve', v.tensor_scalar(out=st[:, 7:8], in0=st[:, 7:8],
                                   scalar1=-1.0, scalar2=None, op0=OP.mult))
    v.drain()


def cast_loads(ctx, dst_sb, src_name, nrow_tiles, cols, dst_stride=None):
    g = ctx.nc.gpsimd
    d = ctx.dram
    if dst_stride is None:
        dst_stride = cols
    for k in range(nrow_tiles):
        ctx.inc('qg', g.dma_start(out=dst_sb[:, k * dst_stride:k * dst_stride + cols],
                                  in_=d[src_name][k * 128:(k + 1) * 128, :]), 16)


def p_l0a_B(ctx):
    nc = ctx.nc
    d = ctx.dram
    q, g, v, s, t = nc.sync, nc.gpsimd, nc.vector, nc.scalar, nc.tensor
    with Scope(ctx):
        wq = ctx.sbuf('aB_wq', [128, NF * DM], BF16)
        wd = ctx.sbuf('aB_wd', [128, NF * DM], BF16)
        bqc = ctx.sbuf('aB_bqc', [128, 8], F32)
        bdr = ctx.sbuf('aB_bdr', [1, DM], BF16)
        kvt = ctx.sbuf('aB_kvt', [128, L0A_NH * 64], BF16)
        kdg = ctx.sbuf('aB_kdg', [128, 128], BF16)
        eye = ctx.sbuf('aB_eye', [128, 128], F32)
        ones = ctx.sbuf('aB_ones', [128, 128], BF16)
        onesf = ctx.sbuf('aB_onesf', [2, 128], F32)
        gam = ctx.sbuf('aB_gam', [1, DM], F32)
        bet = ctx.sbuf('aB_bet', [1, DM], F32)
        psel = ctx.sbuf('aB_psel', [16, 8 * 128], F32)
        gb = ctx.sbuf('aB_gb', [128, DM], F32)
        bb = ctx.sbuf('aB_bb', [128, DM], F32)
        xch = [ctx.sbuf(f'aB_xch{i}', [128, NF * 512], BF16) for i in range(2)]
        qh = ctx.sbuf('aB_qh', [128, NF * 512], BF16)
        qz = ctx.sbuf('aB_qz', [128, NF * 512], BF16)
        ctxT = ctx.sbuf('aB_ctxT', [128, NF * 512], BF16)
        t1 = ctx.sbuf('aB_t1', [128, 512], F32)
        t2 = ctx.sbuf('aB_t2', [128, 512], BF16)
        eb = ctx.sbuf('aB_eb', [128, 512], BF16)
        zt = ctx.sbuf('aB_zt', [128, 16], F32)
        ztr = ctx.sbuf('aB_ztr', [16, 512], F32)
        xtk = ctx.sbuf('aB_xtk', [128, DM], BF16)
        hs = ctx.sbuf('aB_hs', [128, DM], F32)
        hsn = ctx.sbuf('aB_hsn', [128, DM], F32)
        s0t = ctx.sbuf('aB_s0t', [128, DM], F32)
        sq = ctx.sbuf('aB_sq', [128, DM], F32)
        stt = ctx.sbuf('aB_stt', [128, 8], F32)
        pba = ctx.psum('aB_pa', [128, 512], F32)
        pbb = ctx.psum('aB_pb', [128, 512], F32)
        pbc = ctx.psum('aB_pc', [128, 512], F32)
        pbd = ctx.psum('aB_pd', [128, 512], F32)
        pbe = ctx.psum('aB_pe', [128, 512], F32)

        cast_loads(ctx, wq, 'l0a_Wq', NF, DM)
        cast_loads(ctx, wd, 'l0a_Wd', NF, DM)
        ctx.inc('qg', g.dma_start(out=bdr[:, :], in_=d['l0a_bd'][:, :]), 16)
        for nm_, sb_ in [('l0a_bq_c', bqc), ('KVT', kvt), ('KDIAG', kdg), ('eye', eye),
                         ('onesbf', ones), ('onesf', onesf), ('l0a_gamma', gam),
                         ('l0a_beta', bet), ('pairsel', psel)]:
            ctx.inc('qs', q.dma_start(out=sb_[:], in_=d[nm_][:, :]), 16)
        ctx.barrier([t])
        for oh in range(2):
            t.matmul(pba[:, :], onesf[0:1, 0:128], gam[0:1, oh * 512:(oh + 1) * 512],
                     start=True, stop=True)
            mm = t.matmul(pbb[:, :], onesf[0:1, 0:128], bet[0:1, oh * 512:(oh + 1) * 512],
                          start=True, stop=True)
            ctx.inc('pe', mm)
            ctx.wait(v, 'pe')
            ctx.inc('dve', v.tensor_copy(out=gb[:, oh * 512:(oh + 1) * 512], in_=pba[:, :]))
            ctx.inc('dve', v.tensor_copy(out=bb[:, oh * 512:(oh + 1) * 512], in_=pbb[:, :]))
            ctx.wait(t, 'dve')

        ld_tok = [0, 0]
        rel_tok = [0, 0]
        qrel = [0, 0]
        denrel = 0
        zbrel = 0
        ctxrel = [0, 0]
        wdrel = [0, 0]
        sttok = 0
        for c in range(8):
            b = c % 2
            if c >= 2:
                ctx.wait(q, 'pe', rel_tok[b])
            for k in range(NF):
                ctx.inc('qs', q.dma_start(out=xch[b][:, k * 512:(k + 1) * 512],
                                          in_=d['XNT'][k * 128:(k + 1) * 128,
                                                       1 + c * 512:1 + (c + 1) * 512]), 16)
            ld_tok[b] = ctx.cnt['qs']
            ctx.wait(t, 'qs', ld_tok[b])
            for o in range(NF):
                bank = [pba, pbb][o % 2]
                ctx.wait(t, 'dve', qrel[o % 2])
                for k in range(NF):
                    mm = t.matmul(bank[:, :], wq[:, k * DM + o * 128:k * DM + (o + 1) * 128],
                                  xch[b][:, k * 512:(k + 1) * 512],
                                  start=(k == 0), stop=(k == NF - 1))
                ctx.inc('pe', mm)
                if o == NF - 1:
                    rel_tok[b] = ctx.cnt['pe']
                ctx.wait(v, 'pe')
                ctx.wait(v, 'act')
                ctx.inc('dve', v.tensor_scalar(out=t1[:], in0=bank[:, :],
                                               scalar1=bqc[:, o:o + 1], scalar2=0.0,
                                               op0=OP.add, op1=OP.min))
                ctx.wait(s, 'dve')
                ctx.inc('act', s.activation(out=eb[:], in_=t1[:], func=AF.Exp))
                ctx.inc('dve', v.tensor_scalar(out=t2[:], in0=bank[:, :],
                                               scalar1=bqc[:, o:o + 1], scalar2=0.0,
                                               op0=OP.add, op1=OP.max))
                v.drain()
                ctx.wait(v, 'act')
                ctx.inc('dve', v.tensor_tensor(out=qh[:, o * 512:(o + 1) * 512],
                                               in0=eb[:], in1=t2[:], op=OP.add))
                qrel[o % 2] = ctx.cnt['dve']
            ctx.wait(t, 'dve')
            for j in range(4):
                ctx.wait(t, 'dve', denrel)
                for o in range(NF):
                    mm = t.matmul(pbc[:, 0:16],
                                  qh[:, o * 512 + j * 128:o * 512 + (j + 1) * 128],
                                  kdg[:, o * 16:(o + 1) * 16],
                                  start=(o == 0), stop=(o == NF - 1))
                ctx.inc('pe', mm)
                ctx.wait(v, 'pe')
                ctx.inc('dve', v.reciprocal(out=zt[:, 0:16], in_=pbc[:, 0:16]))
                denrel = ctx.cnt['dve']
                ctx.wait(t, 'dve', denrel)
                mm = t.transpose(pbd[0:16, j * 128:(j + 1) * 128], zt[:, 0:16],
                                 eye[:, 0:128])
                ctx.inc('pe', mm)
            ctx.wait(v, 'pe')
            ctx.inc('dve', v.tensor_copy(out=ztr[:, :], in_=pbd[0:16, 0:512]))
            ztok = ctx.cnt['dve']
            for jj in range(NF):
                ctx.wait(t, 'dve', max(zbrel, ztok))
                mm = t.matmul(pbe[:, :], psel[:, jj * 128:(jj + 1) * 128],
                              ztr[:, 0:512], start=True, stop=True)
                ctx.inc('pe', mm)
                ctx.wait(v, 'pe')
                ctx.inc('dve', v.tensor_tensor(out=qz[:, jj * 512:(jj + 1) * 512],
                                               in0=qh[:, jj * 512:(jj + 1) * 512],
                                               in1=pbe[:, :], op=OP.mult))
                zbrel = ctx.cnt['dve']
            ctx.wait(t, 'dve')
            for h in range(L0A_NH):
                bank = [pba, pbb][h % 2]
                ctx.wait(t, 'dve', ctxrel[h % 2])
                hb = (h % 2) * 64
                mm = t.matmul(bank[0:64, 0:512], kvt[hb:hb + 64, h * 64:(h + 1) * 64],
                              qz[hb:hb + 64, (h // 2) * 512:(h // 2 + 1) * 512],
                              start=True, stop=True)
                ctx.inc('pe', mm)
                ctx.wait(v, 'pe')
                ctx.inc('dve', v.tensor_copy(out=ctxT[(h % 2) * 64:(h % 2) * 64 + 64,
                                                      (h // 2) * 512:(h // 2 + 1) * 512],
                                             in_=bank[0:64, 0:512]))
                ctxrel[h % 2] = ctx.cnt['dve']
            ctx.wait(t, 'dve')
            for j in range(4):
                ctx.wait(q, 'dve')
                ctx.inc('qs', q.dma_start(out=xtk[:],
                                          in_=d['XNTOK'][c * 512 + j * 128:c * 512 + (j + 1) * 128, :]), 16)
                xload = ctx.cnt['qs']
                for oh in range(2):
                    bank = [pbc, pbd][oh]
                    ctx.wait(t, 'dve', wdrel[oh])
                    for m in range(NF):
                        t.matmul(bank[:, :], ctxT[:, m * 512 + j * 128:m * 512 + (j + 1) * 128],
                                 wd[:, m * DM + oh * 512:m * DM + (oh + 1) * 512],
                                 start=(m == 0), stop=False)
                    mm = t.matmul(bank[:, :], ones[0:1, 0:128],
                                  bdr[0:1, oh * 512:(oh + 1) * 512], start=False, stop=True)
                    ctx.inc('pe', mm)
                    ctx.wait(v, 'pe')
                    ctx.wait(v, 'qs', xload)
                    ctx.inc('dve', v.tensor_tensor(out=hs[:, oh * 512:(oh + 1) * 512],
                                                   in0=pbc[:, :] if oh == 0 else pbd[:, :],
                                                   in1=xtk[:, oh * 512:(oh + 1) * 512],
                                                   op=OP.add))
                    wdrel[oh] = ctx.cnt['dve']
                v.drain()
                emit_lnstats(ctx, hs[:, 0:DM], stt, sq, 1e-12)
                ctx.wait(v, 'qs', sttok)
                ctx.inc('dve', v.tensor_scalar(out=hsn[:], in0=hs[:], scalar1=stt[:, 6:7],
                                               scalar2=stt[:, 7:8], op0=OP.mult, op1=OP.add))
                v.drain()
                ctx.inc('dve', v.tensor_tensor(out=s0t[:], in0=hsn[:], in1=gb[:], op=OP.mult))
                v.drain()
                ctx.inc('dve', v.tensor_tensor(out=s0t[:], in0=s0t[:], in1=bb[:], op=OP.add))
                ctx.wait(q, 'dve')
                ctx.inc('qs', q.dma_start(out=d['S0'][c * 512 + j * 128:c * 512 + (j + 1) * 128, :],
                                          in_=s0t[:]), 16)
                sttok = ctx.cnt['qs']


def p_tr(ctx, src, dst):
    nc = ctx.nc
    d = ctx.dram
    q, g, v, t = nc.sync, nc.gpsimd, nc.vector, nc.tensor
    with Scope(ctx):
        xb = [ctx.sbuf(f'tr_x{i}', [128, DM], BF16) for i in range(2)]
        eyebf = ctx.sbuf('tr_eye', [128, 128], BF16)
        tacc = ctx.sbuf('tr_tacc', [128, NF * SP], BF16)
        tps = ctx.psum('tr_tps', [128, 1024], BF16)
        ctx.inc('qs', q.dma_start(out=eyebf[:], in_=d['eyebf'][:, :]), 16)
        ctx.wait(v, 'qs')
        for k in range(NF):
            ctx.inc('dve', v.memset(tacc[:, k * SP:k * SP + 1], 0.0))
        ld_tok = [0, 0]
        rel_tok = [0, 0]
        for i in range(NT):
            b = i % 2
            if i >= 2:
                ctx.wait(g, 'pe', rel_tok[b])
            ctx.inc('qg', g.dma_start(out=xb[b][:], in_=d[src][i * 128:(i + 1) * 128, :]), 16)
            ld_tok[b] = ctx.cnt['qg']
            ctx.wait(t, 'qg', ld_tok[b])
            ctx.wait(t, 'dve')
            for k in range(NF):
                mm = t.transpose(tps[:, k * 128:(k + 1) * 128],
                                 xb[b][:, k * 128:(k + 1) * 128], eyebf[:, 0:128])
            ctx.inc('pe', mm)
            rel_tok[b] = ctx.cnt['pe']
            ctx.wait(v, 'pe', rel_tok[b])
            for k in range(NF):
                ctx.inc('dve', v.tensor_copy(out=tacc[:, k * SP + 1 + i * 128:k * SP + 1 + (i + 1) * 128],
                                             in_=tps[:, k * 128:(k + 1) * 128]))
        ctx.wait(q, 'dve')
        for k in range(NF):
            ctx.inc('qs', q.dma_start(out=d[dst][k * 128:(k + 1) * 128, :],
                                      in_=tacc[:, k * SP:(k + 1) * SP]), 16)


def p_ffn_qkv(ctx, l, src_t):
    nc = ctx.nc
    d = ctx.dram
    q, g, v, s, t = nc.sync, nc.gpsimd, nc.vector, nc.scalar, nc.tensor
    with Scope(ctx):
        wq = ctx.sbuf('fq_w', [128, NF * 1536], BF16)
        hch = [ctx.sbuf(f'fq_h{i}', [128, NF * 128], BF16) for i in range(2)]
        taq = ctx.sbuf('fq_taq', [128, NT * 64], F32)
        tsq = ctx.sbuf('fq_tsq', [128, NT * 64], F32)
        tak = ctx.sbuf('fq_tak', [128, NT * 64], F32)
        tsk = ctx.sbuf('fq_tsk', [128, NT * 64], F32)
        rot = ctx.sbuf('fq_rot', [128, 64], F32)
        qsb = ctx.sbuf('fq_qsb', [128, 128], F32)
        qsw = ctx.sbuf('fq_qsw', [128, 128], F32)
        qsr = ctx.sbuf('fq_qsr', [1, 128], F32)
        onesf = ctx.sbuf('fq_onesf', [2, 128], F32)
        eyebf = ctx.sbuf('fq_eye', [128, 128], BF16)
        sq = ctx.sbuf('fq_sq', [128, 512], F32)
        sq2 = ctx.sbuf('fq_sq2', [128, 512], F32)
        nsq = ctx.sbuf('fq_nsq', [128, 16], F32)
        q0 = ctx.sbuf('fq_q0', [128, 512], F32)
        qa = ctx.sbuf('fq_qa', [128, 512], F32)
        qr = ctx.sbuf('fq_qr', [128, 512], F32)
        qf = ctx.sbuf('fq_qf', [128, 512], BF16)
        vh = ctx.sbuf('fq_vh', [128, 512], BF16)
        qht = ctx.sbuf('fq_qht', [128, 4 * S], BF16)
        kht = ctx.sbuf('fq_kht', [128, 4 * S], BF16)
        pq = ctx.psum('fq_pq', [128, 512], F32)
        pk = ctx.psum('fq_pk', [128, 512], F32)
        pv = ctx.psum('fq_pv', [128, 512], F32)
        ptr = ctx.psum('fq_ptr', [128, 512], BF16)
        for k in range(NF):
            ctx.inc('qg', g.dma_start(out=wq[:, k * 1536:(k + 1) * 1536],
                                      in_=d['ff_wqkv'][l, k * 128:(k + 1) * 128, :]), 16)
        ctx.inc('qs', q.dma_start(out=qsr[0:1, 0:64], in_=d['ff_qs'][l, :, :]), 16)
        ctx.inc('qs', q.dma_start(out=qsr[0:1, 64:128], in_=d['ff_ks'][l, :, :]), 16)
        ctx.inc('qs', q.dma_start(out=onesf[:], in_=d['onesf'][:, :]), 16)
        ctx.inc('qs', q.dma_start(out=eyebf[:], in_=d['eyebf'][:, :]), 16)
        ctx.barrier([t])
        t.matmul(pq[:, 0:64], onesf[0:1, 0:128], qsr[0:1, 0:64], start=True, stop=True)
        mm = t.matmul(pq[:, 64:128], onesf[0:1, 0:128], qsr[0:1, 64:128], start=True, stop=True)
        ctx.inc('pe', mm)
        ctx.wait(v, 'pe')
        ctx.inc('dve', v.tensor_copy(out=qsb[:, :], in_=pq[:, 0:128]))
        v.drain()
        ctx.inc('dve', v.tensor_scalar(out=qsb[:, 0:64], in0=qsb[:, 0:64],
                                       scalar1=float(8.0), scalar2=None, op0=OP.mult))
        v.drain()
        for base in (0, 64):
            ctx.inc('dve', v.tensor_copy(out=qsw[:, base:base + 32],
                                         in_=qsb[:, base + 32:base + 64]))
            ctx.inc('dve', v.tensor_copy(out=qsw[:, base + 32:base + 64],
                                         in_=qsb[:, base:base + 32]))
        v.drain()
        for i in range(NT):
            for (nm_, dst_, scb, sc_) in [('rot_qc', taq, qsb, 0), ('rot_qs', tsq, qsw, 0),
                                          ('rot_kc', tak, qsb, 64), ('rot_ks', tsk, qsw, 64)]:
                ctx.wait(q, 'dve')
                ctx.inc('qs', q.dma_start(out=rot[:, :],
                                          in_=d[nm_][i * 128:(i + 1) * 128, :]), 16)
                ctx.wait(v, 'qs')
                ctx.inc('dve', v.tensor_tensor(out=dst_[:, i * 64:(i + 1) * 64],
                                               in0=rot[:, :],
                                               in1=scb[:, sc_:sc_ + 64], op=OP.mult))
        qkvrel = [0, 0]
        ld_tok = [0, 0]
        tp_tok = 0
        vh_tok = [0]
        for i in range(NT):
            b = i % 2
            if i >= 2:
                ctx.wait(q, 'pe', qkvrel[b])
            for k in range(NF):
                off = 1 if k < 4 else 0
                ctx.inc('qs', q.dma_start(out=hch[b][:, k * 128:(k + 1) * 128],
                                          in_=d[src_t][k * 128:(k + 1) * 128,
                                                       off + i * 128:off + (i + 1) * 128]), 16)
            ld_tok[b] = ctx.cnt['qs']
            ctx.wait(t, 'qs', ld_tok[b])
            ctx.wait(t, 'dve')
            for p, bank in [(0, pq), (1, pk), (2, pv)]:
                for k in range(NF):
                    mm = t.matmul(bank[:, :], hch[b][:, k * 128:(k + 1) * 128],
                                  wq[:, k * 1536 + p * 512:k * 1536 + (p + 1) * 512],
                                  start=(k == 0), stop=(k == NF - 1))
            ctx.inc('pe', mm)
            qkvrel[b] = ctx.cnt['pe']
            ctx.wait(v, 'pe')
            ctx.wait(v, 'qs', vh_tok[0])
            ctx.inc('dve', v.tensor_copy(out=vh[:], in_=pv[:, :]))
            ctx.wait(q, 'dve')
            ctx.inc('qs', q.dma_start(out=d['VH'][i * 128:(i + 1) * 128, :], in_=vh[:]), 16)
            vh_tok[0] = ctx.cnt['qs']
            ctx.wait(s, 'pe')
            ctx.wait(s, 'dve')
            ctx.inc('act', s.activation(out=sq[:], in_=pq[:, :], func=AF.Square))
            ctx.inc('act', s.activation(out=sq2[:], in_=pk[:, :], func=AF.Square))
            s.drain()
            ctx.wait(v, 'act')
            ctx.inc('dve', v.tensor_reduce(out=nsq[:, 0:8].unsqueeze(2),
                                           in_=sq[:].rearrange('p (h e) -> p h e', h=8),
                                           axis=AX.X, op=OP.add))
            ctx.inc('dve', v.tensor_reduce(out=nsq[:, 8:16].unsqueeze(2),
                                           in_=sq2[:].rearrange('p (h e) -> p h e', h=8),
                                           axis=AX.X, op=OP.add))
            v.drain()
            ctx.wait(s, 'dve')
            ctx.inc('act', s.activation(out=nsq[:, 0:16], in_=nsq[:, 0:16], func=AF.Sqrt))
            s.drain()
            ctx.wait(v, 'act')
            ctx.inc('dve', v.tensor_scalar(out=nsq[:, 0:16], in0=nsq[:, 0:16],
                                           scalar1=1e-12, scalar2=None, op0=OP.max))
            v.drain()
            ctx.inc('dve', v.reciprocal(out=nsq[:, 0:16], in_=nsq[:, 0:16]))
            v.drain()
            for (bank, tc, tss, scoff, acc) in [(pq, taq, tsq, 0, qht), (pk, tak, tsk, 8, kht)]:
                rn3 = nsq[:, scoff:scoff + 8].unsqueeze(2).broadcast_to([128, 8, 64])
                ctx.inc('dve', v.tensor_tensor(out=q0[:].rearrange('p (h e) -> p h e', h=8),
                                               in0=bank[:, :].rearrange('p (h e) -> p h e', h=8),
                                               in1=rn3, op=OP.mult))
                v.drain()
                dvtok = ctx.cnt['dve']
                tc3 = tc[:, i * 64:(i + 1) * 64].unsqueeze(1).broadcast_to([128, 8, 64])
                ctx.wait(g, 'dve', dvtok)
                ctx.inc('pool', g.tensor_tensor(out=qa[:].rearrange('p (h e) -> p h e', h=8),
                                                in0=q0[:].rearrange('p (h e) -> p h e', h=8),
                                                in1=tc3, op=OP.mult))
                tlo = tss[:, i * 64:i * 64 + 32].unsqueeze(1).broadcast_to([128, 8, 32])
                thi = tss[:, i * 64 + 32:(i + 1) * 64].unsqueeze(1).broadcast_to([128, 8, 32])
                q03 = q0[:].rearrange('p (h e) -> p h e', h=8)
                qr3 = qr[:].rearrange('p (h e) -> p h e', h=8)
                ctx.inc('pool', g.tensor_tensor(out=qr3[:, :, 0:32], in0=q03[:, :, 32:64],
                                                in1=tlo, op=OP.mult))
                ctx.inc('pool', g.tensor_tensor(out=qr3[:, :, 32:64], in0=q03[:, :, 0:32],
                                                in1=thi, op=OP.mult))
                g.drain()
                ctx.wait(g, 'pe', tp_tok)
                ctx.inc('pool', g.tensor_tensor(out=qf[:], in0=qa[:], in1=qr[:], op=OP.add))
                ptok = ctx.cnt['pool']
                ctx.wait(t, 'pool', ptok)
                for db in range(4):
                    mm = t.transpose(ptr[:, db * 128:(db + 1) * 128],
                                     qf[:, db * 128:(db + 1) * 128], eyebf[:, 0:128])
                ctx.inc('pe', mm)
                tp_tok = ctx.cnt['pe']
                ctx.wait(v, 'pe', tp_tok)
                for db in range(4):
                    ctx.inc('dve', v.tensor_copy(out=acc[:, db * S + i * 128:db * S + (i + 1) * 128],
                                                 in_=ptr[:, db * 128:(db + 1) * 128]))
        ctx.wait(q, 'dve')
        for db in range(4):
            ctx.inc('qs', q.dma_start(out=d['QHT'][db * 128:(db + 1) * 128, :],
                                      in_=qht[:, db * S:(db + 1) * S]), 16)
            ctx.inc('qs', q.dma_start(out=d['KHT'][db * 128:(db + 1) * 128, :],
                                      in_=kht[:, db * S:(db + 1) * S]), 16)


def p_ffn_att(ctx, l):
    nc = ctx.nc
    d = ctx.dram
    q, g, v, s, t = nc.sync, nc.gpsimd, nc.vector, nc.scalar, nc.tensor
    with Scope(ctx):
        qht = ctx.sbuf('fa_qht', [128, 4 * S], BF16)
        kht = ctx.sbuf('fa_kht', [128, 4 * S], BF16)
        vhs = ctx.sbuf('fa_vh', [128, NT * FF_INNER], BF16)
        otacc = ctx.sbuf('fa_ot', [128, 4 * S], BF16)
        mgen = ctx.sbuf('fa_mgen', [128, 4 * 640], BF16)
        mw0 = ctx.sbuf('fa_mw0', [128, 4 * 512], BF16)
        eyebf = ctx.sbuf('fa_eye', [128, 128], BF16)
        mcol = [ctx.sbuf(f'fa_mc{i}', [128, 4], F32) for i in range(2)]
        ebuf = [ctx.sbuf(f'fa_e{i}', [128, 640], BF16) for i in range(2)]
        att = [ctx.sbuf(f'fa_at{i}', [128, 640], BF16) for i in range(2)]
        atT = [ctx.sbuf(f'fa_atT{i}', [128, 5 * 128], BF16) for i in range(2)]
        ovec = [ctx.sbuf(f'fa_ov{i}', [64, 512], BF16) for i in range(2)]
        psim = [ctx.psum(f'fa_psim{i}', [128, 1024], F32) for i in range(2)]
        ptr = [ctx.psum(f'fa_ptr{i}', [128, 640], BF16) for i in range(2)]
        po = [ctx.psum(f'fa_po{i}', [64, 512], F32) for i in range(2)]
        for db in range(4):
            ctx.inc('qs', q.dma_start(out=qht[:, db * S:(db + 1) * S],
                                      in_=d['QHT'][db * 128:(db + 1) * 128, :]), 16)
            ctx.inc('qs', q.dma_start(out=kht[:, db * S:(db + 1) * S],
                                      in_=d['KHT'][db * 128:(db + 1) * 128, :]), 16)
        for i in range(NT):
            ctx.inc('qg', g.dma_start(out=vhs[:, i * FF_INNER:(i + 1) * FF_INNER],
                                      in_=d['VH'][i * 128:(i + 1) * 128, :]), 16)
        for isub in range(4):
            ctx.inc('qs', q.dma_start(out=mgen[:, isub * 640:(isub + 1) * 640],
                                      in_=d['mask_gen'][isub * 128:(isub + 1) * 128,
                                                        isub * 128:isub * 128 + 640]), 16)
            ctx.inc('qs', q.dma_start(out=mw0[:, isub * 512:(isub + 1) * 512],
                                      in_=d['mask_w0'][isub * 128:(isub + 1) * 128,
                                                       512:1024]), 16)
        ctx.inc('qs', q.dma_start(out=eyebf[:], in_=d['eyebf'][:, :]), 16)
        ctx.barrier([t, v, s, g])
        units = []
        for h in range(FF_HEADS):
            for w in range(8):
                for isub in range(4):
                    units.append((h, w, isub))
        NU = len(units)

        def geom(u):
            h, w, isub = units[u]
            hb = (h % 2) * 64
            db = h // 2
            if w == 0:
                W = (isub + 1) * 128
                kcol0 = 0
                mask_ap = mw0[:, isub * 512:isub * 512 + W]
            else:
                W = 640
                kcol0 = (w - 1) * FF_WIN + isub * 128
                mask_ap = mgen[:, isub * 640:isub * 640 + W]
            return h, w, isub, hb, db, W, kcol0, mask_ap

        smtok = [0, 0]
        simtok = [0, 0]
        expdone = [0, 0]
        atttok = [0, 0]
        trtok = [0, 0]
        evactok = [0, 0]
        otok = [0, 0]
        ostok = [0, 0]
        for u in range(NU + 2):
            par = u % 2
            if u < NU:
                h, w, isub, hb, db, W, kcol0, mask_ap = geom(u)
                ctx.wait(t, 'act', expdone[par])
                ctx.wait(t, 'dve', smtok[par])
                lhs = qht[hb:hb + 64, db * S + w * FF_WIN + isub * 128:
                          db * S + w * FF_WIN + (isub + 1) * 128]
                done = 0
                while done < W:
                    nn = min(512, W - done)
                    t.matmul(psim[par][:, done:done + nn], lhs,
                             kht[hb:hb + 64, db * S + kcol0 + done:
                                 db * S + kcol0 + done + nn],
                             start=True, stop=False)
                    mm = t.matmul(psim[par][:, done:done + nn], eyebf[:, 0:128],
                                  mask_ap[:, done:done + nn], start=False, stop=True)
                    done += nn
                ctx.inc('pe', mm)
                simtok[par] = ctx.cnt['pe']
                ctx.wait(v, 'pe', simtok[par])
                ctx.wait(v, 'act', expdone[par])
                ctx.inc('dve', v.reduce_max(out=mcol[par][:, 0:1], in_=psim[par][:, 0:W],
                                            axis=AX.X, negate=True))
                v.drain()
                smtok[par] = ctx.cnt['dve']
                ctx.wait(s, 'dve', smtok[par])
                ctx.wait(s, 'dve', atttok[par])
                ctx.inc('act', s.activation(out=ebuf[par][:, 0:W], in_=psim[par][:, 0:W],
                                            func=AF.Exp, bias=mcol[par][:, 0:1], scale=1.0,
                                            accum_out=mcol[par][:, 2:3]))
                s.drain()
                expdone[par] = ctx.cnt['act']
            if 1 <= u < NU + 1:
                u1 = u - 1
                p1 = u1 % 2
                h, w, isub, hb, db, W, kcol0, mask_ap = geom(u1)
                ctx.wait(v, 'act', expdone[p1])
                ctx.inc('dve', v.reciprocal(out=mcol[p1][:, 3:4], in_=mcol[p1][:, 2:3]))
                v.drain()
                ctx.wait(v, 'pe', trtok[p1])
                ctx.inc('dve', v.tensor_scalar(out=att[p1][:, 0:W], in0=ebuf[p1][:, 0:W],
                                               scalar1=mcol[p1][:, 3:4], scalar2=None,
                                               op0=OP.mult))
                atttok[p1] = ctx.cnt['dve']
                ctx.wait(t, 'dve', atttok[p1])
                ctx.wait(t, 'act', evactok[p1])
                nb = W // 128
                for bbk in range(nb):
                    mm = t.transpose(ptr[p1][:, bbk * 128:(bbk + 1) * 128],
                                     att[p1][:, bbk * 128:(bbk + 1) * 128], eyebf[:, 0:128])
                ctx.inc('pe', mm)
                trtok[p1] = ctx.cnt['pe']
            if u >= 2:
                u2 = u - 2
                p2 = u2 % 2
                h, w, isub, hb, db, W, kcol0, mask_ap = geom(u2)
                hwpar = (h * 8 + w) % 2
                nb = W // 128
                ctx.wait(s, 'pe', trtok[p2])
                ctx.wait(s, 'pe', otok[p2])
                ctx.inc('act', s.copy(out=atT[p2][:, 0:nb * 128], in_=ptr[p2][:, 0:nb * 128]))
                s.drain()
                evactok[p2] = ctx.cnt['act']
                ctx.wait(t, 'act', evactok[p2])
                if isub == 0:
                    ctx.wait(t, 'qs', ostok[hwpar])
                for bi in range(nb):
                    ktok = kcol0 + bi * 128
                    vtile = ktok // 128
                    mm = t.matmul(po[hwpar][0:64, isub * 128:(isub + 1) * 128],
                                  vhs[:, vtile * FF_INNER + h * 64:
                                      vtile * FF_INNER + (h + 1) * 64],
                                  atT[p2][:, bi * 128:(bi + 1) * 128],
                                  start=(bi == 0), stop=(bi == nb - 1))
                ctx.inc('pe', mm)
                otok[p2] = ctx.cnt['pe']
                if isub == 3:
                    ctx.wait(v, 'pe', otok[p2])
                    ctx.wait(v, 'qs', ostok[hwpar])
                    ctx.inc('dve', v.tensor_copy(out=ovec[hwpar][:, :], in_=po[hwpar][0:64, 0:512]))
                    ctx.wait(q, 'dve')
                    ctx.inc('qs', q.dma_start(
                        out=d['OT'].rearrange('(a p) s -> a p s', p=128)[db, hb:hb + 64,
                            w * FF_WIN:(w + 1) * FF_WIN],
                        in_=ovec[hwpar][:, :]), 16)
                    ostok[hwpar] = ctx.cnt['qs']


def p_ffn_wo(ctx, l, res_src, res_f32):
    nc = ctx.nc
    d = ctx.dram
    q, g, v, s, t = nc.sync, nc.gpsimd, nc.vector, nc.scalar, nc.tensor
    with Scope(ctx):
        wo = ctx.sbuf('fw_wo', [128, 4 * DM], BF16)
        och = [ctx.sbuf(f'fw_o{i}', [128, 4 * 128], BF16) for i in range(2)]
        res = ctx.sbuf('fw_res', [128, DM], F32 if res_f32 else BF16)
        xo = ctx.sbuf('fw_xo', [128, DM], F32)
        p1 = ctx.psum('fw_p1', [128, 512], F32)
        p2 = ctx.psum('fw_p2', [128, 512], F32)
        for k in range(4):
            ctx.inc('qg', g.dma_start(out=wo[:, k * DM:(k + 1) * DM],
                                      in_=d['ff_wo'][l, k * 128:(k + 1) * 128, :]), 16)
        ctx.barrier([t])
        ld_tok = [0, 0]
        rel_tok = [0, 0]
        st_tok = 0
        for i in range(NT):
            b = i % 2
            if i >= 2:
                ctx.wait(q, 'pe', rel_tok[b])
            for k in range(4):
                ctx.inc('qs', q.dma_start(out=och[b][:, k * 128:(k + 1) * 128],
                                          in_=d['OT'][k * 128:(k + 1) * 128,
                                                      i * 128:(i + 1) * 128]), 16)
            ctx.wait(q, 'dve')
            ctx.inc('qs', q.dma_start(out=res[:], in_=d[res_src][i * 128:(i + 1) * 128, :]), 16)
            ld_tok[b] = ctx.cnt['qs']
            ctx.wait(t, 'qs', ld_tok[b])
            ctx.wait(t, 'dve')
            for oh, bank in [(0, p1), (1, p2)]:
                for k in range(4):
                    mm = t.matmul(bank[:, :], och[b][:, k * 128:(k + 1) * 128],
                                  wo[:, k * DM + oh * 512:k * DM + (oh + 1) * 512],
                                  start=(k == 0), stop=(k == 3))
                ctx.inc('pe', mm)
            rel_tok[b] = ctx.cnt['pe']
            ctx.wait(v, 'pe')
            ctx.wait(v, 'qs', st_tok)
            for oh, bank in [(0, p1), (1, p2)]:
                ctx.inc('dve', v.tensor_tensor(out=xo[:, oh * 512:(oh + 1) * 512],
                                               in0=bank[:, :],
                                               in1=res[:, oh * 512:(oh + 1) * 512], op=OP.add))
            ctx.wait(q, 'dve')
            ctx.inc('qs', q.dma_start(out=d['XIA'][i * 128:(i + 1) * 128, :], in_=xo[:]), 16)
            st_tok = ctx.cnt['qs']


def p_ffn_ffn(ctx, l, dst):
    nc = ctx.nc
    d = ctx.dram
    q, g, v, s, t = nc.sync, nc.gpsimd, nc.vector, nc.scalar, nc.tensor
    NJ = FF_HPAD // 128
    with Scope(ctx):
        hch0 = ctx.sbuf('fn_h0', [128, NF * 512], BF16)
        hch = [hch0, hch0]
        lnh = ctx.sbuf('fn_lnh', [128, NF * 512], BF16)
        lnh1 = ctx.sbuf('fn_lnh1', [128, 512], BF16)
        w1s = ctx.sbuf('fn_w1s', [128, NF * 2 * FF_HPAD], BF16)
        w2b = ctx.sbuf('fn_w2b', [128, NJ * DM], BF16)
        hsb = ctx.sbuf('fn_hsb', [128, NJ * 512], BF16)
        ggb = ctx.sbuf('fn_ggb', [128, 1024], BF16)
        b1r = ctx.sbuf('fn_b1r', [1, 2 * FF_HPAD], BF16)
        b2r = ctx.sbuf('fn_b2r', [1, DM], BF16)
        gamr = ctx.sbuf('fn_gamr', [1, DM], F32)
        betr = ctx.sbuf('fn_betr', [1, DM], F32)
        onesb = ctx.sbuf('fn_onesb', [128, 512], BF16)
        onesf = ctx.sbuf('fn_onesf', [1, 512], F32)
        srow = ctx.sbuf('fn_srow', [1, 4 * 512], F32)
        xi = ctx.sbuf('fn_xi', [128, 512], F32)
        pls = [ctx.psum(f'fn_p{i}', [128, 512], F32) for i in range(8)]
        pS, pS2, pAb, pBb, pa, pg_, pw1, pw2 = pls
        W1W = 2 * FF_HPAD
        for k in range(NF):
            ctx.inc('qg', g.dma_start(out=w1s[:, k * W1W:(k + 1) * W1W],
                                      in_=d['ff_w1'][l, k * 128:(k + 1) * 128, :]), 16)
        for j in range(NJ):
            ctx.inc('qg', g.dma_start(out=w2b[:, j * DM:(j + 1) * DM],
                                      in_=d['ff_w2'][l, j * 128:(j + 1) * 128, :]), 16)
        ctx.inc('qg', g.dma_start(out=b1r[:, :], in_=d['ff_b1'][l, :, :]), 16)
        ctx.inc('qg', g.dma_start(out=b2r[:, :], in_=d['ff_b2'][l, :, :]), 16)
        ctx.inc('qg', g.memset(onesb[:], 1.0), 1)
        ctx.inc('qg', g.memset(onesf[:], 1.0), 1)
        ctx.inc('qs', q.dma_start(out=gamr[:, :], in_=d['ff_lng'][l, :, :]), 16)
        ctx.inc('qs', q.dma_start(out=betr[:, :], in_=d['ff_lnb'][l, :, :]), 16)
        ctx.barrier([t])
        ld_tok = [0, 0]
        hrel = [0, 0]
        st_tok = 0
        w1ld = [0, 0]
        w1rel = [0, 0]
        for c in range(8):
            b = c % 2
            if c >= 2:
                ctx.wait(q, 'dve', hrel[b])
            for k in range(NF):
                off = 1 if k < 4 else 0
                ctx.inc('qs', q.dma_start(out=hch[b][:, k * 512:(k + 1) * 512],
                                          in_=d['XITA'][k * 128:(k + 1) * 128,
                                                        off + c * 512:off + (c + 1) * 512]), 16)
            ld_tok[b] = ctx.cnt['qs']
            ctx.wait(t, 'qs', ld_tok[b])
            ctx.wait(t, 'dve')
            ctx.wait(t, 'act')
            for k in range(NF):
                mm = t.matmul(pS[0:1, :], onesb[:, 0:1], hch[b][:, k * 512:(k + 1) * 512],
                              start=(k == 0), stop=(k == NF - 1))
            ctx.inc('pe', mm)
            sqtok = 0
            sqb = ggb[:, 0:512]
            for k in range(NF):
                ctx.wait(s, 'qs', ld_tok[b])
                if k == 0:
                    ctx.wait(s, 'dve')
                if k >= 1:
                    ctx.wait(s, 'pe', sqtok)
                ctx.inc('act', s.activation(out=sqb, in_=hch[b][:, k * 512:(k + 1) * 512],
                                            func=AF.Square))
                ctx.wait(t, 'act')
                mm = t.matmul(pS2[0:1, :], onesb[:, 0:1], sqb,
                              start=(k == 0), stop=(k == NF - 1))
                ctx.inc('pe', mm)
                sqtok = ctx.cnt['pe']
            ctx.wait(v, 'pe')
            ctx.inc('dve', v.tensor_scalar(out=srow[:, 0:512], in0=pS[0:1, 0:512],
                                           scalar1=1.0 / DM, scalar2=None, op0=OP.mult))
            ctx.inc('dve', v.tensor_scalar(out=srow[:, 512:1024], in0=pS2[0:1, 0:512],
                                           scalar1=1.0 / DM, scalar2=None, op0=OP.mult))
            v.drain()
            ctx.inc('dve', v.tensor_tensor(out=srow[:, 1024:1536], in0=srow[:, 0:512],
                                           in1=srow[:, 0:512], op=OP.mult))
            v.drain()
            ctx.inc('dve', v.tensor_tensor(out=srow[:, 1536:2048], in0=srow[:, 512:1024],
                                           in1=srow[:, 1024:1536], op=OP.subtract))
            v.drain()
            ctx.inc('dve', v.tensor_scalar(out=srow[:, 1536:2048], in0=srow[:, 1536:2048],
                                           scalar1=1e-5, scalar2=None, op0=OP.add))
            ctx.wait(s, 'dve')
            ctx.inc('act', s.activation(out=srow[:, 512:1024], in_=srow[:, 1536:2048],
                                        func=AF.Sqrt))
            s.drain()
            ctx.wait(v, 'act')
            ctx.inc('dve', v.reciprocal(out=srow[:, 512:1024], in_=srow[:, 512:1024]))
            v.drain()
            ctx.inc('dve', v.tensor_tensor(out=srow[:, 1024:1536], in0=srow[:, 0:512],
                                           in1=srow[:, 512:1024], op=OP.mult))
            v.drain()
            ctx.inc('dve', v.tensor_scalar(out=srow[:, 1024:1536], in0=srow[:, 1024:1536],
                                           scalar1=-1.0, scalar2=None, op0=OP.mult))
            v.drain()
            for kf in range(NF):
                ctx.wait(t, 'dve')
                t.matmul(pAb[:, :], gamr[0:1, kf * 128:(kf + 1) * 128],
                         srow[0:1, 512:1024], start=True, stop=True)
                t.matmul(pBb[:, :], gamr[0:1, kf * 128:(kf + 1) * 128],
                         srow[0:1, 1024:1536], start=True, stop=False)
                mm = t.matmul(pBb[:, :], betr[0:1, kf * 128:(kf + 1) * 128],
                              onesf[0:1, 0:512], start=False, stop=True)
                ctx.inc('pe', mm)
                ctx.wait(v, 'pe')
                ctx.wait(v, 'qs', st_tok)
                ctx.inc('dve', v.tensor_tensor(out=lnh1[:], in0=hch[b][:, kf * 512:(kf + 1) * 512],
                                               in1=pAb[:, :], op=OP.mult))
                v.drain()
                ctx.inc('dve', v.tensor_tensor(out=lnh[:, kf * 512:(kf + 1) * 512],
                                               in0=lnh1[:], in1=pBb[:, :], op=OP.add))
            hrel[b] = ctx.cnt['dve']
            if c == 0:
                ctx.wait(t, 'qg')
            ctx.wait(t, 'dve')
            jtt = [0, 0]
            jgg = [0, 0]
            for j in range(NJ):
                pa_, pg2 = [(pS, pS2), (pAb, pBb)][j % 2]
                ctx.wait(t, 'dve', jtt[j % 2])
                ctx.wait(t, 'act', jgg[j % 2])
                for k in range(NF):
                    t.matmul(pa_[:, :], w1s[:, k * W1W + j * 128:k * W1W + (j + 1) * 128],
                             lnh[:, k * 512:(k + 1) * 512], start=(k == 0), stop=False)
                t.matmul(pa_[:, :], b1r[0:1, j * 128:(j + 1) * 128], onesb[0:1, 0:512],
                         start=False, stop=True)
                for k in range(NF):
                    t.matmul(pg2[:, :], w1s[:, k * W1W + FF_HPAD + j * 128:k * W1W + FF_HPAD + (j + 1) * 128],
                             lnh[:, k * 512:(k + 1) * 512], start=(k == 0), stop=False)
                mm = t.matmul(pg2[:, :], b1r[0:1, FF_HPAD + j * 128:FF_HPAD + (j + 1) * 128],
                              onesb[0:1, 0:512], start=False, stop=True)
                ctx.inc('pe', mm)
                ctx.wait(s, 'pe')
                ctx.wait(s, 'dve', jtt[j % 2])
                ctx.inc('act', s.activation(out=ggb[:, (j % 2) * 512:(j % 2 + 1) * 512],
                                            in_=pg2[:, :], func=AF.Gelu))
                jgg[j % 2] = ctx.cnt['act']
                ctx.wait(v, 'act')
                ctx.inc('dve', v.tensor_tensor(out=hsb[:, j * 512:(j + 1) * 512],
                                               in0=pa_[:, :],
                                               in1=ggb[:, (j % 2) * 512:(j % 2 + 1) * 512],
                                               op=OP.mult))
                jtt[j % 2] = ctx.cnt['dve']
            for tsub in range(4):
                ctx.wait(t, 'dve')
                for oh, bank in [(0, pw1), (1, pw2)]:
                    for j in range(NJ):
                        t.matmul(bank[:, :], hsb[:, j * 512 + tsub * 128:j * 512 + (tsub + 1) * 128],
                                 w2b[:, j * DM + oh * 512:j * DM + (oh + 1) * 512],
                                 start=(j == 0), stop=False)
                    mm = t.matmul(bank[:, :], onesb[0:1, 0:128],
                                  b2r[0:1, oh * 512:(oh + 1) * 512], start=False, stop=True)
                    ctx.inc('pe', mm)
                ctx.wait(v, 'pe')
                for oh, bank in [(0, pw1), (1, pw2)]:
                    ctx.wait(q, 'dve')
                    ctx.inc('qs', q.dma_start(
                        out=xi[:, 0:512],
                        in_=d['XIA'][c * 512 + tsub * 128:c * 512 + (tsub + 1) * 128,
                                     oh * 512:(oh + 1) * 512]), 16)
                    xitok = ctx.cnt['qs']
                    ctx.wait(v, 'qs', xitok)
                    ctx.wait(v, 'qs', st_tok)
                    ov = lnh[:, 0:1024].bitcast(F32)
                    ctx.inc('dve', v.tensor_tensor(out=ov,
                                                   in0=bank[:, :],
                                                   in1=xi[:, 0:512], op=OP.add))
                    ctx.wait(q, 'dve')
                    ctx.inc('qs', q.dma_start(
                        out=d[dst][c * 512 + tsub * 128:c * 512 + (tsub + 1) * 128,
                                   oh * 512:(oh + 1) * 512],
                        in_=ov), 16)
                    st_tok = ctx.cnt['qs']


def p_comb0(ctx):
    nc = ctx.nc
    d = ctx.dram
    q, g, v, s, t = nc.sync, nc.gpsimd, nc.vector, nc.scalar, nc.tensor
    with Scope(ctx):
        xt = [ctx.sbuf(f'cb_x{i}', [128, DM], F32) for i in range(2)]
        st_ = [ctx.sbuf(f'cb_s{i}', [128, DM], F32) for i in range(2)]
        gt = [ctx.sbuf(f'cb_g{i}', [128, DM], F32) for i in range(2)]
        sg = ctx.sbuf('cb_sg', [128, DM], F32)
        x1 = ctx.sbuf('cb_xr', [128, DM], F32)
        sq = ctx.sbuf('cb_sq', [128, DM], F32)
        stt = ctx.sbuf('cb_st', [128, 8], F32)
        xnf = ctx.sbuf('cb_xnf', [128, DM], F32)
        xn = ctx.sbuf('cb_xn', [128, DM], BF16)
        xnlo = ctx.sbuf('cb_xnlo', [128, DM], BF16)
        eyebf = ctx.sbuf('cb_eye', [128, 128], BF16)
        tacc = ctx.sbuf('cb_tacc', [128, NF * SP], BF16)
        tacc2 = ctx.sbuf('cb_tacc2', [128, NF * SP], BF16)
        tps = ctx.psum('cb_tps', [128, 1024], BF16)
        tps2 = ctx.psum('cb_tps2', [128, 1024], BF16)
        ctx.inc('qs', q.dma_start(out=eyebf[:], in_=d['eyebf'][:, :]), 16)
        ctx.wait(v, 'qs')
        for k in range(NF):
            ctx.inc('dve', v.memset(tacc[:, k * SP:k * SP + 1], 0.0))
            ctx.inc('dve', v.memset(tacc2[:, k * SP:k * SP + 1], 0.0))
        ld_tok = [0, 0]
        cons_tok = [0, 0]
        st_tok = 0
        tp_tok = 0
        for i in range(NT):
            b = i % 2
            if i >= 2:
                ctx.wait(q, 'dve', cons_tok[b])
                ctx.wait(g, 'dve', cons_tok[b])
            ctx.inc('qs', q.dma_start(out=xt[b][:], in_=d['x'][i * 128:(i + 1) * 128, :]), 16)
            ctx.inc('qs', q.dma_start(out=st_[b][:], in_=d['S0'][i * 128:(i + 1) * 128, :]), 16)
            ctx.inc('qg', g.dma_start(out=gt[b][:], in_=d['FF'][i * 128:(i + 1) * 128, :]), 16)
            ld_tok[b] = (ctx.cnt['qs'], ctx.cnt['qg'])
            ctx.wait(s, 'qs', ld_tok[b][0])
            ctx.wait(s, 'qg', ld_tok[b][1])
            ctx.wait(s, 'dve')
            ctx.inc('act', s.activation(out=gt[b][:], in_=gt[b][:], func=AF.Silu))
            s.drain()
            ctx.wait(v, 'act')
            ctx.wait(v, 'qs', ld_tok[b][0])
            ctx.inc('dve', v.tensor_tensor(out=sg[:], in0=st_[b][:], in1=gt[b][:], op=OP.mult))
            v.drain()
            ctx.wait(v, 'qs', st_tok)
            ctx.inc('dve', v.tensor_tensor(out=x1[:], in0=xt[b][:], in1=sg[:], op=OP.add))
            cons_tok[b] = ctx.cnt['dve']
            v.drain()
            ctx.wait(q, 'dve', cons_tok[b])
            ctx.inc('qs', q.dma_start(out=d['X1'][i * 128:(i + 1) * 128, :], in_=x1[:]), 16)
            emit_lnstats(ctx, x1[:, 0:DM], stt, sq, 1e-5)
            ctx.wait(v, 'pe', tp_tok)
            ctx.inc('dve', v.tensor_scalar(out=xnf[:], in0=x1[:], scalar1=stt[:, 6:7],
                                           scalar2=stt[:, 7:8], op0=OP.mult, op1=OP.add))
            v.drain()
            ctx.inc('dve', v.tensor_copy(out=xn[:], in_=xnf[:]))
            v.drain()
            ctx.inc('dve', v.tensor_tensor(out=xnlo[:], in0=xnf[:], in1=xn[:], op=OP.subtract))
            v.drain()
            ctx.wait(q, 'dve')
            ctx.inc('qs', q.dma_start(out=d['XN1TOK'][i * 128:(i + 1) * 128, :], in_=xn[:]), 16)
            st_tok = ctx.cnt['qs']
            ctx.wait(t, 'dve')
            for k in range(NF):
                t.transpose(tps[:, k * 128:(k + 1) * 128], xn[:, k * 128:(k + 1) * 128],
                            eyebf[:, 0:128])
                mm = t.transpose(tps2[:, k * 128:(k + 1) * 128], xnlo[:, k * 128:(k + 1) * 128],
                                 eyebf[:, 0:128])
            ctx.inc('pe', mm)
            tp_tok = ctx.cnt['pe']
            ctx.wait(v, 'pe', tp_tok)
            for k in range(NF):
                ctx.inc('dve', v.tensor_copy(out=tacc[:, k * SP + 1 + i * 128:k * SP + 1 + (i + 1) * 128],
                                             in_=tps[:, k * 128:(k + 1) * 128]))
                ctx.inc('dve', v.tensor_copy(out=tacc2[:, k * SP + 1 + i * 128:k * SP + 1 + (i + 1) * 128],
                                             in_=tps2[:, k * 128:(k + 1) * 128]))
        ctx.wait(q, 'dve')
        for k in range(NF):
            ctx.inc('qs', q.dma_start(out=d['XN1T'][k * 128:(k + 1) * 128, :],
                                      in_=tacc[:, k * SP:(k + 1) * SP]), 16)
            ctx.inc('qs', q.dma_start(out=d['XN1TLO'][k * 128:(k + 1) * 128, :],
                                      in_=tacc2[:, k * SP:(k + 1) * SP]), 16)


def p_final(ctx):
    nc = ctx.nc
    d = ctx.dram
    q, g, v, s, t = nc.sync, nc.gpsimd, nc.vector, nc.scalar, nc.tensor
    with Scope(ctx):
        xt = [ctx.sbuf(f'fi_x{i}', [128, DM], F32) for i in range(2)]
        st_ = [ctx.sbuf(f'fi_s{i}', [128, DM], F32) for i in range(2)]
        gt = [ctx.sbuf(f'fi_g{i}', [128, DM], F32) for i in range(2)]
        sg = ctx.sbuf('fi_sg', [128, DM], F32)
        x1 = ctx.sbuf('fi_xr', [128, DM], F32)
        ld_tok = [0, 0]
        cons_tok = [0, 0]
        st_tok = 0
        for i in range(NT):
            b = i % 2
            if i >= 2:
                ctx.wait(q, 'dve', cons_tok[b])
                ctx.wait(g, 'dve', cons_tok[b])
            ctx.inc('qs', q.dma_start(out=xt[b][:], in_=d['X1'][i * 128:(i + 1) * 128, :]), 16)
            ctx.inc('qs', q.dma_start(out=st_[b][:], in_=d['S1'][i * 128:(i + 1) * 128, :]), 16)
            ctx.inc('qg', g.dma_start(out=gt[b][:], in_=d['MOEACC'][i * 128:(i + 1) * 128, :]), 16)
            ld_tok[b] = (ctx.cnt['qs'], ctx.cnt['qg'])
            ctx.wait(s, 'qs', ld_tok[b][0])
            ctx.wait(s, 'qg', ld_tok[b][1])
            ctx.wait(s, 'dve')
            ctx.inc('act', s.activation(out=gt[b][:], in_=gt[b][:], func=AF.Silu))
            s.drain()
            ctx.wait(v, 'act')
            ctx.wait(v, 'qs', ld_tok[b][0])
            ctx.inc('dve', v.tensor_tensor(out=sg[:], in0=st_[b][:], in1=gt[b][:], op=OP.mult))
            v.drain()
            ctx.wait(v, 'qs', st_tok)
            ctx.inc('dve', v.tensor_tensor(out=x1[:], in0=xt[b][:], in1=sg[:], op=OP.add))
            cons_tok[b] = ctx.cnt['dve']
            ctx.wait(q, 'dve', cons_tok[b])
            ctx.inc('qs', q.dma_start(out=d['out'][i * 128:(i + 1) * 128, :], in_=x1[:]), 16)
            st_tok = ctx.cnt['qs']


def p_ssd_proj(ctx):
    nc = ctx.nc
    d = ctx.dram
    q, g, v, s, t = nc.sync, nc.gpsimd, nc.vector, nc.scalar, nc.tensor
    with Scope(ctx):
        wint = ctx.sbuf('sp_w', [128, NF * 1184], BF16)
        xch = [ctx.sbuf(f'sp_x{i}', [128, NF * 512], BF16) for i in range(2)]
        btok = ctx.sbuf('sp_btok', [128, 64], BF16)
        bcsb = ctx.sbuf('sp_bcsb', [128, 512], BF16)
        dtr = ctx.sbuf('sp_dtr', [32, S], F32)
        dtb = ctx.sbuf('sp_dtb', [32, 1], F32)
        alog = ctx.sbuf('sp_alog', [32, 2], F32)
        zrow = ctx.sbuf('sp_z', [32, S], F32)
        tmp = ctx.sbuf('sp_tmp', [32, S], F32)
        dt_ = ctx.sbuf('sp_dt', [32, S], F32)
        at_ = ctx.sbuf('sp_at', [32, S], F32)
        act_ = ctx.sbuf('sp_act', [32, S], F32)
        rt_ = ctx.sbuf('sp_rt', [32, S], F32)
        onec = ctx.sbuf('sp_one', [32, 1], F32)
        xo = ctx.sbuf('sp_xo', [128, DM], BF16)
        pxa = ctx.psum('sp_pa', [128, 512], F32)
        pxb = ctx.psum('sp_pb', [128, 512], F32)
        pxc = ctx.psum('sp_pc', [128, 512], F32)
        for k in range(NF):
            ctx.inc('qg', g.dma_start(out=wint[:, k * 1184:(k + 1) * 1184],
                                      in_=d['ssd_wint'][k * 128:(k + 1) * 128, :]), 16)
        ctx.inc('qs', q.dma_start(out=dtb[:], in_=d['ssd_dtb'][:, :]), 16)
        ctx.inc('qs', q.dma_start(out=alog[:, 0:1], in_=d['ssd_alog'][:, :]), 16)
        ctx.inc('qg', g.memset(onec[:], 1.0), 1)
        ctx.barrier([t])
        ld_tok = [0, 0]
        rel_tok = [0, 0]
        evtok = 0
        xstok = 0
        for i in range(NT):
            b = i % 2
            if i >= 2:
                ctx.wait(q, 'pe', rel_tok[b])
            for k in range(NF):
                ctx.inc('qs', q.dma_start(out=xch[b][:, k * 512:k * 512 + 128],
                                          in_=d['XN1T'][k * 128:(k + 1) * 128,
                                                        1 + i * 128:1 + (i + 1) * 128]), 16)
            ld_tok[b] = ctx.cnt['qs']
            ctx.wait(t, 'qs', ld_tok[b])
            ctx.wait(t, 'dve', evtok)
            for oh, bank in [(0, pxa), (1, pxb)]:
                for k in range(NF):
                    mm = t.matmul(bank[:, :], xch[b][:, k * 512:k * 512 + 128],
                                  wint[:, k * 1184 + oh * 512:k * 1184 + (oh + 1) * 512],
                                  start=(k == 0), stop=(k == NF - 1))
                ctx.inc('pe', mm)
            for k in range(NF):
                mm = t.matmul(pxc[:, 0:64], xch[b][:, k * 512:k * 512 + 128],
                              wint[:, k * 1184 + 1024:k * 1184 + 1088],
                              start=(k == 0), stop=(k == NF - 1))
            ctx.inc('pe', mm)
            rel_tok[b] = ctx.cnt['pe']
            ctx.wait(v, 'pe')
            ctx.wait(v, 'qs', xstok)
            ctx.inc('dve', v.tensor_copy(out=xo[:, 0:512], in_=pxa[:, :]))
            ctx.inc('dve', v.tensor_copy(out=xo[:, 512:1024], in_=pxb[:, :]))
            ctx.inc('dve', v.tensor_copy(out=btok[:, :], in_=pxc[:, 0:64]))
            evtok = ctx.cnt['dve']
            ctx.wait(q, 'dve', evtok)
            ctx.inc('qs', q.dma_start(out=d['XTOK'][i * 128:(i + 1) * 128, :], in_=xo[:]), 16)
            ctx.inc('qs', q.dma_start(out=d['BTOK'][i * 128:(i + 1) * 128, :], in_=btok[:]), 16)
            xstok = ctx.cnt['qs']
        bcrel = 0
        bcst = 0
        for c in range(8):
            b = c % 2
            ctx.wait(q, 'pe', rel_tok[b])
            for k in range(NF):
                ctx.inc('qs', q.dma_start(out=xch[b][:, k * 512:(k + 1) * 512],
                                          in_=d['XN1T'][k * 128:(k + 1) * 128,
                                                        1 + c * 512:1 + (c + 1) * 512]), 16)
            ld_tok[b] = ctx.cnt['qs']
            ctx.wait(t, 'qs', ld_tok[b])
            ctx.wait(t, 'dve', bcrel)
            for k in range(NF):
                mm = t.matmul(pxa[:, :], wint[:, k * 1184 + 1024:k * 1184 + 1152],
                              xch[b][:, k * 512:(k + 1) * 512],
                              start=(k == 0), stop=(k == NF - 1))
            ctx.inc('pe', mm)
            for k in range(NF):
                mm = t.matmul(pxb[0:32, :], wint[:, k * 1184 + 1152:k * 1184 + 1184],
                              xch[b][:, k * 512:(k + 1) * 512],
                              start=(k == 0), stop=(k == NF - 1))
            ctx.inc('pe', mm)
            rel_tok[b] = ctx.cnt['pe']
            ctx.wait(v, 'pe')
            ctx.wait(v, 'qs', bcst)
            ctx.inc('dve', v.tensor_copy(out=bcsb[:, :], in_=pxa[:, :]))
            ctx.inc('dve', v.tensor_copy(out=dtr[0:32, c * 512:(c + 1) * 512],
                                         in_=pxb[0:32, :]))
            bcrel = ctx.cnt['dve']
            ctx.wait(q, 'dve', bcrel)
            ctx.inc('qs', q.dma_start(out=d['BT'][:, c * 512:(c + 1) * 512],
                                      in_=bcsb[0:64, :]), 16)
            ctx.inc('qs', q.dma_start(out=d['CT'][:, c * 512:(c + 1) * 512],
                                      in_=bcsb[64:128, :]), 16)
            bcst = ctx.cnt['qs']
        v.drain()
        ctx.inc('dve', v.tensor_scalar(out=zrow[:], in0=dtr[:], scalar1=dtb[:, 0:1],
                                       scalar2=None, op0=OP.add))
        v.drain()
        ctx.wait(s, 'dve')
        ctx.inc('act', s.activation(out=tmp[:], in_=zrow[:], func=AF.Abs))
        s.drain()
        ctx.inc('act', s.activation(out=tmp[:], in_=tmp[:], func=AF.Exp, scale=-1.0))
        s.drain()
        ctx.inc('act', s.activation(out=tmp[:], in_=tmp[:], func=AF.Ln, bias=1.0))
        s.drain()
        ctx.wait(v, 'act')
        ctx.inc('dve', v.tensor_scalar(out=dt_[:], in0=zrow[:], scalar1=0.0,
                                       scalar2=None, op0=OP.max))
        v.drain()
        ctx.inc('dve', v.tensor_tensor(out=dt_[:], in0=dt_[:], in1=tmp[:], op=OP.add))
        v.drain()
        ctx.wait(s, 'dve')
        ctx.inc('act', s.activation(out=alog[:, 1:2], in_=alog[:, 0:1], func=AF.Exp))
        s.drain()
        ctx.wait(v, 'act')
        ctx.inc('dve', v.tensor_scalar(out=alog[:, 1:2], in0=alog[:, 1:2],
                                       scalar1=-1.0, scalar2=None, op0=OP.mult))
        v.drain()
        ctx.inc('dve', v.tensor_scalar(out=at_[:], in0=dt_[:], scalar1=alog[:, 1:2],
                                       scalar2=None, op0=OP.mult))
        v.drain()
        for c in range(SSD_NC):
            ctx.inc('dve', v.tensor_tensor_scan(
                out=act_[:, c * SSD_CS:(c + 1) * SSD_CS],
                data0=onec[:, 0:1].broadcast_to([32, SSD_CS]),
                data1=at_[:, c * SSD_CS:(c + 1) * SSD_CS],
                initial=0.0, op0=OP.mult, op1=OP.add))
        v.drain()
        ctx.wait(s, 'dve')
        ctx.inc('act', s.activation(out=tmp[:], in_=dt_[:], func=AF.Ln))
        s.drain()
        ctx.wait(v, 'act')
        ctx.inc('dve', v.tensor_tensor(out=rt_[:], in0=act_[:], in1=tmp[:], op=OP.subtract))
        v.drain()
        ctx.wait(q, 'dve')
        ctx.inc('qs', q.dma_start(out=d['ACT_s'][:, :], in_=act_[:]), 16)
        ctx.inc('qs', q.dma_start(out=d['RT_s'][:, :], in_=rt_[:]), 16)


def p_ssd_core(ctx):
    nc = ctx.nc
    d = ctx.dram
    q, g, v, s, t = nc.sync, nc.gpsimd, nc.vector, nc.scalar, nc.tensor
    CS = SSD_CS
    with Scope(ctx):
        acT = ctx.sbuf('sc_ac', [32, S], F32)
        rt_ = ctx.sbuf('sc_rt', [32, S], F32)
        bt = ctx.sbuf('sc_bt', [64, S], BF16)
        ct = ctx.sbuf('sc_ct', [64, S], BF16)
        btok = ctx.sbuf('sc_btok', [128, NT * 64], BF16)
        mask = ctx.sbuf('sc_mask', [128, 128], F32)
        sel = ctx.sbuf('sc_sel', [32, 32 * 128], F32)
        drow = ctx.sbuf('sc_drow', [1, 32], F32)
        onesf = ctx.sbuf('sc_onesf', [2, 128], F32)
        eye = ctx.sbuf('sc_eye', [128, 128], F32)
        xtk = [ctx.sbuf(f'sc_x{i}', [128, DM], BF16) for i in range(2)]
        stage = ctx.sbuf('sc_stage', [96, 128], F32)
        stok = ctx.sbuf('sc_stok', [128, 96], F32)
        cdrow = ctx.sbuf('sc_cdrow', [1, 64], F32)
        eac = ctx.sbuf('sc_eac', [128, 32], F32)
        cbt = ctx.sbuf('sc_cbt', [128, 128], BF16)
        xds = ctx.sbuf('sc_xds', [128, DM], BF16)
        expnt = [ctx.sbuf(f'sc_ex{i}', [128, 512], F32) for i in range(2)]
        ebt = [ctx.sbuf(f'sc_eb{i}', [128, 512], BF16) for i in range(2)]
        mt = [ctx.sbuf(f'sc_mt{i}', [128, 512], BF16) for i in range(2)]
        yo = ctx.sbuf('sc_yo', [128, DM], F32)
        y01 = ctx.sbuf('sc_y01', [128, DM], F32)
        xdd = ctx.sbuf('sc_xdd', [128, DM], F32)
        ytok = ctx.sbuf('sc_ytok', [128, DM], BF16)
        dbb = ctx.sbuf('sc_dbb', [128, 32], F32)
        prevf = ctx.sbuf('sc_prevf', [64, DM], F32)
        prevb = ctx.sbuf('sc_prevb', [64, DM], BF16)
        pE = ctx.psum('sc_pE', [128, 512], F32)
        pST = ctx.psum('sc_pST', [128, 512], F32)
        pAc = pE
        pYd = ctx.psum('sc_pYd', [128, 1024], F32)
        pYo = ctx.psum('sc_pYo', [128, 1024], F32)
        pst = ctx.psum('sc_pst', [64, 1024], F32)
        for nm_, sb_ in [('ACT_s', acT), ('RT_s', rt_), ('BT', bt), ('CT', ct),
                         ('ssd_mask', mask), ('sel32', sel), ('ssd_d', drow),
                         ('onesf', onesf), ('eye', eye)]:
            ctx.inc('qs', q.dma_start(out=sb_[:], in_=d[nm_][:, :]), 16)
        for i in range(NT):
            ctx.inc('qg', g.dma_start(out=btok[:, i * 64:(i + 1) * 64],
                                      in_=d['BTOK'][i * 128:(i + 1) * 128, :]), 16)
        ctx.inc('dve', v.memset(prevf[:], 0.0))
        ctx.inc('dve', v.memset(prevb[:], 0.0))
        ctx.barrier([t, v, s, g])
        mm = t.matmul(pE[:, 0:32], onesf[0:1, 0:128], drow[0:1, 0:32], start=True, stop=True)
        ctx.inc('pe', mm)
        ctx.wait(v, 'pe')
        ctx.inc('dve', v.tensor_copy(out=dbb[:], in_=pE[:, 0:32]))
        ld_tok = [0, 0]
        xrel = [0, 0]
        sttok = 0
        ysttok = 0
        cast_tok = ctx.cnt['dve']
        for c in range(SSD_NC):
            b = c % 2
            if c >= 2:
                ctx.wait(q, 'pe', xrel[b])
            ctx.inc('qs', q.dma_start(out=xtk[b][:], in_=d['XTOK'][c * CS:(c + 1) * CS, :]), 16)
            ld_tok[b] = ctx.cnt['qs']
            acl = acT[:, c * CS + CS - 1:c * CS + CS]
            ctx.wait(v, 'pe')
            ctx.inc('dve', v.tensor_copy(out=stage[0:32, :], in_=acT[:, c * CS:(c + 1) * CS]))
            ctx.inc('dve', v.tensor_scalar(out=stage[32:64, :], in0=rt_[:, c * CS:(c + 1) * CS],
                                           scalar1=-1.0, scalar2=None, op0=OP.mult))
            ctx.inc('dve', v.tensor_scalar(out=stage[64:96, :], in0=rt_[:, c * CS:(c + 1) * CS],
                                           scalar1=acl, scalar2=None, op0=OP.subtract))
            v.drain()
            ctx.wait(s, 'dve')
            ctx.inc('act', s.activation(out=stage[64:96, :], in_=stage[64:96, :],
                                        func=AF.Exp, scale=-1.0))
            s.drain()
            ctx.wait(t, 'act')
            ctx.wait(t, 'dve')
            mm = t.transpose(pST[:, 0:96], stage[0:96, :], eye[0:96, 0:96])
            ctx.inc('pe', mm)
            mm = t.transpose(pST[0:1, 96:128], acl, eye[0:32, 0:32])
            ctx.inc('pe', mm)
            ctx.wait(v, 'pe')
            ctx.wait(v, 'act')
            ctx.inc('dve', v.tensor_copy(out=stok[:], in_=pST[:, 0:96]))
            ctx.inc('dve', v.tensor_copy(out=cdrow[0:1, 0:32], in_=pST[0:1, 96:128]))
            v.drain()
            ctx.wait(s, 'dve')
            ctx.inc('act', s.activation(out=cdrow[0:1, 32:64], in_=cdrow[0:1, 0:32],
                                        func=AF.Exp))
            ctx.inc('act', s.activation(out=eac[:], in_=stok[:, 0:32], func=AF.Exp))
            s.drain()
            ctx.wait(t, 'dve')
            mm = t.matmul(pST[:, 288:416], bt[:, c * CS:(c + 1) * CS],
                          ct[:, c * CS:(c + 1) * CS], start=True, stop=True)
            ctx.inc('pe', mm)
            ctx.wait(v, 'pe')
            ctx.inc('dve', v.tensor_copy(out=cbt[:], in_=pST[:, 288:416]))
            cbtok = ctx.cnt['dve']
            ctx.wait(v, 'qs', ld_tok[b])
            ctx.inc('dve', v.tensor_tensor(
                out=xds[:].rearrange('p (h e) -> p h e', h=32),
                in0=xtk[b][:].rearrange('p (h e) -> p h e', h=32),
                in1=stok[:, 64:96].unsqueeze(2).broadcast_to([128, 32, 32]),
                op=OP.mult))
            v.drain()
            ctx.wait(t, 'dve', cast_tok)
            for oh in range(2):
                mm = t.matmul(pYo[:, oh * 512:(oh + 1) * 512], ct[:, c * CS:(c + 1) * CS],
                              prevb[:, oh * 512:(oh + 1) * 512], start=True, stop=True)
                ctx.inc('pe', mm)
            yoff_tok = ctx.cnt['pe']
            for oh in range(2):
                mm = t.matmul(pst[0:64, oh * 512:(oh + 1) * 512],
                              btok[:, c * 64:(c + 1) * 64],
                              xds[:, oh * 512:(oh + 1) * 512], start=True, stop=True)
                ctx.inc('pe', mm)
            pst_tok = ctx.cnt['pe']
            acfree = 0
            exread = [0, 0]
            mtrel = [0, 0]
            ydtok = [0, 0]
            for hg in range(SSD_NH // 4):
                par = hg % 2
                ctx.wait(t, 'dve', acfree)
                for q_ in range(4):
                    h = hg * 4 + q_
                    mm = t.matmul(pAc[:, q_ * 128:(q_ + 1) * 128],
                                  sel[:, h * 128:(h + 1) * 128],
                                  acT[:, c * CS:(c + 1) * CS], start=True, stop=True)
                ctx.inc('pe', mm)
                ctx.wait(v, 'pe')
                ctx.wait(v, 'act', exread[par])
                ctx.inc('dve', v.tensor_tensor(
                    out=expnt[par][:].rearrange('p (g i) -> p g i', g=4),
                    in0=pAc[:, 0:512].rearrange('p (g i) -> p g i', g=4),
                    in1=mask[:].unsqueeze(1).broadcast_to([128, 4, 128]), op=OP.min))
                acfree = ctx.cnt['dve']
                ctx.wait(s, 'dve', acfree)
                ctx.wait(s, 'pool', mtrel[par])
                for q_ in range(4):
                    h = hg * 4 + q_
                    ctx.inc('act', s.activation(out=ebt[par][:, q_ * 128:(q_ + 1) * 128],
                                                in_=expnt[par][:, q_ * 128:(q_ + 1) * 128],
                                                func=AF.Exp,
                                                bias=stok[:, 32 + h:33 + h], scale=1.0))
                exread[par] = ctx.cnt['act']
                ctx.wait(g, 'act')
                ctx.wait(g, 'pe', ydtok[par])
                ctx.inc('pool', g.tensor_tensor(
                    out=mt[par][:].rearrange('p (g i) -> p g i', g=4),
                    in0=ebt[par][:].rearrange('p (g i) -> p g i', g=4),
                    in1=cbt[:].unsqueeze(1).broadcast_to([128, 4, 128]), op=OP.mult))
                mtrel[par] = ctx.cnt['pool']
                ctx.wait(t, 'pool', mtrel[par])
                for q_ in range(4):
                    h = hg * 4 + q_
                    mm = t.matmul(pYd[:, h * 32:(h + 1) * 32],
                                  mt[par][:, q_ * 128:(q_ + 1) * 128],
                                  xtk[b][:, h * 32:(h + 1) * 32],
                                  start=(h % 16 == 0), stop=(h % 16 == 15))
                ctx.inc('pe', mm)
                ydtok[par] = ctx.cnt['pe']
            xrel[b] = ctx.cnt['pe']
            ctx.wait(v, 'pe')
            ctx.wait(v, 'act')
            ctx.inc('dve', v.tensor_tensor(
                out=yo[:].rearrange('p (h e) -> p h e', h=32),
                in0=pYo[:, :].rearrange('p (h e) -> p h e', h=32),
                in1=eac[:].unsqueeze(2).broadcast_to([128, 32, 32]), op=OP.mult))
            v.drain()
            ctx.inc('dve', v.tensor_tensor(out=y01[:], in0=pYd[:, :], in1=yo[:], op=OP.add))
            ctx.inc('dve', v.tensor_tensor(
                out=xdd[:].rearrange('p (h e) -> p h e', h=32),
                in0=xtk[b][:].rearrange('p (h e) -> p h e', h=32),
                in1=dbb[:].unsqueeze(2).broadcast_to([128, 32, 32]), op=OP.mult))
            v.drain()
            ctx.wait(v, 'qs', ysttok)
            ctx.inc('dve', v.tensor_tensor(out=ytok[:], in0=y01[:], in1=xdd[:], op=OP.add))
            ctx.wait(q, 'dve')
            ctx.inc('qs', q.dma_start(out=d['YTOK'][c * CS:(c + 1) * CS, :], in_=ytok[:]), 16)
            ysttok = ctx.cnt['qs']
            ctx.wait(t, 'act')
            mm = t.matmul(pST[0:64, 128:160], onesf[0:1, 0:64], cdrow[0:1, 32:64],
                          start=True, stop=True)
            ctx.inc('pe', mm)
            ctx.wait(v, 'pe')
            ctx.inc('dve', v.tensor_tensor(
                out=prevf[:].rearrange('p (h e) -> p h e', h=32),
                in0=prevf[:].rearrange('p (h e) -> p h e', h=32),
                in1=pST[0:64, 128:160].unsqueeze(2).broadcast_to([64, 32, 32]),
                op=OP.mult))
            v.drain()
            ctx.inc('dve', v.tensor_tensor(out=prevf[:], in0=prevf[:], in1=pst[0:64, :],
                                           op=OP.add))
            v.drain()
            ctx.inc('dve', v.tensor_copy(out=prevb[:], in_=prevf[:]))
            cast_tok = ctx.cnt['dve']


def p_ssd_out(ctx):
    nc = ctx.nc
    d = ctx.dram
    q, g, v, s, t = nc.sync, nc.gpsimd, nc.vector, nc.scalar, nc.tensor
    with Scope(ctx):
        wout = ctx.sbuf('so_w', [128, NF * DM], BF16)
        yt = [ctx.sbuf(f'so_y{i}', [128, DM], BF16) for i in range(2)]
        ytT = ctx.sbuf('so_ytT', [128, NF * 128], BF16)
        eyebf = ctx.sbuf('so_eye', [128, 128], BF16)
        s1 = ctx.sbuf('so_s1', [128, DM], F32)
        ptA = ctx.psum('so_ptA', [128, 1024], BF16)
        pm1 = ctx.psum('so_pm1', [128, 512], F32)
        pm2 = ctx.psum('so_pm2', [128, 512], F32)
        cast_loads(ctx, wout, 'ssd_wout', NF, DM)
        ctx.inc('qs', q.dma_start(out=eyebf[:], in_=d['eyebf'][:, :]), 16)
        ctx.barrier([t])
        ld_tok = [0, 0]
        rel_tok = [0, 0]
        sttok = 0
        for i in range(NT):
            b = i % 2
            if i >= 2:
                ctx.wait(q, 'pe', rel_tok[b])
            ctx.inc('qs', q.dma_start(out=yt[b][:], in_=d['YTOK'][i * 128:(i + 1) * 128, :]), 16)
            ld_tok[b] = ctx.cnt['qs']
            ctx.wait(t, 'qs', ld_tok[b])
            ctx.wait(t, 'dve')
            for k in range(NF):
                mm = t.transpose(ptA[:, k * 128:(k + 1) * 128],
                                 yt[b][:, k * 128:(k + 1) * 128], eyebf[:, 0:128])
            ctx.inc('pe', mm)
            rel_tok[b] = ctx.cnt['pe']
            ctx.wait(v, 'pe')
            ctx.inc('dve', v.tensor_copy(out=ytT[:], in_=ptA[:, 0:1024]))
            tok = ctx.cnt['dve']
            ctx.wait(t, 'dve', tok)
            for oh, bank in [(0, pm1), (1, pm2)]:
                for k in range(NF):
                    mm = t.matmul(bank[:, :], ytT[:, k * 128:(k + 1) * 128],
                                  wout[:, k * DM + oh * 512:k * DM + (oh + 1) * 512],
                                  start=(k == 0), stop=(k == NF - 1))
                ctx.inc('pe', mm)
            ctx.wait(v, 'pe')
            ctx.wait(v, 'qs', sttok)
            ctx.inc('dve', v.tensor_copy(out=s1[:, 0:512], in_=pm1[:, :]))
            ctx.inc('dve', v.tensor_copy(out=s1[:, 512:1024], in_=pm2[:, :]))
            ctx.wait(q, 'dve')
            ctx.inc('qs', q.dma_start(out=d['S1'][i * 128:(i + 1) * 128, :], in_=s1[:]), 16)
            sttok = ctx.cnt['qs']


def p_moe_rt(ctx):
    nc = ctx.nc
    d = ctx.dram
    q, g, v, s, t = nc.sync, nc.gpsimd, nc.vector, nc.scalar, nc.tensor
    with Scope(ctx):
        rt = ctx.sbuf('mr_rt', [128, NF * 8], BF16)
        rtf = ctx.sbuf('mr_rtf', [128, NF * 8], F32)
        rtlo = ctx.sbuf('mr_rtlo', [128, NF * 8], BF16)
        xch = [ctx.sbuf(f'mr_x{i}', [128, 2 * NF * 128], BF16) for i in range(2)]
        eye = ctx.sbuf('mr_eye', [128, 128], F32)
        w_ = ctx.sbuf('mr_w', [128, 24], F32)
        oh1 = ctx.sbuf('mr_oh1', [128, 8], F32)
        oh2 = ctx.sbuf('mr_oh2', [128, 8], F32)
        t1 = ctx.sbuf('mr_t1', [128, 8], F32)
        s2 = ctx.sbuf('mr_s2', [128, 8], F32)
        e1 = ctx.sbuf('mr_e1', [128, 8], F32)
        apl = ctx.sbuf('mr_apl', [128, 8], F32)
        dw = ctx.sbuf('mr_dw', [128, 8], F32)
        dwT = ctx.sbuf('mr_dwT', [8, S], F32)
        pl = ctx.psum('mr_pl', [128, 512], F32)
        ptr = ctx.psum('mr_ptr', [128, 512], F32)
        cast_loads(ctx, rt, 'moe_rt', NF, 8)
        for k in range(NF):
            ctx.inc('qs', q.dma_start(out=rtf[:, k * 8:(k + 1) * 8],
                                      in_=d['moe_rt'][k * 128:(k + 1) * 128, :]), 16)
        ctx.inc('qs', q.dma_start(out=eye[:], in_=d['eye'][:, :]), 16)
        ctx.wait(v, 'qs')
        ctx.wait(v, 'qg')
        ctx.inc('dve', v.tensor_tensor(out=rtlo[:], in0=rtf[:], in1=rt[:], op=OP.subtract))
        ctx.barrier([t])
        ld_tok = [0, 0]
        rel_tok = [0, 0]
        for i in range(NT):
            b = i % 2
            if i >= 2:
                ctx.wait(q, 'pe', rel_tok[b])
            for k in range(NF):
                ctx.inc('qs', q.dma_start(out=xch[b][:, k * 128:(k + 1) * 128],
                                          in_=d['XN1T'][k * 128:(k + 1) * 128,
                                                        1 + i * 128:1 + (i + 1) * 128]), 16)
            for k in range(NF):
                ctx.inc('qs', q.dma_start(out=xch[b][:, (NF + k) * 128:(NF + k + 1) * 128],
                                          in_=d['XN1TLO'][k * 128:(k + 1) * 128,
                                                          1 + i * 128:1 + (i + 1) * 128]), 16)
            ld_tok[b] = ctx.cnt['qs']
            ctx.wait(t, 'qs', ld_tok[b])
            ctx.wait(t, 'dve')
            ctx.wait(t, 'act')
            for k in range(NF):
                t.matmul(pl[:, 0:8], xch[b][:, k * 128:(k + 1) * 128],
                         rt[:, k * 8:(k + 1) * 8], start=(k == 0), stop=False)
            for k in range(NF):
                t.matmul(pl[:, 0:8], xch[b][:, k * 128:(k + 1) * 128],
                         rtlo[:, k * 8:(k + 1) * 8], start=False, stop=False)
            for k in range(NF):
                mm = t.matmul(pl[:, 0:8], xch[b][:, (NF + k) * 128:(NF + k + 1) * 128],
                              rt[:, k * 8:(k + 1) * 8], start=False, stop=(k == NF - 1))
            ctx.inc('pe', mm)
            rel_tok[b] = ctx.cnt['pe']
            ctx.wait(v, 'pe')
            ctx.inc('dve', v.reduce_max(out=w_[:, 0:1], in_=pl[:, 0:8], axis=AX.X))
            v.drain()
            ctx.inc('dve', v.tensor_scalar(out=t1[:], in0=pl[:, 0:8], scalar1=w_[:, 0:1],
                                           scalar2=None, op0=OP.subtract))
            ctx.inc('dve', v.tensor_scalar(out=oh1[:], in0=pl[:, 0:8], scalar1=w_[:, 0:1],
                                           scalar2=None, op0=OP.is_equal))
            ctx.wait(s, 'pe')
            ctx.wait(s, 'dve')
            ctx.inc('act', s.activation(out=apl[:], in_=pl[:, 0:8], func=AF.Abs))
            s.drain()
            ctx.wait(v, 'act')
            ctx.inc('dve', v.tensor_scalar(out=w_[:, 8:16], in0=apl[:], scalar1=w_[:, 0:1],
                                           scalar2=-0.02, op0=OP.max, op1=OP.mult))
            v.drain()
            ctx.inc('dve', v.tensor_tensor(out=w_[:, 8:16], in0=t1[:], in1=w_[:, 8:16],
                                           op=OP.is_ge))
            v.drain()
            ctx.wait(s, 'dve')
            ctx.inc('act', s.activation(out=e1[:], in_=t1[:], func=AF.Exp))
            s.drain()
            ctx.wait(v, 'act')
            ctx.inc('dve', v.tensor_tensor(out=e1[:], in0=e1[:], in1=w_[:, 8:16], op=OP.mult))
            v.drain()
            ctx.inc('dve', v.reduce_sum(out=w_[:, 2:3], in_=e1[:], axis=AX.X))
            v.drain()
            ctx.inc('dve', v.reciprocal(out=w_[:, 2:3], in_=w_[:, 2:3]))
            ctx.inc('dve', v.scalar_tensor_tensor(out=s2[:], in0=oh1[:], scalar=-BIG,
                                                  in1=pl[:, 0:8], op0=OP.mult, op1=OP.add))
            v.drain()
            ctx.inc('dve', v.reduce_max(out=w_[:, 3:4], in_=s2[:], axis=AX.X))
            v.drain()
            ctx.inc('dve', v.tensor_scalar(out=oh2[:], in0=s2[:], scalar1=w_[:, 3:4],
                                           scalar2=None, op0=OP.is_equal))
            ctx.inc('dve', v.tensor_scalar(out=s2[:], in0=s2[:], scalar1=w_[:, 3:4],
                                           scalar2=None, op0=OP.subtract))
            ctx.inc('dve', v.tensor_scalar(out=w_[:, 16:24], in0=apl[:], scalar1=w_[:, 3:4],
                                           scalar2=-0.02, op0=OP.max, op1=OP.mult))
            ctx.inc('dve', v.tensor_scalar(out=t1[:], in0=pl[:, 0:8], scalar1=w_[:, 3:4],
                                           scalar2=None, op0=OP.subtract))
            v.drain()
            ctx.inc('dve', v.tensor_tensor(out=w_[:, 16:24], in0=t1[:], in1=w_[:, 16:24],
                                           op=OP.is_ge))
            v.drain()
            ctx.wait(s, 'dve')
            ctx.inc('act', s.activation(out=s2[:], in_=s2[:], func=AF.Exp))
            s.drain()
            ctx.wait(v, 'act')
            ctx.inc('dve', v.tensor_tensor(out=s2[:], in0=s2[:], in1=w_[:, 16:24], op=OP.mult))
            v.drain()
            ctx.inc('dve', v.reduce_sum(out=w_[:, 5:6], in_=s2[:], axis=AX.X))
            v.drain()
            ctx.inc('dve', v.reciprocal(out=w_[:, 5:6], in_=w_[:, 5:6]))
            v.drain()
            ctx.inc('dve', v.tensor_scalar(out=dw[:], in0=oh1[:], scalar1=w_[:, 2:3],
                                           scalar2=None, op0=OP.mult))
            ctx.inc('dve', v.tensor_scalar(out=oh2[:], in0=oh2[:], scalar1=w_[:, 5:6],
                                           scalar2=None, op0=OP.mult))
            v.drain()
            ctx.inc('dve', v.tensor_tensor(out=dw[:], in0=dw[:], in1=oh2[:], op=OP.add))
            dwtok = ctx.cnt['dve']
            ctx.wait(t, 'dve', dwtok)
            mm = t.transpose(ptr[0:8, 0:128], dw[:, 0:8], eye[0:128, 0:128])
            ctx.inc('pe', mm)
            ctx.wait(v, 'pe')
            ctx.inc('dve', v.tensor_copy(out=dwT[0:8, i * 128:(i + 1) * 128],
                                         in_=ptr[0:8, 0:128]))
        ctx.wait(q, 'dve')
        ctx.inc('qs', q.dma_start(out=d['DWT'][:, :], in_=dwT[:]), 16)


def p_moe_exp(ctx):
    nc = ctx.nc
    d = ctx.dram
    q, g, v, s, t = nc.sync, nc.gpsimd, nc.vector, nc.scalar, nc.tensor
    NM = MOE_F // 128
    with Scope(ctx):
        wgA = ctx.sbuf('me_wgA', [128, NF * MOE_F], BF16)
        wgB = ctx.sbuf('me_wgB', [128, NF * MOE_F], BF16)
        wu_ = ctx.sbuf('me_wu', [128, NF * MOE_F], BF16)
        wd_ = ctx.sbuf('me_wd', [128, NM * DM], BF16)
        wg = [wgA, wgB]
        wu = [wu_, wu_]
        wd = [wd_, wd_]
        gaterel = [0, 0]
        xch = [ctx.sbuf(f'me_x{i}', [128, NF * 512], BF16) for i in range(2)]
        xdw = ctx.sbuf('me_xdw', [128, NF * 512], BF16)
        dwT = ctx.sbuf('me_dwT', [8, S], F32)
        sel = ctx.sbuf('me_sel', [8, 8 * 128], F32)
        dwb = ctx.sbuf('me_dwb', [128, 512], BF16)
        sg = ctx.sbuf('me_sg', [128, 1024], BF16)
        gu = ctx.sbuf('me_gu', [128, NM * 512], BF16)
        mo = ctx.sbuf('me_mo', [128, DM], F32)
        pga = ctx.psum('me_pga', [128, 512], F32)
        pgb = ctx.psum('me_pgb', [128, 512], F32)
        pua = ctx.psum('me_pua', [128, 512], F32)
        pub = ctx.psum('me_pub', [128, 512], F32)
        pdw = ctx.psum('me_pdw', [128, 512], F32)
        pd1 = ctx.psum('me_pd1', [128, 512], F32)
        pd2 = ctx.psum('me_pd2', [128, 512], F32)
        pd3 = ctx.psum('me_pd3', [128, 512], F32)
        ctx.inc('qs', q.dma_start(out=dwT[:], in_=d['DWT'][:, :]), 16)
        ctx.inc('qs', q.dma_start(out=sel[:], in_=d['sel8'][:, :]), 16)
        wld = [0, 0]
        wrel = [0, 0]
        ld_tok = [0, 0]
        xrel = [0, 0]
        dwbtok = 0
        storetok = 0
        for e in range(MOE_E):
            eb = e % 2
            if e >= 2:
                ctx.wait(g, 'pe', gaterel[eb])
            for k in range(NF):
                ctx.inc('qg', g.dma_start(out=wg[eb][:, k * MOE_F:(k + 1) * MOE_F],
                                          in_=d['moe_wg'][e, k * 128:(k + 1) * 128, :]), 16)
            if e >= 1:
                ctx.wait(g, 'pe', wrel[(e - 1) % 2])
            for k in range(NF):
                ctx.inc('qg', g.dma_start(out=wu[eb][:, k * MOE_F:(k + 1) * MOE_F],
                                          in_=d['moe_wu'][e, k * 128:(k + 1) * 128, :]), 16)
            for j in range(NM):
                ctx.inc('qg', g.dma_start(out=wd[eb][:, j * DM:(j + 1) * DM],
                                          in_=d['moe_wd'][e, j * 128:(j + 1) * 128, :]), 16)
            wld[eb] = ctx.cnt['qg']
            for c in range(8):
                cb = c % 2
                if e > 0 or c >= 2:
                    ctx.wait(q, 'pe', xrel[cb])
                for k in range(NF):
                    ctx.inc('qs', q.dma_start(out=xch[cb][:, k * 512:(k + 1) * 512],
                                              in_=d['XN1T'][k * 128:(k + 1) * 128,
                                                            1 + c * 512:1 + (c + 1) * 512]), 16)
                ld_tok[cb] = ctx.cnt['qs']
                ctx.wait(t, 'qs', ld_tok[cb])
                ctx.wait(t, 'dve')
                mm = t.matmul(pdw[:, :], sel[:, e * 128:(e + 1) * 128],
                              dwT[:, c * 512:(c + 1) * 512], start=True, stop=True)
                ctx.inc('pe', mm)
                ctx.wait(v, 'pe')
                ctx.inc('dve', v.tensor_copy(out=dwb[:], in_=pdw[:, :]))
                v.drain()
                for k in range(NF):
                    ctx.inc('dve', v.tensor_tensor(out=xdw[:, k * 512:(k + 1) * 512],
                                                   in0=xch[cb][:, k * 512:(k + 1) * 512],
                                                   in1=dwb[:], op=OP.mult))
                dwbtok = ctx.cnt['dve']
                if c == 0:
                    ctx.wait(t, 'qg', wld[eb])
                ctx.wait(t, 'dve', dwbtok)
                silu_tok = [0, 0]
                gu_tok = [0, 0]
                for m in range(NM):
                    bank_g = [pga, pgb][m % 2]
                    bank_u = [pua, pub][m % 2]
                    ctx.wait(t, 'act', silu_tok[m % 2])
                    for k in range(NF):
                        mm = t.matmul(bank_g[:, :], wg[eb][:, k * MOE_F + m * 128:k * MOE_F + (m + 1) * 128],
                                      xch[cb][:, k * 512:(k + 1) * 512],
                                      start=(k == 0), stop=(k == NF - 1))
                    ctx.inc('pe', mm)
                    ctx.wait(t, 'dve', gu_tok[m % 2])
                    for k in range(NF):
                        mm = t.matmul(bank_u[:, :], wu[eb][:, k * MOE_F + m * 128:k * MOE_F + (m + 1) * 128],
                                      xdw[:, k * 512:(k + 1) * 512],
                                      start=(k == 0), stop=(k == NF - 1))
                    ctx.inc('pe', mm)
                    ctx.wait(s, 'pe')
                    ctx.wait(s, 'dve', gu_tok[m % 2])
                    ctx.inc('act', s.activation(out=sg[:, (m % 2) * 512:(m % 2 + 1) * 512],
                                                in_=bank_g[:, :], func=AF.Silu))
                    silu_tok[m % 2] = ctx.cnt['act']
                    ctx.wait(v, 'act')
                    ctx.inc('dve', v.tensor_tensor(out=gu[:, m * 512:(m + 1) * 512],
                                                   in0=bank_u[:, :],
                                                   in1=sg[:, (m % 2) * 512:(m % 2 + 1) * 512],
                                                   op=OP.mult))
                    gu_tok[m % 2] = ctx.cnt['dve']
                xrel[cb] = ctx.cnt['pe']
                if c == 7:
                    gaterel[eb] = ctx.cnt['pe']
                gutok = ctx.cnt['dve']
                ctx.wait(t, 'dve', gutok)
                evactok = [0, 0]
                for tsub in range(4):
                    pdx = [(pd1, pd2), (pd3, pdw)][tsub % 2]
                    ctx.wait(t, 'dve', evactok[tsub % 2])
                    for oh, bank in [(0, pdx[0]), (1, pdx[1])]:
                        for j in range(NM):
                            mm = t.matmul(bank[:, :],
                                          gu[:, j * 512 + tsub * 128:j * 512 + (tsub + 1) * 128],
                                          wd[eb][:, j * DM + oh * 512:j * DM + (oh + 1) * 512],
                                          start=(j == 0), stop=(j == NM - 1))
                        ctx.inc('pe', mm)
                    ctx.wait(v, 'pe')
                    ctx.wait(v, 'qg', storetok)
                    ctx.inc('dve', v.tensor_copy(out=mo[:, 0:512], in_=pdx[0][:, :]))
                    ctx.inc('dve', v.tensor_copy(out=mo[:, 512:1024], in_=pdx[1][:, :]))
                    evactok[tsub % 2] = ctx.cnt['dve']
                    ctx.wait(g, 'dve')
                    row = c * 512 + tsub * 128
                    if e == 0:
                        ctx.inc('qg', g.dma_start(out=d['MOEACC'][row:row + 128, :],
                                                  in_=mo[:]), 16)
                    else:
                        ctx.inc('qg', g.dma_start(out=d['MOEACC'][row:row + 128, :],
                                                  in_=mo[:], accum_op=OP.add), 16)
                    storetok = ctx.cnt['qg']
            wrel[eb] = ctx.cnt['pe']


def _phase_list():
    return [
        ('ln0', ['x', 'eyebf'], ['XNTOK', 'XNT', 'XNF32'], lambda c: p_ln0(c)),
        ('l0aA', ['XNT', 'l0a_Wk', 'l0a_Wv', 'l0a_bk', 'l0a_bv', 'onesbf'],
         ['KVT', 'KDIAG'], lambda c: p_l0a_A(c)),
        ('l0aB', ['XNT', 'XNTOK', 'KVT', 'KDIAG', 'l0a_Wq', 'l0a_Wd', 'l0a_bq_c',
                  'l0a_bd', 'l0a_gamma', 'l0a_beta', 'eye', 'onesbf', 'onesf', 'pairsel'],
         ['S0'], lambda c: p_l0a_B(c)),
        ('fqkv0', ['XNT', 'ff_wqkv', 'ff_qs', 'ff_ks', 'onesf', 'eyebf',
                   'rot_qc', 'rot_qs', 'rot_kc', 'rot_ks'],
         ['QHT', 'KHT', 'VH'], lambda c: p_ffn_qkv(c, 0, 'XNT')),
        ('fatt0', ['QHT', 'KHT', 'VH', 'mask_gen', 'mask_w0', 'eyebf'],
         ['OT'], lambda c: p_ffn_att(c, 0)),
        ('fwo0', ['OT', 'XNF32', 'ff_wo'], ['XIA'], lambda c: p_ffn_wo(c, 0, 'XNF32', True)),
        ('ftr0a', ['XIA', 'eyebf'], ['XITA'], lambda c: p_tr(c, 'XIA', 'XITA')),
        ('fffn0', ['XITA', 'XIA', 'ff_w1', 'ff_b1', 'ff_w2', 'ff_b2', 'ff_lng', 'ff_lnb'],
         ['XIB'], lambda c: p_ffn_ffn(c, 0, 'XIB')),
        ('ftr0b', ['XIB', 'eyebf'], ['XITB'], lambda c: p_tr(c, 'XIB', 'XITB')),
        ('fqkv1', ['XITB', 'ff_wqkv', 'ff_qs', 'ff_ks', 'onesf', 'eyebf',
                   'rot_qc', 'rot_qs', 'rot_kc', 'rot_ks'],
         ['QHT', 'KHT', 'VH'], lambda c: p_ffn_qkv(c, 1, 'XITB')),
        ('fatt1', ['QHT', 'KHT', 'VH', 'mask_gen', 'mask_w0', 'eyebf'],
         ['OT'], lambda c: p_ffn_att(c, 1)),
        ('fwo1', ['OT', 'XIB', 'ff_wo'], ['XIA'], lambda c: p_ffn_wo(c, 1, 'XIB', True)),
        ('ftr1a', ['XIA', 'eyebf'], ['XITA'], lambda c: p_tr(c, 'XIA', 'XITA')),
        ('fffn1', ['XITA', 'XIA', 'ff_w1', 'ff_b1', 'ff_w2', 'ff_b2', 'ff_lng', 'ff_lnb'],
         ['FF'], lambda c: p_ffn_ffn(c, 1, 'FF')),
        ('comb0', ['x', 'S0', 'FF', 'eyebf'],
         ['X1', 'XN1TOK', 'XN1T', 'XN1TLO'], lambda c: p_comb0(c)),
        ('sproj', ['XN1T', 'ssd_wint', 'ssd_dtb', 'ssd_alog'],
         ['XTOK', 'BTOK', 'BT', 'CT', 'ACT_s', 'RT_s'], lambda c: p_ssd_proj(c)),
        ('score', ['XTOK', 'BTOK', 'BT', 'CT', 'ACT_s', 'RT_s', 'ssd_mask', 'sel32',
                   'ssd_d', 'onesf', 'eye'], ['YTOK'], lambda c: p_ssd_core(c)),
        ('sout', ['YTOK', 'ssd_wout', 'eyebf'], ['S1'], lambda c: p_ssd_out(c)),
        ('mrt', ['XN1T', 'XN1TLO', 'moe_rt', 'eye'], ['DWT'], lambda c: p_moe_rt(c)),
        ('mexp', ['XN1T', 'DWT', 'sel8', 'moe_wg', 'moe_wu', 'moe_wd'],
         ['MOEACC'], lambda c: p_moe_exp(c)),
        ('final', ['X1', 'S1', 'MOEACC'], ['out'], lambda c: p_final(c)),
    ]


FUSED = True


def build_fused():
    ctx = Ctx(fused=True)
    nc = ctx.nc
    engines = [nc.tensor, nc.vector, nc.scalar, nc.gpsimd, nc.sync]
    for name, ins, outs, emit in _phase_list():
        emit(ctx)
        ctx.barrier(engines)
    finish(ctx)
    ctx.close()
    return ctx


def kernel(**inputs):
    consts = host_prep(inputs)
    x_all = np.ascontiguousarray(np.asarray(inputs['L0_attn_input_tensor'], np.float32))
    if FUSED:
        ctx = build_fused()
        in_maps = []
        for b in range(BATCH):
            m = dict(consts)
            m['x'] = x_all[b]
            in_maps.append(m)
        res = run_bass_kernel_spmd(ctx.nc, in_maps, core_ids=list(range(BATCH)))
        return np.stack([res.results[b]['out'] for b in range(BATCH)], axis=0)
    percore = []
    for b in range(BATCH):
        vals = dict(consts)
        vals['x'] = x_all[b]
        percore.append(vals)
    for name, ins, outs, emit in _phase_list():
        ctx = Ctx(fused=False, phase_inputs=ins, phase_outputs=outs)
        emit(ctx)
        finish(ctx)
        ctx.close()
        in_maps = [{n: percore[b][n] for n in ins} for b in range(BATCH)]
        res = run_bass_kernel_spmd(ctx.nc, in_maps, core_ids=list(range(BATCH)))
        for b in range(BATCH):
            for n in outs:
                percore[b][n] = res.results[b][n]
    return np.stack([percore[b]['out'] for b in range(BATCH)], axis=0)
